# version 03
# baseline (speedup 1.0000x reference)
from contextlib import ExitStack
import numpy as np
import ml_dtypes
import concourse.bass as bass
import concourse.mybir as mybir
from concourse.bass_utils import run_bass_kernel_spmd
from concourse.bass_core_rust import IndirectOffsetOnAxis

F32 = mybir.dt.float32
BF16 = mybir.dt.bfloat16
I32 = mybir.dt.int32
U32 = mybir.dt.uint32
AF = mybir.ActivationFunctionType
ALU = mybir.AluOpType
AX = mybir.AxisListType
BF = ml_dtypes.bfloat16

NCORES = 8
B = 2
S = 8192
D = 1024
SLOC = S // NCORES
T = B * SLOC
NT = T // 128
E = 8
CAP = 2560
LN_EPS = 1e-5
RMS_EPS = float(np.finfo(np.float32).eps)
BIG = 1.0e9

_NC_CACHE = {}



def _consts(nc):
    lexT = np.triu(np.ones((128, 128), np.float32), 1)
    c = {
        "lexT": nc.inline_tensor(lexT, name="c_lexT").ap(),
        "onescol": nc.inline_tensor(np.ones((128, 1), np.float32), name="c_onescol").ap(),
        "onesrow": nc.inline_tensor(np.ones((1, 128), np.float32), name="c_onesrow").ap(),
        "e8": nc.inline_tensor(np.tile(np.arange(8, dtype=np.float32), (128, 1)), name="c_e8").ap(),
        "identbf": nc.inline_tensor(np.eye(128, dtype=BF), name="c_identbf").ap(),
        "identf32": nc.inline_tensor(np.eye(128, dtype=np.float32), name="c_identf32").ap(),
        "bigtile": nc.inline_tensor(np.full((128, 128), BIG, np.float32), name="c_bigtile").ap(),
    }
    return c


class Counter:

    def __init__(self):
        self.v = 0

    def inc(self, n=1):
        self.v += n
        return self.v



def build_launch_A(topk=2, rms=True):
    nc = bass.Bass("TRN2", target_bir_lowering=False, debug=False)
    xs = nc.dram_tensor("xs", [T, D], F32, kind="ExternalInput").ap()
    rwTf = nc.dram_tensor("rwT", [D, E], F32, kind="ExternalInput").ap()
    agrow = nc.dram_tensor("agrow", [128, NT], F32, kind="ExternalInput").ap()

    xn_o = nc.dram_tensor("xn_o", [T, D], BF16, kind="ExternalOutput").ap()
    d2_o = nc.dram_tensor("d2_o", [2 * E * SLOC, 1], F32, kind="ExternalOutput").ap()
    cnt_o = nc.dram_tensor("cnt_o", [1, 16], F32, kind="ExternalOutput").ap()
    meta_o = nc.dram_tensor("meta_o", [T, 8], F32, kind="ExternalOutput").ap()

    C = _consts(nc)
    es = ExitStack()
    with es:
        sb = lambda name, shape, dt: es.enter_context(nc.sbuf_tensor(name, shape, dt))
        ps = lambda name, shape, dt: es.enter_context(nc.psum_tensor(name, shape, dt))
        x_all = sb("x_all", [128, NT * D], F32)
        xbf_all = sb("xbf_all", [128, NT * D], BF16)
        XT = sb("XT", [128, 8 * T], BF16)
        rwTs = sb("rwTs", [128, 8 * E], F32)
        rwTb = sb("rwTb", [128, 8 * E], BF16)
        agrow_s = sb("agrow_s", [128, NT], F32)
        ident_s = sb("ident_s", [128, 128], BF16)
        lexT_s = sb("lexT_s", [128, 128], F32)
        onescol_s = sb("onescol_s", [128, 1], F32)
        onesrow_s = sb("onesrow_s", [1, 128], F32)
        e8_s = sb("e8_s", [128, 8], F32)
        sum16 = sb("sum16", [128, NT], F32)
        sumsq16 = sb("sumsq16", [128, NT], F32)
        mean16 = sb("mean16", [128, NT], F32)
        var16 = sb("var16", [128, NT], F32)
        rstd16 = sb("rstd16", [128, NT], F32)
        scale16 = sb("scale16", [128, NT], F32)
        mus16 = sb("mus16", [128, NT], F32)
        tmpA = sb("tmpA", [128, NT], F32)
        tmpB = sb("tmpB", [128, NT], F32)
        sd16 = sb("sd16", [128, NT], F32)
        srms16 = sb("srms16", [128, NT], F32)
        sq_scr = sb("sq_scr", [128, D], F32)
        expsc = sb("expsc", [128, E], F32)
        zcol = sb("zcol", [128, NT], F32)
        xn_sb = sb("xn_sb", [128, NT * D], BF16)
        rws_s = sb("rws_s", [1, E], F32)
        rwsb_s = sb("rwsb_s", [128, E], F32)
        lga = sb("lga", [128, E], F32)
        lgb = sb("lgb", [128, E], F32)
        lg = sb("lg", [128, E], F32)
        m8 = sb("m8", [128, E], F32)
        i8 = sb("i8", [128, E], U32)
        e1_all = sb("e1_all", [128, NT], F32)
        e2_all = sb("e2_all", [128, NT], F32)
        w1_all = sb("w1_all", [128, NT], F32)
        w2_all = sb("w2_all", [128, NT], F32)
        d12 = sb("d12", [128, NT], F32)
        mask1_all = sb("mask1_all", [128, NT * E], F32)
        mask2_all = sb("mask2_all", [128, NT * E], F32)
        maskC = sb("maskC", [128, E], F32)
        cume_all = sb("cume_all", [128, NT * E], F32)
        totA = sb("totA", [1, E * 9], F32)
        totB = sb("totB", [1, E * 9], F32)
        totrowA = sb("totrowA", [1, E * 8], F32)
        totrowB = sb("totrowB", [1, E * 8], F32)
        tofc = sb("tofc", [1, E], F32)
        tob_s = sb("tob_s", [128, E], F32)
        cnt_s = sb("cnt_s", [1, 16], F32)
        pos1 = sb("pos1", [128, 1], F32)
        pos2 = sb("pos2", [128, 1], F32)
        cumfull = sb("cumfull", [128, E], F32)
        selt = sb("selt", [128, E], F32)
        idxf = sb("idxf", [128, 1], F32)
        idx1_i = sb("idx1_i", [128, NT], I32)
        idx2_i = sb("idx2_i", [128, NT], I32)
        meta_s = sb("meta_s", [128, NT * 8], F32)
        pXT0 = ps("pXT0", [128, 128], BF16)
        pXT1 = ps("pXT1", [128, 128], BF16)
        pLG = ps("pLG", [128, E], F32)
        pC = ps("pC", [128, E], F32)
        pT = ps("pT", [1, E], F32)
        pB2 = ps("pB2", [128, E], F32)
        pR = ps("pR", [1, E], F32)
        pB = ps("pB", [128, E], F32)

        s_ld = es.enter_context(nc.semaphore("s_ld"))
        s_gld = es.enter_context(nc.semaphore("s_gld"))
        s_gp = es.enter_context(nc.semaphore("s_gp"))
        s_stx = es.enter_context(nc.semaphore("s_stx"))
        s_st = es.enter_context(nc.semaphore("s_st"))
        s_pe = es.enter_context(nc.semaphore("s_pe"))
        s_pelg = es.enter_context(nc.semaphore("s_pelg"))
        s_pec = es.enter_context(nc.semaphore("s_pec"))
        s_peb = es.enter_context(nc.semaphore("s_peb"))
        s_perb = es.enter_context(nc.semaphore("s_perb"))
        s_aev = es.enter_context(nc.semaphore("s_aev"))
        s_asq = es.enter_context(nc.semaphore("s_asq"))
        s_asd = es.enter_context(nc.semaphore("s_asd"))
        s_asig = es.enter_context(nc.semaphore("s_asig"))
        s_aev2 = es.enter_context(nc.semaphore("s_aev2"))
        s_vsum = es.enter_context(nc.semaphore("s_vsum"))
        s_vvar = es.enter_context(nc.semaphore("s_vvar"))
        s_vr2 = es.enter_context(nc.semaphore("s_vr2"))
        s_vscale = es.enter_context(nc.semaphore("s_vscale"))
        s_vxn = es.enter_context(nc.semaphore("s_vxn"))
        s_vd = es.enter_context(nc.semaphore("s_vd"))
        s_vmask = es.enter_context(nc.semaphore("s_vmask"))
        s_vtot = es.enter_context(nc.semaphore("s_vtot"))
        s_vscan = es.enter_context(nc.semaphore("s_vscan"))
        s_vtof = es.enter_context(nc.semaphore("s_vtof"))
        s_vidx = es.enter_context(nc.semaphore("s_vidx"))
        s_vmeta = es.enter_context(nc.semaphore("s_vmeta"))
        block = es.enter_context(nc.Block())

        xs_r = xs.rearrange("(t p) d -> p t d", p=128)
        xn_r = xn_o.rearrange("(t p) d -> p t d", p=128)
        meta_r = meta_o.rearrange("(t p) m -> p t m", p=128)
        rwT_r = rwTf.rearrange("(k p) e -> p k e", p=128)
        totA_r = lambda ap: ap.rearrange("p (e j) -> p e j", j=9)

        @block.sync
        def _(sync):
            sync.dma_start(out=x_all[:, :].rearrange("p (t d) -> p t d", d=D), in_=xs_r).then_inc(s_ld, 16)
            sync.dma_start(out=rwTs[:, :].rearrange("p (k e) -> p k e", e=E), in_=rwT_r).then_inc(s_ld, 16)
            sync.dma_start(out=agrow_s[:, :], in_=agrow[:, :]).then_inc(s_ld, 16)
            sync.dma_start(out=ident_s[:, :], in_=C["identbf"][:, :]).then_inc(s_ld, 16)
            sync.dma_start(out=lexT_s[:, :], in_=C["lexT"][:, :]).then_inc(s_ld, 16)
            sync.dma_start(out=onescol_s[:, :], in_=C["onescol"][:, :]).then_inc(s_ld, 16)
            sync.dma_start(out=onesrow_s[:, :], in_=C["onesrow"][:, :]).then_inc(s_ld, 16)
            sync.dma_start(out=e8_s[:, :], in_=C["e8"][:, :]).then_inc(s_ld, 16)
            for i in range(NT):
                sync.wait_ge(s_vxn, i + 1)
                sync.dma_start(out=xn_r[:, i, :], in_=xn_sb[:, i * D:(i + 1) * D]).then_inc(s_stx, 16)
            sync.wait_ge(s_vscan, 1)
            sync.dma_start(out=cnt_o[:, :], in_=cnt_s[:, :]).then_inc(s_st, 16)
            for i in range(NT):
                sync.wait_ge(s_vmeta, i + 1)
                sync.dma_start(out=meta_r[:, i, :], in_=meta_s[:, i * 8:(i + 1) * 8]).then_inc(s_st, 16)
            sync.wait_ge(s_stx, 16 * NT)
            sync.wait_ge(s_gp, 16 + (32 if topk == 2 else 16) * 16)
            sync.wait_ge(s_st, 16 * (NT + 1))

        @block.gpsimd
        def _(gpsimd):
            gpsimd.dma_start(out=xbf_all[:, :].rearrange("p (t d) -> p t d", d=D), in_=xs_r).then_inc(s_gld, 16)
            gpsimd.dma_start(out=rwTb[:, :].rearrange("p (k e) -> p k e", e=E), in_=rwT_r).then_inc(s_gld, 16)
            gpsimd.dma_start(
                out=d2_o.rearrange("(a b) one -> a (b one)", a=128),
                in_=C["bigtile"][:, :]).then_inc(s_gp, 16)
            gpsimd.wait_ge(s_gp, 16)
            gpsimd.wait_ge(s_ld, 128)
            for i in range(NT):
                gpsimd.wait_ge(s_vidx, i + 1)
                gpsimd.indirect_dma_start(
                    out=d2_o[:, :], out_offset=IndirectOffsetOnAxis(ap=idx1_i[:, i:i + 1], axis=0),
                    in_=agrow_s[:, i:i + 1], in_offset=None,
                ).then_inc(s_gp, 16)
                if topk == 2:
                    gpsimd.indirect_dma_start(
                        out=d2_o[:, :], out_offset=IndirectOffsetOnAxis(ap=idx2_i[:, i:i + 1], axis=0),
                        in_=agrow_s[:, i:i + 1], in_offset=None,
                    ).then_inc(s_gp, 16)

        @block.scalar
        def _(scalar):
            scalar.wait_ge(s_ld, 128)
            for i in range(NT):
                if i >= 1:
                    scalar.wait_ge(s_asq, i)
                scalar.activation(out=sq_scr[:, :], in_=x_all[:, i * D:(i + 1) * D],
                                  func=AF.Square, bias=0.0, scale=1.0,
                                  accum_out=sumsq16[:, i:i + 1]).then_inc(s_asq, 1)
            scalar.wait_ge(s_vvar, 1)
            scalar.activation(out=sd16[:, :], in_=tmpA[:, :], func=AF.Sqrt,
                              bias=0.0, scale=1.0).then_inc(s_asd, 1)
            if rms:
                scalar.wait_ge(s_vr2, 1)
                scalar.activation(out=srms16[:, :], in_=tmpB[:, :], func=AF.Sqrt,
                                  bias=0.0, scale=1.0).then_inc(s_asd, 1)
            for j in range(NT * 8):
                scalar.wait_ge(s_pe, j + 1)
                i, k = divmod(j, 8)
                scalar.copy(out=XT[:, k * T + i * 128:k * T + (i + 1) * 128],
                            in_=(pXT0 if j % 2 == 0 else pXT1)[:, :]).then_inc(s_aev, 1)
            for i in range(NT):
                scalar.wait_ge(s_vd, i + 1)
                if topk == 2:
                    scalar.activation(out=w1_all[:, i:i + 1], in_=d12[:, i:i + 1],
                                      func=AF.Sigmoid, bias=0.0, scale=1.0).then_inc(s_asig, 1)
                else:
                    scalar.activation(out=expsc[:, :], in_=lg[:, :], func=AF.Exp,
                                      bias=d12[:, i:i + 1], scale=1.0,
                                      accum_out=zcol[:, i:i + 1]).then_inc(s_asig, 1)
            for i in range(NT):
                scalar.wait_ge(s_peb, i + 1)
                scalar.copy(out=tob_s[:, :], in_=pB2[:, :]).then_inc(s_aev2, 1)

        @block.tensor
        def _(tensor):
            tensor.wait_ge(s_gld, 32)
            tensor.wait_ge(s_ld, 128)
            for j in range(NT * 8):
                i, k = divmod(j, 8)
                if j >= 2:
                    tensor.wait_ge(s_aev, j - 1)
                tensor.transpose((pXT0 if j % 2 == 0 else pXT1)[:, :],
                                 xbf_all[:, i * D + k * 128:i * D + (k + 1) * 128],
                                 ident_s[:, :]).then_inc(s_pe, 1)
            for k in range(8):
                mm = tensor.matmul(pR[:, :], onescol_s[:, :], rwTs[:, k * E:(k + 1) * E],
                                   start=(k == 0), stop=(k == 7))
            mm.then_inc(s_perb, 1)
            tensor.wait_ge(s_vscale, 2)
            tensor.matmul(pB[:, :], onesrow_s[:1, :], rws_s[:1, :], start=True, stop=True).then_inc(s_perb, 1)
            tensor.wait_ge(s_aev, NT * 8)
            for i in range(NT):
                if i >= 1:
                    tensor.wait_ge(s_vmask, i)
                for k in range(8):
                    mm = tensor.matmul(pLG[:, :], XT[:, k * T + i * 128:k * T + (i + 1) * 128],
                                       rwTb[:, k * E:(k + 1) * E],
                                       start=(k == 0), stop=(k == 7))
                mm.then_inc(s_pelg, 1)
                tensor.wait_ge(s_vmask, i + 1)
                if i >= 1:
                    tensor.wait_ge(s_vtot, i)
                tensor.matmul(pC[:, :], lexT_s[:, :], maskC[:, :], start=True, stop=True).then_inc(s_pec, 1)
                tensor.matmul(pT[:, :], onescol_s[:, :], maskC[:, :], start=True, stop=True).then_inc(s_pec, 1)
            tensor.wait_ge(s_vscan, 1)
            for i in range(NT):
                tensor.wait_ge(s_vtof, i + 1)
                if i >= 1:
                    tensor.wait_ge(s_aev2, i)
                tensor.matmul(pB2[:, :], onesrow_s[:1, :], tofc[:1, :], start=True, stop=True).then_inc(s_peb, 1)

        @block.vector
        def _(vector):
            vector.wait_ge(s_ld, 128)
            for i in range(NT):
                vector.reduce_sum(sum16[:, i:i + 1], x_all[:, i * D:(i + 1) * D], axis=AX.X)
            vector.nop().then_inc(s_vsum, 1)
            vector.wait_ge(s_asq, NT)
            vector.drain()
            vector.tensor_scalar(out=mean16[:, :], in0=sum16[:, :], scalar1=1.0 / D, scalar2=None, op0=ALU.mult)
            vector.tensor_scalar(out=tmpA[:, :], in0=sumsq16[:, :], scalar1=1.0 / D, scalar2=None, op0=ALU.mult)
            vector.drain()
            vector.tensor_tensor(out=tmpB[:, :], in0=mean16[:, :], in1=mean16[:, :], op=ALU.mult)
            vector.drain()
            vector.tensor_tensor(out=var16[:, :], in0=tmpA[:, :], in1=tmpB[:, :], op=ALU.subtract)
            vector.drain()
            vector.tensor_scalar(out=tmpA[:, :], in0=var16[:, :], scalar1=LN_EPS, scalar2=None, op0=ALU.add)
            vector.drain().then_inc(s_vvar, 1)
            if rms:
                vector.reciprocal(tmpB[:, :], tmpA[:, :])
                vector.drain()
                vector.tensor_tensor(out=tmpB[:, :], in0=var16[:, :], in1=tmpB[:, :], op=ALU.mult)
                vector.drain()
                vector.tensor_scalar(out=tmpB[:, :], in0=tmpB[:, :], scalar1=RMS_EPS, scalar2=None, op0=ALU.add)
                vector.drain().then_inc(s_vr2, 1)
            vector.wait_ge(s_asd, 1)
            vector.reciprocal(rstd16[:, :], sd16[:, :])
            vector.drain()
            if rms:
                vector.wait_ge(s_asd, 2)
                vector.reciprocal(scale16[:, :], srms16[:, :])
                vector.drain()
                vector.tensor_tensor(out=scale16[:, :], in0=scale16[:, :], in1=rstd16[:, :], op=ALU.mult)
            else:
                vector.tensor_copy(scale16[:, :], rstd16[:, :])
            vector.drain()
            vector.tensor_tensor(out=mus16[:, :], in0=mean16[:, :], in1=scale16[:, :], op=ALU.mult)
            vector.drain().then_inc(s_vscale, 1)
            vector.wait_ge(s_perb, 1)
            vector.tensor_copy(rws_s[:1, :], pR[:1, :])
            vector.drain().then_inc(s_vscale, 1)
            vector.wait_ge(s_perb, 2)
            vector.tensor_copy(rwsb_s[:, :], pB[:, :])
            vector.drain()
            for i in range(NT):
                vector.tensor_scalar(out=xn_sb[:, i * D:(i + 1) * D],
                                     in0=x_all[:, i * D:(i + 1) * D],
                                     scalar1=mean16[:, i:i + 1], scalar2=rstd16[:, i:i + 1],
                                     op0=ALU.subtract, op1=ALU.mult).then_inc(s_vxn, 1)
            for i in range(NT):
                vector.drain()
                vector.wait_ge(s_pelg, i + 1)
                if topk == 1 and i >= 1:
                    vector.wait_ge(s_asig, i)
                vector.tensor_scalar(out=lga[:, :], in0=pLG[:, :], scalar1=scale16[:, i:i + 1], scalar2=None, op0=ALU.mult)
                vector.tensor_scalar(out=lgb[:, :], in0=rwsb_s[:, :], scalar1=mus16[:, i:i + 1], scalar2=None, op0=ALU.mult)
                vector.drain()
                vector.tensor_tensor(out=lg[:, :], in0=lga[:, :], in1=lgb[:, :], op=ALU.subtract)
                vector.drain()
                vector.max(m8[:, :], lg[:, :])
                vector.drain()
                vector.max_index(i8[:, :], m8[:, :], lg[:, :])
                vector.drain()
                vector.tensor_copy(e1_all[:, i:i + 1], i8[:, 0:1])
                if topk == 2:
                    vector.tensor_copy(e2_all[:, i:i + 1], i8[:, 1:2])
                    vector.tensor_tensor(out=d12[:, i:i + 1], in0=m8[:, 0:1], in1=m8[:, 1:2], op=ALU.subtract)
                else:
                    vector.tensor_scalar(out=d12[:, i:i + 1], in0=m8[:, 0:1], scalar1=-1.0,
                                         scalar2=None, op0=ALU.mult)
                vector.drain().then_inc(s_vd, 1)
                vector.tensor_scalar(out=mask1_all[:, i * E:(i + 1) * E], in0=e8_s[:, :],
                                     scalar1=e1_all[:, i:i + 1], scalar2=None, op0=ALU.is_equal)
                if topk == 2:
                    vector.tensor_scalar(out=mask2_all[:, i * E:(i + 1) * E], in0=e8_s[:, :],
                                         scalar1=e2_all[:, i:i + 1], scalar2=None, op0=ALU.is_equal)
                    vector.drain()
                    vector.tensor_tensor(out=maskC[:, :], in0=mask1_all[:, i * E:(i + 1) * E],
                                         in1=mask2_all[:, i * E:(i + 1) * E], op=ALU.add)
                else:
                    vector.drain()
                    vector.tensor_copy(maskC[:, :], mask1_all[:, i * E:(i + 1) * E])
                vector.drain().then_inc(s_vmask, 1)
                vector.wait_ge(s_pec, 2 * (i + 1))
                vector.tensor_copy(cume_all[:, i * E:(i + 1) * E], pC[:, :])
                trow = (totrowA if i < 8 else totrowB)
                vector.tensor_copy(trow[0:1, (i % 8) * E:(i % 8 + 1) * E], pT[:1, :])
                vector.drain().then_inc(s_vtot, 1)
            vector.memzero(totA[:1, :])
            vector.memzero(totB[:1, :])
            vector.drain()
            for e in range(E):
                rA = totrowA[0:1, :].rearrange("p (j e2) -> p e2 j", e2=8)
                vector.tensor_tensor_scan(
                    out=totA[0:1, e * 9 + 1:e * 9 + 9], data0=onesrow_s[0:1, 0:8],
                    data1=rA[0:1, e:e + 1, :].rearrange("p a j -> p (a j)"),
                    initial=0.0, op0=ALU.mult, op1=ALU.add)
                rB = totrowB[0:1, :].rearrange("p (j e2) -> p e2 j", e2=8)
                vector.tensor_tensor_scan(
                    out=totB[0:1, e * 9 + 1:e * 9 + 9], data0=onesrow_s[0:1, 0:8],
                    data1=rB[0:1, e:e + 1, :].rearrange("p a j -> p (a j)"),
                    initial=0.0, op0=ALU.mult, op1=ALU.add)
            vector.drain()
            cAr = totA[0:1, :].rearrange("p (e2 j) -> p j e2", j=9)
            cBr = totB[0:1, :].rearrange("p (e2 j) -> p j e2", j=9)
            vector.tensor_copy(cnt_s[0:1, 0:8], cAr[0:1, 8:9, :].rearrange("p a e2 -> p (a e2)"))
            vector.tensor_copy(cnt_s[0:1, 8:16], cBr[0:1, 8:9, :].rearrange("p a e2 -> p (a e2)"))
            vector.drain().then_inc(s_vscan, 1)
            for i in range(NT):
                vector.drain()
                seg = 0 if i < 8 else 1
                srcr = cAr if seg == 0 else cBr
                m = i % 8
                vector.tensor_copy(tofc[0:1, :], srcr[0:1, m:m + 1, :].rearrange("p a e2 -> p (a e2)"))
                vector.drain().then_inc(s_vtof, 1)
                vector.wait_ge(s_aev2, i + 1)
                vector.tensor_tensor(out=cumfull[:, :], in0=cume_all[:, i * E:(i + 1) * E],
                                     in1=tob_s[:, :], op=ALU.add)
                vector.drain()
                vector.tensor_tensor(out=selt[:, :], in0=cumfull[:, :],
                                     in1=mask1_all[:, i * E:(i + 1) * E], op=ALU.mult)
                vector.drain()
                vector.reduce_sum(pos1[:, :], selt[:, :], axis=AX.X)
                vector.drain()
                if topk == 2:
                    vector.tensor_tensor(out=selt[:, :], in0=cumfull[:, :],
                                         in1=mask2_all[:, i * E:(i + 1) * E], op=ALU.mult)
                    vector.drain()
                    vector.reduce_sum(pos2[:, :], selt[:, :], axis=AX.X)
                    vector.drain()
                vector.scalar_tensor_tensor(out=idxf[:, :], in0=e1_all[:, i:i + 1],
                                            scalar=float(SLOC), in1=pos1[:, :],
                                            op0=ALU.mult, op1=ALU.add)
                vector.drain()
                vector.tensor_scalar(out=idxf[:, :], in0=idxf[:, :], scalar1=float(seg * E * SLOC), scalar2=None, op0=ALU.add)
                vector.drain()
                vector.tensor_copy(idx1_i[:, i:i + 1], idxf[:, :])
                vector.drain()
                if topk == 2:
                    vector.scalar_tensor_tensor(out=idxf[:, :], in0=e2_all[:, i:i + 1],
                                                scalar=float(SLOC), in1=pos2[:, :],
                                                op0=ALU.mult, op1=ALU.add)
                    vector.drain()
                    vector.tensor_scalar(out=idxf[:, :], in0=idxf[:, :], scalar1=float(seg * E * SLOC), scalar2=None, op0=ALU.add)
                    vector.drain()
                    vector.tensor_copy(idx2_i[:, i:i + 1], idxf[:, :])
                vector.drain().then_inc(s_vidx, 1)
                vector.wait_ge(s_asig, i + 1)
                mb8 = i * 8
                vector.tensor_copy(meta_s[:, mb8 + 0:mb8 + 1], e1_all[:, i:i + 1])
                vector.tensor_copy(meta_s[:, mb8 + 1:mb8 + 2], pos1[:, :])
                if topk == 2:
                    vector.tensor_copy(meta_s[:, mb8 + 2:mb8 + 3], w1_all[:, i:i + 1])
                    vector.tensor_copy(meta_s[:, mb8 + 3:mb8 + 4], e2_all[:, i:i + 1])
                    vector.tensor_copy(meta_s[:, mb8 + 4:mb8 + 5], pos2[:, :])
                    vector.tensor_scalar(out=meta_s[:, mb8 + 5:mb8 + 6], in0=w1_all[:, i:i + 1],
                                         scalar1=-1.0, scalar2=1.0, op0=ALU.mult, op1=ALU.add)
                    vector.memzero(meta_s[:, mb8 + 6:mb8 + 8])
                else:
                    vector.reciprocal(meta_s[:, mb8 + 2:mb8 + 3], zcol[:, i:i + 1])
                    vector.memzero(meta_s[:, mb8 + 3:mb8 + 8])
                vector.drain().then_inc(s_vmeta, 1)

    return nc



def stage_x_shard(X, c):
    sl = X[:, c * SLOC:(c + 1) * SLOC, :]
    return np.ascontiguousarray(sl.reshape(T, D))


def stage_A_inmaps(X, router_w):
    rwT = np.ascontiguousarray(router_w.T)
    maps = []
    for c in range(NCORES):
        maps.append({
            "xs": stage_x_shard(X, c),
            "rwT": rwT,
            "agrow": np.ascontiguousarray(
                (c * T + np.arange(T, dtype=np.float32)).reshape(NT, 128).T),
        })
    return maps


_CACHE = {}


def _get_nc(name, builder, *args):
    key = (name,) + tuple(args)
    if key not in _CACHE:
        _CACHE[key] = builder(*args)
    return _CACHE[key]


def _run(nc, maps):
    return run_bass_kernel_spmd(nc, maps, core_ids=list(range(NCORES))).results


def _meta_to_combine(meta):
    return np.ascontiguousarray(meta.reshape(NT, 128, 8).transpose(1, 0, 2).reshape(128, NT * 8))


def kernel(**inputs):
    g = {k: np.asarray(v) for k, v in inputs.items()}
    X = np.ascontiguousarray(g["L0_attn_X"], dtype=np.float32)

    ncA = _get_nc("A2", build_launch_A, 2, True)
    resA = _run(ncA, stage_A_inmaps(X, g["L0_ffn_router_w"]))
    xn_sh = [np.asarray(resA[c]["xn_o"]) for c in range(NCORES)]
    xnall = np.ascontiguousarray(np.concatenate(xn_sh, 0))
    d2g = np.stack([np.asarray(resA[c]["d2_o"]).reshape(2, E, SLOC) for c in range(NCORES)])
    D2G = np.zeros((16, E, SLOC), np.float32)
    cntG = np.zeros((16, E), np.float32)
    for c in range(NCORES):
        cr = np.asarray(resA[c]["cnt_o"]).reshape(2, E)
        for b in range(2):
            D2G[b * 8 + c] = d2g[c, b]
            cntG[b * 8 + c] = cr[b]
    metas0 = [np.asarray(resA[c]["meta_o"]) for c in range(NCORES)]

    wkvg, wq, wo, bg = stage_B_weights(g["L0_attn_w_G"], g["L0_attn_b_G"],
                                       g["L0_attn_w_K"], g["L0_attn_w_V"],
                                       g["L0_attn_w_Q"], g["L0_attn_w_O"])
    mapsB = []
    for c in range(NCORES):
        xprev = np.zeros((512, D), dtype=xn_sh[0].dtype)
        if c > 0:
            xprev[0:256] = xn_sh[c - 1][768:1024]
            xprev[256:512] = xn_sh[c - 1][1792:2048]
        mapsB.append(dict(
            xnm_i=xn_sh[c], xnp_i=xprev, wkvg_i=wkvg, wq_i=wq, wo_i=wo, bg_i=bg,
            mk_i=np.full((64, 1), 0.0 if c == 0 else 1.0, np.float32),
            ini_i=stage_B_inits(g["L0_attn_init_K_rec"], g["L0_attn_init_V_rec"], c)))
    ncB = _get_nc("B", build_launch_B)
    resB = _run(ncB, mapsB)
    attn0_sh = [np.asarray(resB[c]["attn_o"]) for c in range(NCORES)]

    sg = np.ascontiguousarray(g["L0_ffn_sg"], np.float32)
    su = np.ascontiguousarray(g["L0_ffn_su"], np.float32)
    sd = np.ascontiguousarray(g["L0_ffn_sd"], np.float32)
    ncC = _get_nc("C", build_launch_C)
    resC = _run(ncC, [dict(xn_i=xn_sh[c], sg_i=sg, su_i=su, sd_i=sd) for c in range(NCORES)])
    shared_sh = [np.ascontiguousarray(np.asarray(resC[c]["sharedT_o"]).T) for c in range(NCORES)]

    ncD0 = _get_nc("D0", build_launch_D, "l0")
    mapsD = []
    for e in range(NCORES):
        mapsD.append(dict(
            xnall_i=xnall,
            d2my_i=np.ascontiguousarray(D2G[:, e, :].reshape(-1, 1)),
            cntmy_i=np.ascontiguousarray(cntG[:, e].reshape(1, 16)),
            wg_i=np.ascontiguousarray(g["L0_ffn_rg"][e], np.float32),
            wu_i=np.ascontiguousarray(g["L0_ffn_ru"][e], np.float32),
            wd_i=np.ascontiguousarray(g["L0_ffn_rd"][e], np.float32)))
    resD0 = _run(ncD0, mapsD)
    eout0 = np.ascontiguousarray(np.concatenate(
        [np.asarray(resD0[e]["eout_o"]) for e in range(NCORES)], 0))

    ncE0 = _get_nc("E0", build_combine, 0)
    mapsE = []
    for c in range(NCORES):
        maskA = (np.arange(16) < c).astype(np.float32).reshape(16, 1)
        maskB = (np.arange(16) < 8 + c).astype(np.float32).reshape(16, 1)
        mapsE.append(dict(
            xres_i=stage_x_shard(X, c), attn_i=attn0_sh[c], shared_i=shared_sh[c],
            eout_i=eout0, meta_i=_meta_to_combine(metas0[c]),
            cnt_i=np.ascontiguousarray(cntG), maskA_i=maskA, maskB_i=maskB))
    resE0 = _run(ncE0, mapsE)
    x1_sh = [np.asarray(resE0[c]["xo_o"]) for c in range(NCORES)]

    ncA1 = _get_nc("A1", build_launch_A, 1, False)
    mapsA1 = []
    rwT1 = np.ascontiguousarray(g["L1_ffn_wg"], np.float32)
    for c in range(NCORES):
        mapsA1.append({"xs": x1_sh[c], "rwT": rwT1,
                       "agrow": np.ascontiguousarray(
                           (c * T + np.arange(T, dtype=np.float32)).reshape(NT, 128).T)})
    resA1 = _run(ncA1, mapsA1)
    xn1_sh = [np.asarray(resA1[c]["xn_o"]) for c in range(NCORES)]
    xn1all = np.ascontiguousarray(np.concatenate(xn1_sh, 0))
    d2g1 = np.stack([np.asarray(resA1[c]["d2_o"]).reshape(2, E, SLOC) for c in range(NCORES)])
    D2G1 = np.zeros((16, E, SLOC), np.float32)
    cntG1 = np.zeros((16, E), np.float32)
    for c in range(NCORES):
        cr = np.asarray(resA1[c]["cnt_o"]).reshape(2, E)
        for b in range(2):
            D2G1[b * 8 + c] = d2g1[c, b]
            cntG1[b * 8 + c] = cr[b]
    metas1 = [np.asarray(resA1[c]["meta_o"]) for c in range(NCORES)]

    ncF = _get_nc("F", build_launch_F)
    mapsF = []
    fw = dict(
        w1T_i=np.ascontiguousarray(g["L1_attn_w1"].T, np.float32),
        b1bc_i=np.tile(np.asarray(g["L1_attn_b1"], np.float32), (128, 1)),
        wwT_i=np.ascontiguousarray(g["L1_attn_ww"].T, np.float32),
        bwbc_i=np.tile(np.asarray(g["L1_attn_bw"], np.float32), (128, 1)),
        w2T_i=np.ascontiguousarray(g["L1_attn_w2"].T, np.float32),
        b2bc_i=np.tile(np.asarray(g["L1_attn_b2"], np.float32), (128, 1)),
        gbc_i=np.tile(np.asarray(g["L1_attn_gamma"], np.float32), (128, 1)),
        bbc_i=np.tile(np.asarray(g["L1_attn_beta"], np.float32), (128, 1)))
    for c in range(NCORES):
        m = dict(fw)
        m["xn_i"] = xn1_sh[c]
        mapsF.append(m)
    resF = _run(ncF, mapsF)
    attn1_sh = [np.asarray(resF[c]["attn_o"]) for c in range(NCORES)]

    ncD1 = _get_nc("D1", build_launch_D, "l1")
    mapsG = []
    for e in range(NCORES):
        mapsG.append(dict(
            xnall_i=xn1all,
            d2my_i=np.ascontiguousarray(D2G1[:, e, :].reshape(-1, 1)),
            cntmy_i=np.ascontiguousarray(cntG1[:, e].reshape(1, 16)),
            wg_i=np.ascontiguousarray(g["L1_ffn_w1"][e], np.float32),
            wd_i=np.ascontiguousarray(g["L1_ffn_w2"][e], np.float32),
            b1_i=np.ascontiguousarray(np.asarray(g["L1_ffn_b1"][e], np.float32).reshape(32, 128).T),
            b2bc_i=np.tile(np.asarray(g["L1_ffn_b2"][e], np.float32), (128, 1))))
    resD1 = _run(ncD1, mapsG)
    eout1 = np.ascontiguousarray(np.concatenate(
        [np.asarray(resD1[e]["eout_o"]) for e in range(NCORES)], 0))

    ncE1 = _get_nc("E1", build_combine, 1)
    mapsE1 = []
    for c in range(NCORES):
        maskA = (np.arange(16) < c).astype(np.float32).reshape(16, 1)
        maskB = (np.arange(16) < 8 + c).astype(np.float32).reshape(16, 1)
        mapsE1.append(dict(
            xres_i=x1_sh[c], attn_i=attn1_sh[c],
            eout_i=eout1, meta_i=_meta_to_combine(metas1[c]),
            cnt_i=np.ascontiguousarray(cntG1), maskA_i=maskA, maskB_i=maskB))
    resE1 = _run(ncE1, mapsE1)

    out = np.zeros((B, S, D), np.float32)
    for c in range(NCORES):
        sh = np.asarray(resE1[c]["xo_o"]).reshape(2, SLOC, D)
        out[0, c * SLOC:(c + 1) * SLOC] = sh[0]
        out[1, c * SLOC:(c + 1) * SLOC] = sh[1]
    return out



NI = 8192
NMI = NI // 128


def build_launch_C():
    nc = bass.Bass("TRN2", target_bir_lowering=False, debug=False)
    xn_i = nc.dram_tensor("xn_i", [T, D], BF16, kind="ExternalInput").ap()
    sg_i = nc.dram_tensor("sg_i", [D, NI], F32, kind="ExternalInput").ap()
    su_i = nc.dram_tensor("su_i", [D, NI], F32, kind="ExternalInput").ap()
    sd_i = nc.dram_tensor("sd_i", [NI, D], F32, kind="ExternalInput").ap()
    sharedT_o = nc.dram_tensor("sharedT_o", [D, T], F32, kind="ExternalOutput").ap()
    gu_d = nc.dram_tensor("gu_d", [NMI, 128, T], BF16, kind="Internal")

    es = ExitStack()
    with es:
        sb = lambda name, shape, dt: es.enter_context(nc.sbuf_tensor(name, shape, dt))
        ps = lambda name, shape, dt: es.enter_context(nc.psum_tensor(name, shape, dt))
        xnT = sb("xnT", [128, 8 * T], BF16)
        sgb = sb("sgb", [128, 2 * 8 * 128], BF16)
        sub = sb("sub", [128, 2 * 8 * 128], BF16)
        gsil = sb("gsil", [128, 1024], F32)
        gu_sb = sb("gu_sb", [128, 2 * T], BF16)
        sdb = sb("sdb", [128, NMI * 256], BF16)
        gub = sb("gub", [128, 4 * T], BF16)
        outb = sb("outb", [128, 2 * T], F32)
        pGA = ps("pGA", [128, T], F32)
        pGB = ps("pGB", [128, T], F32)

        s_ld = es.enter_context(nc.semaphore("s_ld"))
        s_wgP = [es.enter_context(nc.semaphore(f"s_wg{p}")) for p in range(2)]
        s_mm1 = es.enter_context(nc.semaphore("s_mm1"))
        s_sil = es.enter_context(nc.semaphore("s_sil"))
        s_gu = es.enter_context(nc.semaphore("s_gu"))
        s_gstP = [es.enter_context(nc.semaphore(f"s_gst{p}")) for p in range(2)]
        s_sd = es.enter_context(nc.semaphore("s_sd"))
        s_gldP = [es.enter_context(nc.semaphore(f"s_gld{p}")) for p in range(2)]
        s_mm2 = es.enter_context(nc.semaphore("s_mm2"))
        s_oev = es.enter_context(nc.semaphore("s_oev"))
        s_ost = es.enter_context(nc.semaphore("s_ost"))
        block = es.enter_context(nc.Block())

        @block.sync
        def _(sync):
            for k in range(8):
                sync.dma_start(out=xnT[:, k * T:(k + 1) * T],
                               in_=xn_i[:, k * 128:(k + 1) * 128],
                               transpose=True).then_inc(s_ld, 16)
            for mi in range(NMI):
                sync.wait_ge(s_gu, 8 * mi + 8)
                sync.dma_start(out=gu_d.ap()[mi, :, :],
                               in_=gu_sb[:, (mi % 2) * T:(mi % 2 + 1) * T]).then_inc(s_gstP[mi % 2], 16)
            for mg in range(4):
                for k2 in range(NMI // 2):
                    j2 = mg * (NMI // 2) + k2
                    if j2 >= 2:
                        sync.wait_ge(s_mm2, 2 * (j2 - 1))
                    sync.wait_ge(s_gstP[0], 16 * (k2 + 1))
                    sync.wait_ge(s_gstP[1], 16 * (k2 + 1))
                    sync.dma_start(out=gub[:, (j2 % 2) * 2 * T:(j2 % 2 + 1) * 2 * T]
                                   .rearrange("p (a t) -> p a t", a=2),
                                   in_=gu_d.ap()[2 * k2:2 * k2 + 2, :, :]
                                   .rearrange("a p t -> p a t")).then_inc(s_gldP[j2 % 2], 16)
                for md in range(2):
                    sync.wait_ge(s_oev, mg * 2 + md + 1)
                    sync.dma_start(out=sharedT_o[(mg * 2 + md) * 128:(mg * 2 + md + 1) * 128, :],
                                   in_=outb[:, md * T:(md + 1) * T]).then_inc(s_ost, 16)
            sync.wait_ge(s_ost, 8 * 16)

        @block.gpsimd
        def _(gpsimd):
            for mi in range(NMI):
                if mi >= 2:
                    gpsimd.wait_ge(s_mm1, 4 * (mi - 1))
                gpsimd.dma_start(out=sgb[:, (mi % 2) * 1024:(mi % 2) * 1024 + 1024]
                                 .rearrange("p (k i) -> p k i", i=128),
                                 in_=sg_i[:, mi * 128:(mi + 1) * 128]
                                 .rearrange("(k p) i -> p k i", p=128)).then_inc(s_wgP[mi % 2], 16)
                gpsimd.dma_start(out=sub[:, (mi % 2) * 1024:(mi % 2) * 1024 + 1024]
                                 .rearrange("p (k i) -> p k i", i=128),
                                 in_=su_i[:, mi * 128:(mi + 1) * 128]
                                 .rearrange("(k p) i -> p k i", p=128)).then_inc(s_wgP[mi % 2], 16)
            for mg in range(4):
                if mg >= 1:
                    gpsimd.wait_ge(s_mm2, mg * NMI)
                gpsimd.dma_start(
                    out=sdb[:, :].rearrange("p (k c) -> p k c", c=256),
                    in_=sd_i[:, mg * 256:(mg + 1) * 256].rearrange("(k p) c -> p k c", p=128),
                ).then_inc(s_sd, 16)

        @block.vector
        def _(vector):
            for mi in range(NMI):
                if mi >= 2:
                    vector.wait_ge(s_gstP[mi % 2], 16 * ((mi - 2) // 2 + 1))
                for n in range(4):
                    cidx = mi * 4 + n
                    np_ = (cidx % 2) * 1024
                    vector.wait_ge(s_sil, cidx + 1)
                    vector.tensor_tensor(
                        out=gu_sb[:, (mi % 2) * T + n * 512:(mi % 2) * T + (n + 1) * 512],
                        in0=gsil[:, (cidx % 2) * 512:(cidx % 2) * 512 + 512],
                        in1=pGB[:, np_:np_ + 512], op=ALU.mult).then_inc(s_gu, 2)
                    vector.drain()
            for mg in range(4):
                if mg >= 1:
                    vector.wait_ge(s_ost, 32 * mg)
                vector.wait_ge(s_mm2, mg * NMI + NMI)
                for md in range(2):
                    vector.tensor_copy(outb[:, md * T:(md + 1) * T],
                                       (pGA if md == 0 else pGB)[:, :]).then_inc(s_oev, 1)
                vector.drain()

        @block.scalar
        def _(scalar):
            for mi in range(NMI):
                for n in range(4):
                    cidx = mi * 4 + n
                    np_ = (cidx % 2) * 1024
                    scalar.wait_ge(s_mm1, cidx + 1)
                    if cidx >= 2:
                        scalar.wait_ge(s_gu, 2 * (cidx - 1))
                    scalar.activation(out=gsil[:, (cidx % 2) * 512:(cidx % 2) * 512 + 512],
                                      in_=pGA[:, np_:np_ + 512],
                                      func=AF.Silu, bias=0.0, scale=1.0).then_inc(s_sil, 1)

        @block.tensor
        def _(tensor):
            tensor.wait_ge(s_ld, 128)
            for mi in range(NMI):
                tensor.wait_ge(s_wgP[mi % 2], 32 * (mi // 2 + 1))
                base = (mi % 2) * 1024
                for n in range(4):
                    cidx = mi * 4 + n
                    if cidx >= 2:
                        tensor.wait_ge(s_gu, 2 * (cidx - 1))
                    np_ = (cidx % 2) * 1024
                    for k in range(8):
                        tensor.matmul(pGA[:, np_:np_ + 512],
                                      sgb[:, base + k * 128:base + (k + 1) * 128],
                                      xnT[:, k * T + n * 512:k * T + (n + 1) * 512],
                                      start=(k == 0), stop=(k == 7))
                    for k in range(8):
                        mm = tensor.matmul(pGB[:, np_:np_ + 512],
                                           sub[:, base + k * 128:base + (k + 1) * 128],
                                           xnT[:, k * T + n * 512:k * T + (n + 1) * 512],
                                           start=(k == 0), stop=(k == 7))
                    mm.then_inc(s_mm1, 1)
            tensor.wait_ge(s_gu, 512)
            for mg in range(4):
                tensor.wait_ge(s_sd, 16 * (mg + 1))
                if mg >= 1:
                    tensor.wait_ge(s_oev, mg * 2)
                for k in range(NMI):
                    j2 = (mg * NMI + k) // 2
                    tensor.wait_ge(s_gldP[j2 % 2], 16 * (j2 // 2 + 1))
                    for md in range(2):
                        pdst = pGA if md == 0 else pGB
                        for n in range(4):
                            mm = tensor.matmul(
                                pdst[:, n * 512:(n + 1) * 512],
                                sdb[:, k * 256 + md * 128:k * 256 + (md + 1) * 128],
                                gub[:, (((mg * NMI + k) // 2) % 2) * 2 * T + (k % 2) * T + n * 512:
                                    (((mg * NMI + k) // 2) % 2) * 2 * T + (k % 2) * T + (n + 1) * 512],
                                start=(k == 0), stop=(k == NMI - 1))
                    mm.then_inc(s_mm2, 1)

    return nc



IEXP = 4096
NSLOT_T = CAP // 128
NMIE = IEXP // 128


def build_launch_D(mode):
    nc = bass.Bass("TRN2", target_bir_lowering=False, debug=False)
    xnall_i = nc.dram_tensor("xnall_i", [NCORES * T, D], BF16, kind="ExternalInput").ap()
    d2my_i = nc.dram_tensor("d2my_i", [16 * SLOC, 1], F32, kind="ExternalInput").ap()
    cntmy_i = nc.dram_tensor("cntmy_i", [1, 16], F32, kind="ExternalInput").ap()
    wg_i = nc.dram_tensor("wg_i", [D, IEXP], F32, kind="ExternalInput").ap()
    if mode == "l0":
        wu_i = nc.dram_tensor("wu_i", [D, IEXP], F32, kind="ExternalInput").ap()
    wd_i = nc.dram_tensor("wd_i", [IEXP, D], F32, kind="ExternalInput").ap()
    if mode == "l1":
        b1_i = nc.dram_tensor("b1_i", [128, 32], F32, kind="ExternalInput").ap()
        b2bc_i = nc.dram_tensor("b2bc_i", [128, D], F32, kind="ExternalInput").ap()
    eout_o = nc.dram_tensor("eout_o", [CAP, D], BF16, kind="ExternalOutput").ap()
    xg_d = nc.dram_tensor("xg_d", [CAP, D], BF16, kind="Internal")
    h_d = nc.dram_tensor("h_d", [NMIE, 128, CAP], BF16, kind="Internal")

    slotcol_np = np.arange(CAP, dtype=np.float32).reshape(NSLOT_T, 128).T.copy()
    slotcol_c = nc.inline_tensor(slotcol_np, name="c_slotcol").ap()
    onesrow_c = nc.inline_tensor(np.ones((1, 128), np.float32), name="c_onesrow2").ap()

    groups = [list(range(g, min(g + 3, NSLOT_T))) for g in range(0, NSLOT_T, 3)]
    n_chunks = CAP // 512

    es = ExitStack()
    with es:
        sb = lambda name, shape, dt: es.enter_context(nc.sbuf_tensor(name, shape, dt))
        ps = lambda name, shape, dt: es.enter_context(nc.psum_tensor(name, shape, dt))
        cnt_s = sb("cnt_s", [1, 16], F32)
        pref = sb("pref", [1, 17], F32)
        onesrow_s = sb("onesrow_s", [1, 128], F32)
        slotcol_s = sb("slotcol_s", [128, NSLOT_T], F32)
        prefb = sb("prefb", [128, 17], F32)
        maskk = sb("maskk", [128, 16], F32)
        kcnt = sb("kcnt", [128, 1], F32)
        prefsel = sb("prefsel", [128, 16], F32)
        psel = sb("psel", [128, 1], F32)
        jcol = sb("jcol", [128, 1], F32)
        srcf = sb("srcf", [128, 1], F32)
        srci = sb("srci", [128, NSLOT_T], I32)
        d2val = sb("d2val", [128, NSLOT_T], F32)
        rowi = sb("rowi", [128, NSLOT_T], I32)
        xg_all = sb("xg_all", [128, NSLOT_T * D], BF16)
        xgT = sb("xgT", [128, 8 * CAP], BF16)
        wgb = sb("wgb", [128, 2 * 8 * 128], BF16)
        wub = sb("wub", [128, 2 * 8 * 128], BF16) if mode == "l0" else None
        gsil = sb("gsil", [128, CAP], F32) if mode == "l0" else None
        h_sb = sb("h_sb", [128, 2 * CAP], BF16)
        wdb = sb("wdb", [128, NMIE * D], BF16)
        hkb = sb("hkb", [128, 4 * CAP], BF16)
        outb = sb("outb", [128, 3 * D], BF16)
        if mode == "l1":
            b1_s = sb("b1_s", [128, 32], F32)
            b2bc_s = sb("b2bc_s", [128, D], F32)
        pP = ps("pP", [128, 512], F32)
        pU = ps("pU", [128, 512], F32)
        pO = [ps(f"pO{m}", [128, D], F32) for m in range(3)]

        s_ld = es.enter_context(nc.semaphore("s_ld"))
        s_vp = es.enter_context(nc.semaphore("s_vp"))
        s_pb = es.enter_context(nc.semaphore("s_pb"))
        s_vr = es.enter_context(nc.semaphore("s_vr"))
        s_g1 = es.enter_context(nc.semaphore("s_g1"))
        s_g2 = es.enter_context(nc.semaphore("s_g2"))
        s_vm = es.enter_context(nc.semaphore("s_vm"))
        s_xst = es.enter_context(nc.semaphore("s_xst"))
        s_xT = es.enter_context(nc.semaphore("s_xT"))
        s_wgP = [es.enter_context(nc.semaphore(f"s_wg{p}")) for p in range(2)]
        s_wd = es.enter_context(nc.semaphore("s_wd"))
        s_mm1 = es.enter_context(nc.semaphore("s_mm1"))
        s_mmc = es.enter_context(nc.semaphore("s_mmc"))
        s_sil = es.enter_context(nc.semaphore("s_sil"))
        s_h = es.enter_context(nc.semaphore("s_h"))
        s_hstP = [es.enter_context(nc.semaphore(f"s_hst{p}")) for p in range(2)]
        s_hldP = [es.enter_context(nc.semaphore(f"s_hld{p}")) for p in range(2)]
        s_mm2 = es.enter_context(nc.semaphore("s_mm2"))
        s_oev = es.enter_context(nc.semaphore("s_oev"))
        s_ost = es.enter_context(nc.semaphore("s_ost"))
        block = es.enter_context(nc.Block())

        nld = 5 if mode == "l1" else 3
        ost_counts = []
        tot = 0
        for g in groups:
            tot += len(g)
            ost_counts.append(tot)

        @block.sync
        def _(sync):
            sync.dma_start(out=cnt_s[:, :], in_=cntmy_i[:, :]).then_inc(s_ld, 16)
            sync.dma_start(out=onesrow_s[:, :], in_=onesrow_c[:, :]).then_inc(s_ld, 16)
            sync.dma_start(out=slotcol_s[:, :], in_=slotcol_c[:, :]).then_inc(s_ld, 16)
            if mode == "l1":
                sync.dma_start(out=b1_s[:, :], in_=b1_i[:, :]).then_inc(s_ld, 16)
                sync.dma_start(out=b2bc_s[:, :], in_=b2bc_i[:, :]).then_inc(s_ld, 16)
            sync.wait_ge(s_g2, 16 * NSLOT_T)
            sync.dma_start(out=xg_d.ap().rearrange("(t p) d -> p t d", p=128),
                           in_=xg_all[:, :].rearrange("p (t d) -> p t d", d=D)).then_inc(s_xst, 16)
            sync.wait_ge(s_xst, 16)
            for k in range(8):
                sync.dma_start(out=xgT[:, k * CAP:(k + 1) * CAP],
                               in_=xg_d.ap()[:, k * 128:(k + 1) * 128],
                               transpose=True).then_inc(s_xT, 16)
            for mi in range(NMIE):
                sync.wait_ge(s_h, n_chunks * (mi + 1))
                sync.dma_start(out=h_d.ap()[mi, :, :],
                               in_=h_sb[:, (mi % 2) * CAP:(mi % 2 + 1) * CAP]).then_inc(s_hstP[mi % 2], 16)
            eout_r = eout_o.rearrange("(t p) d -> p t d", p=128)
            nst = 0
            for gi, g in enumerate(groups):
                for k2 in range(NMIE // 2):
                    j2 = gi * (NMIE // 2) + k2
                    if j2 >= 2:
                        sync.wait_ge(s_mm2, 2 * (j2 - 1))
                    sync.wait_ge(s_hstP[0], 16 * (k2 + 1))
                    sync.wait_ge(s_hstP[1], 16 * (k2 + 1))
                    sync.dma_start(out=hkb[:, (j2 % 2) * 2 * CAP:(j2 % 2 + 1) * 2 * CAP]
                                   .rearrange("p (a t) -> p a t", a=2),
                                   in_=h_d.ap()[2 * k2:2 * k2 + 2, :, :]
                                   .rearrange("a p t -> p a t")).then_inc(s_hldP[j2 % 2], 16)
                for mm_, m in enumerate(g):
                    sync.wait_ge(s_oev, ost_counts[gi] - len(g) + mm_ + 1)
                    nst += 1
                    sync.dma_start(out=eout_r[:, m, :],
                                   in_=outb[:, mm_ * D:(mm_ + 1) * D]).then_inc(s_ost, 16)
            sync.wait_ge(s_ost, 16 * NSLOT_T)

        @block.gpsimd
        def _(gpsimd):
            gpsimd.wait_ge(s_vm, 1)
            for it in range(NSLOT_T):
                gpsimd.wait_ge(s_vp, 10 + it + 1)
                gpsimd.indirect_dma_start(
                    out=d2val[:, it:it + 1], out_offset=None,
                    in_=d2my_i[:, :],
                    in_offset=IndirectOffsetOnAxis(ap=srci[:, it:it + 1], axis=0),
                    bounds_check=16 * SLOC - 1, oob_is_err=False,
                ).then_inc(s_g1, 16)
                gpsimd.wait_ge(s_vr, it + 1)
                gpsimd.indirect_dma_start(
                    out=xg_all[:, it * D:(it + 1) * D], out_offset=None,
                    in_=xnall_i[:, :],
                    in_offset=IndirectOffsetOnAxis(ap=rowi[:, it:it + 1], axis=0),
                    bounds_check=NCORES * T - 1, oob_is_err=False,
                ).then_inc(s_g2, 16)
            for mi in range(NMIE):
                if mi >= 2:
                    gpsimd.wait_ge(s_mmc, n_chunks * (mi - 1))
                gpsimd.dma_start(out=wgb[:, (mi % 2) * 1024:(mi % 2) * 1024 + 1024]
                                 .rearrange("p (k i) -> p k i", i=128),
                                 in_=wg_i[:, mi * 128:(mi + 1) * 128]
                                 .rearrange("(k p) i -> p k i", p=128)).then_inc(s_wgP[mi % 2], 16)
                if mode == "l0":
                    gpsimd.dma_start(out=wub[:, (mi % 2) * 1024:(mi % 2) * 1024 + 1024]
                                     .rearrange("p (k i) -> p k i", i=128),
                                     in_=wu_i[:, mi * 128:(mi + 1) * 128]
                                     .rearrange("(k p) i -> p k i", p=128)).then_inc(s_wgP[mi % 2], 16)
            gpsimd.dma_start(out=wdb[:, :].rearrange("p (k c) -> p k c", c=D),
                             in_=wd_i[:, :].rearrange("(k p) c -> p k c", p=128)).then_inc(s_wd, 16)

        @block.vector
        def _(vector):
            vector.wait_ge(s_ld, 16 * nld)
            vector.memset(xg_all[:, :], 0.0).then_inc(s_vm, 1)
            vector.memzero(pref[:1, :])
            vector.drain()
            vector.tensor_tensor_scan(out=pref[0:1, 1:17], data0=onesrow_s[0:1, 0:16],
                                      data1=cnt_s[0:1, :], initial=0.0,
                                      op0=ALU.mult, op1=ALU.add)
            vector.drain().then_inc(s_vp, 1)
            vector.wait_ge(s_pb, 1)
            vector.tensor_copy(prefb[:, :], pP[:, 0:17])
            vector.drain().then_inc(s_pb, 1)
            for it in range(NSLOT_T):
                vector.drain()
                vector.tensor_scalar(out=maskk[:, :], in0=prefb[:, 0:16],
                                     scalar1=slotcol_s[:, it:it + 1], scalar2=None, op0=ALU.is_le)
                vector.drain()
                vector.reduce_sum(kcnt[:, :], maskk[:, :], axis=AX.X)
                vector.tensor_tensor(out=prefsel[:, :], in0=prefb[:, 0:16], in1=maskk[:, :], op=ALU.mult)
                vector.drain()
                vector.reduce_max(psel[:, :], prefsel[:, :], axis=AX.X)
                vector.drain()
                vector.tensor_scalar(out=jcol[:, :], in0=psel[:, :], scalar1=-1.0,
                                     scalar2=slotcol_s[:, it:it + 1], op0=ALU.mult, op1=ALU.add)
                vector.tensor_scalar(out=srcf[:, :], in0=kcnt[:, :], scalar1=float(SLOC),
                                     scalar2=-float(SLOC), op0=ALU.mult, op1=ALU.add)
                vector.drain()
                vector.tensor_tensor(out=srcf[:, :], in0=srcf[:, :], in1=jcol[:, :], op=ALU.add)
                vector.drain()
                vector.tensor_copy(srci[:, it:it + 1], srcf[:, :])
                vector.drain().then_inc(s_vp, 10 if it == 0 else 1)
                vector.wait_ge(s_g1, 16 * (it + 1))
                vector.tensor_copy(rowi[:, it:it + 1], d2val[:, it:it + 1])
                vector.drain().then_inc(s_vr, 1)
            if mode == "l0":
                for mi in range(NMIE):
                    if mi >= 2:
                        vector.wait_ge(s_hstP[mi % 2], 16 * ((mi - 2) // 2 + 1))
                    for n in range(n_chunks):
                        vector.wait_ge(s_sil, n_chunks * mi + n + 1)
                        vector.tensor_tensor(
                            out=h_sb[:, (mi % 2) * CAP + n * 512:(mi % 2) * CAP + (n + 1) * 512],
                            in0=gsil[:, n * 512:(n + 1) * 512],
                            in1=pU[:, :], op=ALU.mult).then_inc(s_h, 1)
                    vector.drain()
            for gi, g in enumerate(groups):
                if gi >= 1:
                    vector.wait_ge(s_ost, 16 * ost_counts[gi - 1])
                vector.wait_ge(s_mm2, (gi + 1) * NMIE)
                for mm_, m in enumerate(g):
                    if mode == "l0":
                        vector.tensor_copy(outb[:, mm_ * D:(mm_ + 1) * D], pO[mm_][:, :]).then_inc(s_oev, 1)
                    else:
                        vector.tensor_tensor(out=outb[:, mm_ * D:(mm_ + 1) * D],
                                             in0=pO[mm_][:, :], in1=b2bc_s[:, :],
                                             op=ALU.add).then_inc(s_oev, 1)
                vector.drain()

        @block.scalar
        def _(scalar):
            if mode == "l0":
                for mi in range(NMIE):
                    for n in range(n_chunks):
                        scalar.wait_ge(s_mmc, n_chunks * mi + n + 1)
                        if mi >= 1:
                            scalar.wait_ge(s_h, n_chunks * (mi - 1) + n + 1)
                        scalar.activation(out=gsil[:, n * 512:(n + 1) * 512], in_=pP[:, :],
                                          func=AF.Silu, bias=0.0, scale=1.0).then_inc(s_sil, 1)
            else:
                scalar.wait_ge(s_ld, 16 * nld)
                for mi in range(NMIE):
                    if mi >= 2:
                        scalar.wait_ge(s_hstP[mi % 2], 16 * ((mi - 2) // 2 + 1))
                    for n in range(n_chunks):
                        scalar.wait_ge(s_mmc, n_chunks * mi + n + 1)
                        scalar.activation(
                            out=h_sb[:, (mi % 2) * CAP + n * 512:(mi % 2) * CAP + (n + 1) * 512],
                            in_=pP[:, :], func=AF.Relu,
                            bias=b1_s[:, mi:mi + 1], scale=1.0).then_inc(s_h, 1)

        @block.tensor
        def _(tensor):
            tensor.wait_ge(s_vp, 1)
            tensor.wait_ge(s_ld, 16 * nld)
            tensor.matmul(pP[:, 0:17], onesrow_s[:1, :], pref[:1, :],
                          start=True, stop=True).then_inc(s_pb, 1)
            tensor.wait_ge(s_pb, 2)
            tensor.wait_ge(s_xT, 128)
            for mi in range(NMIE):
                nw = 32 if mode == "l0" else 16
                tensor.wait_ge(s_wgP[mi % 2], nw * (mi // 2 + 1))
                for n in range(n_chunks):
                    cidx = n_chunks * mi + n
                    if cidx >= 1:
                        tensor.wait_ge(s_h, cidx)
                    base = (mi % 2) * 1024
                    for k in range(8):
                        mmg = tensor.matmul(pP[:, :],
                                            wgb[:, base + k * 128:base + (k + 1) * 128],
                                            xgT[:, k * CAP + n * 512:k * CAP + (n + 1) * 512],
                                            start=(k == 0), stop=(k == 7))
                    if mode == "l0":
                        for k in range(8):
                            mmg = tensor.matmul(pU[:, :],
                                                wub[:, base + k * 128:base + (k + 1) * 128],
                                                xgT[:, k * CAP + n * 512:k * CAP + (n + 1) * 512],
                                                start=(k == 0), stop=(k == 7))
                    mmg.then_inc(s_mmc, 1)
            tensor.wait_ge(s_wd, 16)
            for gi, g in enumerate(groups):
                if gi >= 1:
                    tensor.wait_ge(s_oev, ost_counts[gi - 1])
                for k in range(NMIE):
                    j2 = (gi * NMIE + k) // 2
                    hb = (j2 % 2) * 2 * CAP + (k % 2) * CAP
                    if k % 2 == 0:
                        tensor.wait_ge(s_hldP[j2 % 2], 16 * (j2 // 2 + 1))
                    for mm_, m in enumerate(g):
                        for nn in range(2):
                            mm2 = tensor.matmul(
                                pO[mm_][:, nn * 512:(nn + 1) * 512],
                                hkb[:, hb + m * 128:hb + (m + 1) * 128],
                                wdb[:, k * D + nn * 512:k * D + (nn + 1) * 512],
                                start=(k == 0), stop=(k == NMIE - 1))
                    mm2.then_inc(s_mm2, 1)

    return nc



def build_launch_F():
    nc = bass.Bass("TRN2", target_bir_lowering=False, debug=False)
    xn_i = nc.dram_tensor("xn_i", [T, D], BF16, kind="ExternalInput").ap()
    w1T_i = nc.dram_tensor("w1T_i", [D, D], F32, kind="ExternalInput").ap()
    b1bc_i = nc.dram_tensor("b1bc_i", [128, D], F32, kind="ExternalInput").ap()
    wwT_i = nc.dram_tensor("wwT_i", [D, 496], F32, kind="ExternalInput").ap()
    bwbc_i = nc.dram_tensor("bwbc_i", [128, 496], F32, kind="ExternalInput").ap()
    w2T_i = nc.dram_tensor("w2T_i", [D, D], F32, kind="ExternalInput").ap()
    b2bc_i = nc.dram_tensor("b2bc_i", [128, D], F32, kind="ExternalInput").ap()
    gbc_i = nc.dram_tensor("gbc_i", [128, D], F32, kind="ExternalInput").ap()
    bbc_i = nc.dram_tensor("bbc_i", [128, D], F32, kind="ExternalInput").ap()
    attn_o = nc.dram_tensor("attn_o", [T, D], F32, kind="ExternalOutput").ap()
    h_dr = nc.dram_tensor("h_dr", [T, D], BF16, kind="Internal")
    cv_dr = nc.dram_tensor("cv_dr", [T, D], BF16, kind="Internal")

    es = ExitStack()
    with es:
        sb = lambda name, shape, dt: es.enter_context(nc.sbuf_tensor(name, shape, dt))
        ps = lambda name, shape, dt: es.enter_context(nc.psum_tensor(name, shape, dt))
        xT = sb("xT", [128, 8 * T], BF16)
        h_all = sb("h_all", [128, NT * D], BF16)
        w1b = sb("w1b", [128, 8 * D], BF16)
        wwb = sb("wwb", [128, 8 * 496], BF16)
        w2b = sb("w2b", [128, 8 * D], BF16)
        b1bc = sb("b1bc", [128, D], F32)
        bwbc = sb("bwbc", [128, 496], F32)
        b2bc = sb("b2bc", [128, D], F32)
        gbc = sb("gbc", [128, D], F32)
        bbc = sb("bbc", [128, D], F32)
        wl_all = sb("wl_all", [128, NT * 496], F32)
        mx = sb("mx", [128, NT * 16], F32)
        zz = sb("zz", [128, NT * 16], F32)
        rz = sb("rz", [128, NT * 16], F32)
        w29a = sb("w29a", [128, NT * 16], F32)
        w30a = sb("w30a", [128, NT * 16], F32)
        cv2 = sb("cv2", [128, 2 * D], BF16)
        tmp64 = sb("tmp64", [128, D], F32)
        xn2 = sb("xn2", [128, 2 * D], BF16)
        z2 = sb("z2", [128, 2 * D], F32)
        o2 = sb("o2", [128, 2 * D], F32)
        colA = sb("colA", [128, 8], F32)
        pM = [ps(f"pM{i}", [128, D], F32) for i in range(2)]

        s_ld = es.enter_context(nc.semaphore("s_ld"))
        s_xT = es.enter_context(nc.semaphore("s_xT"))
        s_wld = es.enter_context(nc.semaphore("s_wld"))
        s_mmh = es.enter_context(nc.semaphore("s_mmh"))
        s_hev = es.enter_context(nc.semaphore("s_hev"))
        s_hst = es.enter_context(nc.semaphore("s_hst"))
        s_hT = es.enter_context(nc.semaphore("s_hT"))
        s_mmw = es.enter_context(nc.semaphore("s_mmw"))
        s_wev = es.enter_context(nc.semaphore("s_wev"))
        s_sm = es.enter_context(nc.semaphore("s_sm"))
        s_exp = es.enter_context(nc.semaphore("s_exp"))
        s_cv = es.enter_context(nc.semaphore("s_cv"))
        s_cst = [es.enter_context(nc.semaphore(f"s_cst{p}")) for p in range(2)]
        s_cT = es.enter_context(nc.semaphore("s_cT"))
        s_mmz = es.enter_context(nc.semaphore("s_mmz"))
        s_xn2 = [es.enter_context(nc.semaphore(f"s_xn2{p}")) for p in range(2)]
        s_zev = es.enter_context(nc.semaphore("s_zev"))
        s_sq = es.enter_context(nc.semaphore("s_sq"))
        s_oev = es.enter_context(nc.semaphore("s_oev"))
        s_ostP = [es.enter_context(nc.semaphore(f"s_ost{p}")) for p in range(2)]
        block = es.enter_context(nc.Block())

        xn_r = xn_i.rearrange("(t p) d -> p t d", p=128)
        attn_r = attn_o.rearrange("(t p) d -> p t d", p=128)
        h_r = h_dr.ap().rearrange("(t p) d -> p t d", p=128)
        cv_r = cv_dr.ap().rearrange("(t p) d -> p t d", p=128)

        @block.sync
        def _(sync):
            for k in range(8):
                sync.dma_start(out=xT[:, k * T:(k + 1) * T],
                               in_=xn_i[:, k * 128:(k + 1) * 128], transpose=True).then_inc(s_xT, 16)
            for dst, srcin in [(b1bc, b1bc_i), (bwbc, bwbc_i), (b2bc, b2bc_i),
                               (gbc, gbc_i), (bbc, bbc_i)]:
                sync.dma_start(out=dst[:, :], in_=srcin[:, :]).then_inc(s_ld, 16)
            sync.wait_ge(s_hev, NT)
            sync.dma_start(out=h_r, in_=h_all[:, :].rearrange("p (t d) -> p t d", d=D)).then_inc(s_hst, 16)
            sync.wait_ge(s_hst, 16)
            sync.wait_ge(s_mmh, NT)
            for k in range(8):
                sync.dma_start(out=xT[:, k * T:(k + 1) * T],
                               in_=h_dr.ap()[:, k * 128:(k + 1) * 128], transpose=True).then_inc(s_hT, 16)
            for i in range(NT):
                sync.wait_ge(s_cv, i + 1)
                sync.dma_start(out=cv_r[:, i, :],
                               in_=cv2[:, (i % 2) * D:(i % 2 + 1) * D]).then_inc(s_cst[i % 2], 16)
            for p in range(2):
                sync.wait_ge(s_cst[p], 16 * 8)
            sync.wait_ge(s_mmw, NT)
            for k in range(8):
                sync.dma_start(out=xT[:, k * T:(k + 1) * T],
                               in_=cv_dr.ap()[:, k * 128:(k + 1) * 128], transpose=True).then_inc(s_cT, 16)
            for i in range(NT):
                if i >= 2:
                    sync.wait_ge(s_zev, i - 1)
                sync.dma_start(out=xn2[:, (i % 2) * D:(i % 2 + 1) * D],
                               in_=xn_r[:, i, :]).then_inc(s_xn2[i % 2], 16)
                sync.wait_ge(s_oev, i + 1)
                sync.dma_start(out=attn_r[:, i, :],
                               in_=o2[:, (i % 2) * D:(i % 2 + 1) * D]).then_inc(s_ostP[i % 2], 16)
            for p in range(2):
                sync.wait_ge(s_ostP[p], 16 * 8)

        @block.gpsimd
        def _(gpsimd):
            gpsimd.dma_start(out=w1b[:, :].rearrange("p (k c) -> p k c", c=D),
                             in_=w1T_i.rearrange("(k p) c -> p k c", p=128)).then_inc(s_wld, 16)
            gpsimd.dma_start(out=wwb[:, :].rearrange("p (k c) -> p k c", c=496),
                             in_=wwT_i.rearrange("(k p) c -> p k c", p=128)).then_inc(s_wld, 16)
            gpsimd.dma_start(out=w2b[:, :].rearrange("p (k c) -> p k c", c=D),
                             in_=w2T_i.rearrange("(k p) c -> p k c", p=128)).then_inc(s_wld, 16)

        @block.tensor
        def _(tensor):
            tensor.wait_ge(s_xT, 128)
            tensor.wait_ge(s_wld, 48)
            for i in range(NT):
                if i >= 2:
                    tensor.wait_ge(s_hev, i - 1)
                for n in range(2):
                    for k in range(8):
                        mm = tensor.matmul(pM[i % 2][:, n * 512:(n + 1) * 512],
                                           xT[:, k * T + i * 128:k * T + (i + 1) * 128],
                                           w1b[:, k * D + n * 512:k * D + (n + 1) * 512],
                                           start=(k == 0), stop=(k == 7))
                mm.then_inc(s_mmh, 1)
            tensor.wait_ge(s_hT, 128)
            tensor.wait_ge(s_hev, NT)
            for i in range(NT):
                if i >= 2:
                    tensor.wait_ge(s_wev, i - 1)
                for k in range(8):
                    mm = tensor.matmul(pM[i % 2][:, 0:496],
                                       xT[:, k * T + i * 128:k * T + (i + 1) * 128],
                                       wwb[:, k * 496:(k + 1) * 496],
                                       start=(k == 0), stop=(k == 7))
                mm.then_inc(s_mmw, 1)
            tensor.wait_ge(s_cT, 128)
            tensor.wait_ge(s_wev, NT)
            for i in range(NT):
                if i >= 2:
                    tensor.wait_ge(s_zev, i - 1)
                for n in range(2):
                    for k in range(8):
                        mm = tensor.matmul(pM[i % 2][:, n * 512:(n + 1) * 512],
                                           xT[:, k * T + i * 128:k * T + (i + 1) * 128],
                                           w2b[:, k * D + n * 512:k * D + (n + 1) * 512],
                                           start=(k == 0), stop=(k == 7))
                mm.then_inc(s_mmz, 1)

        @block.vector
        def _(vector):
            vector.wait_ge(s_ld, 80)
            for i in range(NT):
                vector.wait_ge(s_mmh, i + 1)
                vector.tensor_tensor(out=h_all[:, i * D:(i + 1) * D], in0=pM[i % 2][:, :],
                                     in1=b1bc[:, :], op=ALU.add).then_inc(s_hev, 1)
                vector.drain()
            for i in range(NT):
                vector.wait_ge(s_mmw, i + 1)
                vector.tensor_tensor(out=wl_all[:, i * 496:(i + 1) * 496],
                                     in0=pM[i % 2][:, 0:496], in1=bwbc[:, :],
                                     op=ALU.add).then_inc(s_wev, 1)
                vector.drain()
            wl4 = wl_all[:, :].rearrange("p (g kk) -> p g kk", kk=31)
            vector.reduce_max(mx[:, :], wl4, axis=AX.X)
            vector.drain()
            for gidx in range(NT * 16):
                vector.tensor_scalar(out=wl_all[:, gidx * 31:(gidx + 1) * 31],
                                     in0=wl_all[:, gidx * 31:(gidx + 1) * 31],
                                     scalar1=mx[:, gidx:gidx + 1], scalar2=None, op0=ALU.subtract)
            vector.drain().then_inc(s_sm, 1)
            vector.wait_ge(s_exp, 1)
            vector.reduce_sum(zz[:, :], wl4, axis=AX.X)
            vector.drain()
            vector.reciprocal(rz[:, :], zz[:, :])
            vector.drain()
            wlk = wl_all[:, :].rearrange("p (g kk) -> p kk g", kk=31)
            vector.tensor_copy(w29a[:, :], wlk[:, 29:30, :].rearrange("p a g -> p (a g)"))
            vector.tensor_copy(w30a[:, :], wlk[:, 30:31, :].rearrange("p a g -> p (a g)"))
            vector.drain()
            vector.tensor_tensor(out=w29a[:, :], in0=w29a[:, :], in1=rz[:, :], op=ALU.mult)
            vector.tensor_tensor(out=w30a[:, :], in0=w30a[:, :], in1=rz[:, :], op=ALU.mult)
            vector.drain()
            for i in range(NT):
                if i >= 2:
                    vector.wait_ge(s_cst[i % 2], 16 * (i // 2))
                if i < 8:
                    for hh in range(16):
                        vector.tensor_scalar(
                            out=cv2[:, (i % 2) * D + hh * 64:(i % 2) * D + (hh + 1) * 64],
                            in0=h_all[:, i * D + hh * 64:i * D + (hh + 1) * 64],
                            scalar1=w30a[:, i * 16 + hh:i * 16 + hh + 1], scalar2=None,
                            op0=ALU.mult)
                else:
                    for hh in range(16):
                        vector.tensor_scalar(
                            out=tmp64[:, hh * 64:(hh + 1) * 64],
                            in0=h_all[:, (i - 8) * D + hh * 64:(i - 8) * D + (hh + 1) * 64],
                            scalar1=w29a[:, i * 16 + hh:i * 16 + hh + 1], scalar2=None,
                            op0=ALU.mult)
                    vector.drain()
                    for hh in range(16):
                        vector.scalar_tensor_tensor(
                            out=cv2[:, (i % 2) * D + hh * 64:(i % 2) * D + (hh + 1) * 64],
                            in0=h_all[:, i * D + hh * 64:i * D + (hh + 1) * 64],
                            scalar=w30a[:, i * 16 + hh:i * 16 + hh + 1],
                            in1=tmp64[:, hh * 64:(hh + 1) * 64],
                            op0=ALU.mult, op1=ALU.add)
                vector.drain().then_inc(s_cv, 1)
            for i in range(NT):
                vector.wait_ge(s_mmz, i + 1)
                vector.wait_ge(s_xn2[i % 2], 16 * (i // 2 + 1))
                if i >= 2:
                    vector.wait_ge(s_ostP[i % 2], 16 * (i // 2))
                bb = (i % 2) * D
                vector.tensor_tensor(out=z2[:, bb:bb + D], in0=pM[i % 2][:, :],
                                     in1=b2bc[:, :], op=ALU.add)
                vector.drain().then_inc(s_zev, 1)
                vector.tensor_tensor(out=z2[:, bb:bb + D], in0=z2[:, bb:bb + D],
                                     in1=xn2[:, bb:bb + D], op=ALU.add)
                vector.drain()
                vector.reduce_sum(colA[:, 0:1], z2[:, bb:bb + D], axis=AX.X)
                vector.drain().then_inc(s_sq, 1)
                vector.wait_ge(s_sq, 4 * i + 2)
                vector.tensor_scalar(out=colA[:, 1:2], in0=colA[:, 0:1], scalar1=1.0 / D,
                                     scalar2=None, op0=ALU.mult)
                vector.tensor_scalar(out=colA[:, 2:3], in0=colA[:, 4:5], scalar1=1.0 / D,
                                     scalar2=None, op0=ALU.mult)
                vector.drain()
                vector.tensor_tensor(out=colA[:, 3:4], in0=colA[:, 1:2], in1=colA[:, 1:2], op=ALU.mult)
                vector.drain()
                vector.tensor_tensor(out=colA[:, 3:4], in0=colA[:, 2:3], in1=colA[:, 3:4], op=ALU.subtract)
                vector.drain()
                vector.tensor_scalar(out=colA[:, 3:4], in0=colA[:, 3:4], scalar1=LN_EPS,
                                     scalar2=None, op0=ALU.add)
                vector.drain().then_inc(s_sq, 1)
                vector.wait_ge(s_sq, 4 * i + 4)
                vector.reciprocal(colA[:, 6:7], colA[:, 5:6])
                vector.drain()
                vector.tensor_scalar(out=o2[:, bb:bb + D], in0=z2[:, bb:bb + D],
                                     scalar1=colA[:, 1:2], scalar2=colA[:, 6:7],
                                     op0=ALU.subtract, op1=ALU.mult)
                vector.drain()
                vector.tensor_tensor(out=o2[:, bb:bb + D], in0=o2[:, bb:bb + D],
                                     in1=gbc[:, :], op=ALU.mult)
                vector.drain()
                vector.tensor_tensor(out=o2[:, bb:bb + D], in0=o2[:, bb:bb + D],
                                     in1=bbc[:, :], op=ALU.add)
                vector.drain().then_inc(s_oev, 1)

        @block.scalar
        def _(scalar):
            scalar.wait_ge(s_sm, 1)
            scalar.activation(out=wl_all[:, :], in_=wl_all[:, :], func=AF.Exp,
                              bias=0.0, scale=1.0).then_inc(s_exp, 1)
            for i in range(NT):
                scalar.wait_ge(s_sq, 4 * i + 1)
                bb = (i % 2) * D
                scalar.activation(out=tmp64[:, :], in_=z2[:, bb:bb + D], func=AF.Square,
                                  bias=0.0, scale=1.0, accum_out=colA[:, 4:5]).then_inc(s_sq, 1)
                scalar.wait_ge(s_sq, 4 * i + 3)
                scalar.activation(out=colA[:, 5:6], in_=colA[:, 3:4], func=AF.Sqrt,
                                  bias=0.0, scale=1.0).then_inc(s_sq, 1)

    return nc



def build_combine(layer):
    nch = 2 if layer == 0 else 1
    nc = bass.Bass("TRN2", target_bir_lowering=False, debug=False)
    xres_i = nc.dram_tensor("xres_i", [T, D], F32, kind="ExternalInput").ap()
    attn_i = nc.dram_tensor("attn_i", [T, D], F32, kind="ExternalInput").ap()
    if layer == 0:
        shared_i = nc.dram_tensor("shared_i", [T, D], F32, kind="ExternalInput").ap()
    eout_i = nc.dram_tensor("eout_i", [E * CAP, D], BF16, kind="ExternalInput").ap()
    meta_i = nc.dram_tensor("meta_i", [128, NT * 8], F32, kind="ExternalInput").ap()
    cnt_i = nc.dram_tensor("cnt_i", [16, E], F32, kind="ExternalInput").ap()
    maskA_i = nc.dram_tensor("maskA_i", [16, 1], F32, kind="ExternalInput").ap()
    maskB_i = nc.dram_tensor("maskB_i", [16, 1], F32, kind="ExternalInput").ap()
    xo_o = nc.dram_tensor("xo_o", [T, D], F32, kind="ExternalOutput").ap()
    e8_c = nc.inline_tensor(np.tile(np.arange(8, dtype=np.float32), (128, 1)), name="c_e8c").ap()
    onesrow_c = nc.inline_tensor(np.ones((1, 128), np.float32), name="c_onesrow3").ap()

    es = ExitStack()
    with es:
        sb = lambda name, shape, dt: es.enter_context(nc.sbuf_tensor(name, shape, dt))
        ps = lambda name, shape, dt: es.enter_context(nc.psum_tensor(name, shape, dt))
        acc = sb("acc", [128, NT * D], F32)
        str2 = sb("str2", [128, 2 * D], F32)
        str3 = sb("str3", [128, 2 * D], F32)
        meta_s = sb("meta_s", [128, NT * 8], F32)
        cnt_s = sb("cnt_s", [16, E], F32)
        mA_s = sb("mA_s", [16, 1], F32)
        mB_s = sb("mB_s", [16, 1], F32)
        e8_s = sb("e8_s", [128, 8], F32)
        onesrow_s = sb("onesrow_s", [1, 128], F32)
        segb = sb("segb", [1, 16], F32)
        segbb = sb("segbb", [128, 16], F32)
        maskc = sb("maskc", [128, 8], F32)
        basec = sb("basec", [128, 1], F32)
        posg = sb("posg", [128, 1], F32)
        kept = sb("kept", [128, 1], F32)
        wfin = sb("wfin", [128, 1], F32)
        slotf = sb("slotf", [128, 1], F32)
        sloti = sb("sloti", [128, 2], I32)
        gbuf = sb("gbuf", [128, 2 * D], BF16)
        pS = ps("pS", [1, 16], F32)
        pB = ps("pB", [128, 16], F32)

        s_ld = es.enter_context(nc.semaphore("s_ld"))
        s_acc = es.enter_context(nc.semaphore("s_acc"))
        s_st2 = [es.enter_context(nc.semaphore(f"s_st2{p}")) for p in range(2)]
        s_st3 = [es.enter_context(nc.semaphore(f"s_st3{p}")) for p in range(2)]
        s_pe = es.enter_context(nc.semaphore("s_pe"))
        s_vseg = es.enter_context(nc.semaphore("s_vseg"))
        s_vidx = es.enter_context(nc.semaphore("s_vidx"))
        s_g = es.enter_context(nc.semaphore("s_g"))
        s_gc = es.enter_context(nc.semaphore("s_gc"))
        s_ost = [es.enter_context(nc.semaphore(f"s_ost{p}")) for p in range(2)]
        block = es.enter_context(nc.Block())

        xres_r = xres_i.rearrange("(t p) d -> p t d", p=128)
        attn_r = attn_i.rearrange("(t p) d -> p t d", p=128)
        xo_r = xo_o.rearrange("(t p) d -> p t d", p=128)

        @block.sync
        def _(sync):
            sync.dma_start(out=acc[:, :].rearrange("p (t d) -> p t d", d=D), in_=xres_r).then_inc(s_ld, 16)
            sync.dma_start(out=meta_s[:, :], in_=meta_i[:, :]).then_inc(s_ld, 16)
            sync.dma_start(out=cnt_s[:, :], in_=cnt_i[:, :]).then_inc(s_ld, 16)
            sync.dma_start(out=mA_s[:, :], in_=maskA_i[:, :]).then_inc(s_ld, 16)
            sync.dma_start(out=mB_s[:, :], in_=maskB_i[:, :]).then_inc(s_ld, 16)
            sync.dma_start(out=e8_s[:, :], in_=e8_c[:, :]).then_inc(s_ld, 16)
            sync.dma_start(out=onesrow_s[:, :], in_=onesrow_c[:, :]).then_inc(s_ld, 16)
            for i in range(NT):
                if i >= 2:
                    sync.wait_ge(s_acc, i - 1)
                sync.dma_start(out=str2[:, (i % 2) * D:(i % 2 + 1) * D],
                               in_=attn_r[:, i, :]).then_inc(s_st2[i % 2], 16)
                if layer == 0:
                    sync.dma_start(out=str3[:, (i % 2) * D:(i % 2 + 1) * D],
                                   in_=shared_i.rearrange("(t p) d -> p t d", p=128)[:, i, :]
                                   ).then_inc(s_st3[i % 2], 16)
            for i in range(NT):
                sync.wait_ge(s_gc, nch * (i + 1))
                sync.dma_start(out=xo_r[:, i, :],
                               in_=acc[:, i * D:(i + 1) * D]).then_inc(s_ost[i % 2], 16)
            for p in range(2):
                sync.wait_ge(s_ost[p], 16 * 8)

        @block.tensor
        def _(tensor):
            tensor.wait_ge(s_ld, 112)
            tensor.matmul(pS[:1, 0:8], mA_s[:, :], cnt_s[:, :], start=True, stop=True).then_inc(s_pe, 1)
            tensor.matmul(pS[:1, 8:16], mB_s[:, :], cnt_s[:, :], start=True, stop=True).then_inc(s_pe, 1)
            tensor.wait_ge(s_vseg, 1)
            tensor.matmul(pB[:, :], onesrow_s[:1, :], segb[:1, :], start=True, stop=True).then_inc(s_pe, 1)

        @block.gpsimd
        def _(gpsimd):
            for g in range(NT * nch):
                gpsimd.wait_ge(s_vidx, g + 1)
                if g >= 2:
                    gpsimd.wait_ge(s_gc, g - 1)
                gpsimd.indirect_dma_start(
                    out=gbuf[:, (g % 2) * D:(g % 2 + 1) * D], out_offset=None,
                    in_=eout_i[:, :],
                    in_offset=IndirectOffsetOnAxis(ap=sloti[:, g % 2:g % 2 + 1], axis=0),
                    bounds_check=E * CAP - 1, oob_is_err=False,
                ).then_inc(s_g, 16)

        @block.vector
        def _(vector):
            vector.wait_ge(s_ld, 112)
            vector.wait_ge(s_pe, 2)
            vector.tensor_copy(segb[:1, :], pS[:1, :])
            vector.drain().then_inc(s_vseg, 1)
            vector.wait_ge(s_pe, 3)
            vector.tensor_copy(segbb[:, :], pB[:, :])
            vector.drain()
            for i in range(NT):
                vector.wait_ge(s_st2[i % 2], 16 * (i // 2 + 1))
                vector.tensor_tensor(out=acc[:, i * D:(i + 1) * D], in0=acc[:, i * D:(i + 1) * D],
                                     in1=str2[:, (i % 2) * D:(i % 2 + 1) * D], op=ALU.add)
                vector.drain()
                if layer == 0:
                    vector.wait_ge(s_st3[i % 2], 16 * (i // 2 + 1))
                    vector.tensor_tensor(out=acc[:, i * D:(i + 1) * D], in0=acc[:, i * D:(i + 1) * D],
                                         in1=str3[:, (i % 2) * D:(i % 2 + 1) * D], op=ALU.add)
                    vector.drain()
                vector.nop().then_inc(s_acc, 1)
                seg = 0 if i < 8 else 1
                for ch in range(nch):
                    g = i * nch + ch
                    mb8 = i * 8 + (0 if ch == 0 else 3)
                    ecol = meta_s[:, mb8:mb8 + 1]
                    pcol = meta_s[:, mb8 + 1:mb8 + 2]
                    wcol = meta_s[:, mb8 + 2:mb8 + 3]
                    vector.drain()
                    vector.tensor_scalar(out=maskc[:, :], in0=e8_s[:, :], scalar1=ecol,
                                         scalar2=None, op0=ALU.is_equal)
                    vector.drain()
                    vector.tensor_tensor(out=maskc[:, :], in0=maskc[:, :],
                                         in1=segbb[:, seg * 8:(seg + 1) * 8], op=ALU.mult)
                    vector.drain()
                    vector.reduce_sum(basec[:, :], maskc[:, :], axis=AX.X)
                    vector.drain()
                    vector.tensor_tensor(out=posg[:, :], in0=pcol, in1=basec[:, :], op=ALU.add)
                    vector.drain()
                    vector.tensor_scalar(out=kept[:, :], in0=posg[:, :], scalar1=float(CAP),
                                         scalar2=None, op0=ALU.is_lt)
                    vector.drain()
                    vector.tensor_tensor(out=wfin[:, :], in0=wcol, in1=kept[:, :], op=ALU.mult)
                    vector.scalar_tensor_tensor(out=slotf[:, :], in0=ecol, scalar=float(CAP),
                                                in1=posg[:, :], op0=ALU.mult, op1=ALU.add)
                    vector.drain()
                    vector.tensor_scalar(out=kept[:, :], in0=kept[:, :], scalar1=-BIG,
                                         scalar2=BIG, op0=ALU.mult, op1=ALU.add)
                    vector.drain()
                    vector.tensor_tensor(out=slotf[:, :], in0=slotf[:, :], in1=kept[:, :], op=ALU.add)
                    vector.drain()
                    vector.tensor_copy(sloti[:, g % 2:g % 2 + 1], slotf[:, :])
                    vector.drain().then_inc(s_vidx, 1)
                    vector.wait_ge(s_g, 16 * (g + 1))
                    vector.scalar_tensor_tensor(out=acc[:, i * D:(i + 1) * D],
                                                in0=gbuf[:, (g % 2) * D:(g % 2 + 1) * D],
                                                scalar=wfin[:, :],
                                                in1=acc[:, i * D:(i + 1) * D],
                                                op0=ALU.mult, op1=ALU.add)
                    vector.drain().then_inc(s_gc, 1)

    return nc



def build_launch_B():
    nc = bass.Bass("TRN2", target_bir_lowering=False, debug=False)
    xnm_i = nc.dram_tensor("xnm_i", [T, D], BF16, kind="ExternalInput").ap()
    xnp_i = nc.dram_tensor("xnp_i", [512, D], BF16, kind="ExternalInput").ap()
    wkvg_i = nc.dram_tensor("wkvg_i", [D, 1056], F32, kind="ExternalInput").ap()
    wq_i = nc.dram_tensor("wq_i", [D, 512], F32, kind="ExternalInput").ap()
    wo_i = nc.dram_tensor("wo_i", [512, D], F32, kind="ExternalInput").ap()
    bg_i = nc.dram_tensor("bg_i", [128, 32], F32, kind="ExternalInput").ap()
    mk_i = nc.dram_tensor("mk_i", [64, 1], F32, kind="ExternalInput").ap()
    ini_i = nc.dram_tensor("ini_i", [64, 288], F32, kind="ExternalInput").ap()
    attn_o = nc.dram_tensor("attn_o", [T, D], F32, kind="ExternalOutput").ap()
    identc = nc.inline_tensor(np.eye(128, dtype=BF), name="c_idB").ap()
    identfc = nc.inline_tensor(np.eye(128, dtype=np.float32), name="c_idBf").ap()
    onesrc = nc.inline_tensor(np.ones((1, 128), BF), name="c_onesB").ap()
    bandm_np = np.zeros((128, 1024), np.float32)
    for e_ in range(4):
        for st2_ in range(2):
            r_ = e_ * 2 + st2_
            for srow_ in range(128 if st2_ == 0 else 31):
                sglob_ = st2_ * 128 + srow_
                for p_ in range(128):
                    if p_ <= sglob_ <= p_ + 31:
                        bandm_np[srow_, r_ * 128 + p_] = 1.0
    bandc = nc.inline_tensor(bandm_np.astype(BF), name="c_bandB").ap()

    es = ExitStack()
    with es:
        sb = lambda name, shape, dt: es.enter_context(nc.sbuf_tensor(name, shape, dt))
        ps = lambda name, shape, dt: es.enter_context(nc.psum_tensor(name, shape, dt))
        xmT = sb("xmT", [128, 8 * T], BF16)
        xpT = sb("xpT", [128, 8 * 512], BF16)
        wkvg_s = sb("wkvg_s", [128, 8 * 1056], BF16)
        wq_s = sb("wq_s", [128, 8 * 512], BF16)
        wo_s = sb("wo_s", [128, 4 * D], BF16)
        bg_s = sb("bg_s", [128, 32], F32)
        mk_s = sb("mk_s", [64, 1], F32)
        ini_s = sb("ini_s", [64, 288], F32)
        ident_s = sb("ident_s", [128, 128], BF16)
        onesr_s = sb("onesr_s", [1, 128], BF16)
        kv_f = sb("kv_f", [128, 2 * 1024], F32)
        gt_f = sb("gt_f", [128, 2 * 32], F32)
        g_s = sb("g_s", [128, 2 * 32], F32)
        a_s = sb("a_s", [128, 2 * 4], F32)
        abf = sb("abf", [128, 2 * 4], BF16)
        gk_f = sb("gk_f", [128, 2 * 256], F32)
        gv_f = sb("gv_f", [128, 2 * 256], F32)
        gkbf = sb("gkbf", [128, 512], BF16)
        gvbf = sb("gvbf", [128, 512], BF16)
        gkT = sb("gkT", [64, 1024], BF16)
        gvT = sb("gvT", [64, 1024], BF16)
        arow = sb("arow", [1, 1024], BF16)
        ab_s = sb("ab_s", [64, 1024], F32)
        sk_s = sb("sk_s", [64, 1024], F32)
        sv_s = sb("sv_s", [64, 1024], F32)
        w31k = sb("w31k", [64, 124], F32)
        w31v = sb("w31v", [64, 124], F32)
        fkA = sb("fkA", [65, 4 * 287], BF16)
        svb = sb("svb", [64, 4 * 287], BF16)
        fv1 = sb("fv1", [128, 4 * 65], BF16)
        fv2 = sb("fv2", [128, 4 * 65], BF16)
        fv3 = sb("fv3", [32, 4 * 65], BF16)
        q_s = sb("q_s", [64, 8 * 256], BF16)
        qaug = sb("qaug", [65, 2 * 128], BF16)
        mx4 = sb("mx4", [128, 4], F32)
        mcol = sb("mcol", [128, 1], F32)
        nmb = sb("nmb", [128, 2], F32)
        expT = sb("expT", [128, 2048], BF16)
        identf = sb("identf", [128, 128], F32)
        band_s = sb("band_s", [128, 1024], BF16)
        rz1 = sb("rz1", [128, 1], F32)
        y_s = sb("y_s", [128, 2 * 64], BF16)
        yT = sb("yT", [128, 2 * 512], BF16)
        ao_s = sb("ao_s", [128, 2 * D], F32)
        pA = ps("pA", [128, 2048], F32)
        pB = ps("pB", [128, 1536], F32)
        pBF = ps("pBF", [128, 128], BF16)

        semnames = ["ld", "gld", "proj", "kv", "ga", "gs", "am", "gk", "pbfT", "pbfe",
                    "abm", "abe", "scan", "fk",
                    "qm", "qe", "pa", "mx", "qa", "m1", "expd", "ngT",
                    "m2", "msk", "ye", "wo", "woe", "st0", "st1"]
        S = {n: es.enter_context(nc.semaphore("sB_" + n)) for n in semnames}
        block = es.enter_context(nc.Block())

        from collections import defaultdict
        C = defaultdict(int)
        ops = {k: [] for k in ["sync", "gpsimd", "vector", "scalar", "tensor"]}
        mx_hist, ngT_hist, m1_hist, yT_hist, m2_hist = {}, {}, {}, {}, {}
        carry = []

        def em(eng, fn):
            ops[eng].append(fn)

        def f_loads(sync):
            for k in range(8):
                sync.dma_start(out=xmT[:, k * T:(k + 1) * T],
                               in_=xnm_i[:, k * 128:(k + 1) * 128], transpose=True).then_inc(S["ld"], 16)
            for k in range(8):
                sync.dma_start(out=xpT[:, k * 512:(k + 1) * 512],
                               in_=xnp_i[:, k * 128:(k + 1) * 128], transpose=True).then_inc(S["ld"], 16)
            sync.dma_start(out=bg_s[:, :], in_=bg_i[:, :]).then_inc(S["ld"], 16)
            sync.dma_start(out=mk_s[:, :], in_=mk_i[:, :]).then_inc(S["ld"], 16)
            sync.dma_start(out=ini_s[:, :], in_=ini_i[:, :]).then_inc(S["ld"], 16)
            sync.dma_start(out=ident_s[:, :], in_=identc[:, :]).then_inc(S["ld"], 16)
            sync.dma_start(out=identf[:, :], in_=identfc[:, :]).then_inc(S["ld"], 16)
            sync.dma_start(out=onesr_s[:, :], in_=onesrc[:, :]).then_inc(S["ld"], 16)
            sync.dma_start(out=band_s[:, :], in_=bandc[:, :]).then_inc(S["ld"], 16)
        em("sync", f_loads)
        C["ld"] = 23 * 16

        def f_gld(gpsimd):
            gpsimd.dma_start(out=wkvg_s[:, :].rearrange("p (k c) -> p k c", c=1056),
                             in_=wkvg_i.rearrange("(k p) c -> p k c", p=128)).then_inc(S["gld"], 16)
            gpsimd.dma_start(out=wq_s[:, :].rearrange("p (k c) -> p k c", c=512),
                             in_=wq_i.rearrange("(k p) c -> p k c", p=128)).then_inc(S["gld"], 16)
            gpsimd.dma_start(out=wo_s[:, :].rearrange("p (k c) -> p k c", c=D),
                             in_=wo_i.rearrange("(k p) c -> p k c", p=128)).then_inc(S["gld"], 16)
        em("gpsimd", f_gld)
        C["gld"] = 48
        LD, GLD = C["ld"], C["gld"]

        def f_init(vector):
            vector.wait_ge(S["ld"], LD)
            vector.memset(pB[:, :], 0.0)
            vector.memset(pA[:, :], 0.0)
            vector.memset(fkA[64:65, :], 1.0)
            for fvx in (fv1, fv2, fv3):
                for e in range(4):
                    vector.memset(fvx[:, e * 65 + 64:e * 65 + 65], 1.0)
            vector.drain().then_inc(S["fk"], 1)
        em("vector", f_init)
        C["fk"] += 1

        chunks = []
        for b in range(2):
            chunks.append((b, "prev", -1))
            for j in range(4):
                chunks.append((b, "main", j))

        attn_r = attn_o.rearrange("(t p) d -> p t d", p=128)
        ST3 = [(0, 128), (128, 128), (256, 31)]

        for (b, kind, j) in chunks:
            if kind == "prev":
                xsrc, xbase, xstride = xpT, b * 256, 512
            else:
                xsrc, xbase, xstride = xmT, b * 1024 + j * 256, T

            for tt in range(2):
                w = dict(kv=C["kv"], ga=C["ga"], qe=C["qe"], woe=C["woe"], pa=C["pa"])
                def f(tensor, tt=tt, xsrc=xsrc, xbase=xbase, xstride=xstride, w=w):
                    tensor.wait_ge(S["ld"], LD)
                    tensor.wait_ge(S["gld"], GLD)
                    tensor.wait_ge(S["fk"], 1)
                    tensor.wait_ge(S["kv"], w["kv"])
                    tensor.wait_ge(S["ga"], w["ga"])
                    tensor.wait_ge(S["qe"], w["qe"])
                    tensor.wait_ge(S["woe"], w["woe"])
                    tensor.wait_ge(S["pa"], w["pa"])
                    for (c0, nn) in [(0, 512), (512, 512), (1024, 32)]:
                        for k in range(8):
                            mm = tensor.matmul(
                                pA[:, c0:c0 + nn],
                                xsrc[:, k * xstride + xbase + tt * 128:
                                     k * xstride + xbase + tt * 128 + 128],
                                wkvg_s[:, k * 1056 + c0:k * 1056 + c0 + nn],
                                start=(k == 0), stop=(k == 7))
                    mm.then_inc(S["proj"], 1)
                em("tensor", f)
                C["proj"] += 1
                pv = C["proj"]
                wgk = C["gk"]
                def f2(scalar, tt=tt, pv=pv, wgk=wgk):
                    scalar.wait_ge(S["proj"], pv)
                    scalar.wait_ge(S["gk"], wgk)
                    scalar.copy(out=kv_f[:, tt * 1024:(tt + 1) * 1024],
                                in_=pA[:, 0:1024]).then_inc(S["kv"], 1)
                em("scalar", f2)
                C["kv"] += 1
                wgs = C["gs"]
                def f3(vector, tt=tt, pv=pv, wgs=wgs):
                    vector.wait_ge(S["proj"], pv)
                    vector.wait_ge(S["gs"], wgs)
                    vector.drain()
                    vector.tensor_tensor(out=gt_f[:, tt * 32:(tt + 1) * 32],
                                         in0=pA[:, 1024:1056], in1=bg_s[:, :], op=ALU.add)
                    vector.drain().then_inc(S["ga"], 1)
                em("vector", f3)
                C["ga"] += 1
                gav2 = C["ga"]
                wgk2 = C["gk"]
                def f4(scalar, tt=tt, gav2=gav2, wgk2=wgk2):
                    scalar.wait_ge(S["ga"], gav2)
                    scalar.wait_ge(S["gk"], wgk2)
                    scalar.activation(out=g_s[:, tt * 32:(tt + 1) * 32],
                                      in_=gt_f[:, tt * 32:(tt + 1) * 32],
                                      func=AF.Sigmoid, bias=0.0, scale=1.0).then_inc(S["gs"], 1)
                em("scalar", f4)
                C["gs"] += 1
                gsv2, kvv2 = C["gs"], C["kv"]
                wT = C["pbfT"]
                def f5(vector, tt=tt, gsv2=gsv2, kvv2=kvv2, wT=wT):
                    vector.wait_ge(S["gs"], gsv2)
                    vector.wait_ge(S["kv"], kvv2)
                    vector.wait_ge(S["pbfT"], wT)
                    vector.drain()
                    g3 = g_s[:, tt * 32:(tt + 1) * 32].rearrange("p (h e) -> p e h", e=4)
                    vector.reduce_sum(a_s[:, tt * 4:(tt + 1) * 4], g3, axis=AX.X)
                    vector.drain()
                    vector.tensor_scalar(out=a_s[:, tt * 4:(tt + 1) * 4],
                                         in0=a_s[:, tt * 4:(tt + 1) * 4],
                                         scalar1=1.0, scalar2=None, op0=ALU.min)
                    vector.drain()
                    vector.tensor_scalar(out=abf[:, tt * 4:(tt + 1) * 4],
                                         in0=a_s[:, tt * 4:(tt + 1) * 4],
                                         scalar1=-1.0, scalar2=1.0, op0=ALU.mult, op1=ALU.add)
                    vector.drain().then_inc(S["am"], 1)
                    for (srcoff, accf, accb) in [(0, gk_f, gkbf), (512, gv_f, gvbf)]:
                        for e in range(4):
                            for h in range(8):
                                kslc = kv_f[:, tt * 1024 + srcoff + h * 64:
                                            tt * 1024 + srcoff + (h + 1) * 64]
                                gcol = g_s[:, tt * 32 + h * 4 + e:tt * 32 + h * 4 + e + 1]
                                dstf = accf[:, tt * 256 + e * 64:tt * 256 + (e + 1) * 64]
                                dstb = accb[:, tt * 256 + e * 64:tt * 256 + (e + 1) * 64]
                                if h == 0:
                                    vector.tensor_scalar(out=dstf, in0=kslc, scalar1=gcol,
                                                         scalar2=None, op0=ALU.mult)
                                elif h < 7:
                                    vector.scalar_tensor_tensor(out=dstf, in0=kslc, scalar=gcol,
                                                                in1=dstf, op0=ALU.mult, op1=ALU.add)
                                else:
                                    vector.scalar_tensor_tensor(out=dstb, in0=kslc, scalar=gcol,
                                                                in1=dstf, op0=ALU.mult, op1=ALU.add)
                                vector.drain()
                    vector.nop().then_inc(S["gk"], 1)
                em("vector", f5)
                C["am"] += 1
                C["gk"] += 1

            gknow, amnow = C["gk"], C["am"]
            for kvsel in range(2):
                for e in range(4):
                    for tt in range(2):
                        src = gkbf if kvsel == 0 else gvbf
                        dstT = gkT if kvsel == 0 else gvT
                        ev = C["pbfe"]
                        wscan = C["scan"]
                        def f(tensor, src=src, e=e, tt=tt, gknow=gknow, ev=ev, wscan=wscan):
                            tensor.wait_ge(S["gk"], gknow)
                            tensor.wait_ge(S["pbfe"], ev)
                            tensor.transpose(pBF[0:64, :],
                                             src[:, tt * 256 + e * 64:tt * 256 + (e + 1) * 64],
                                             ident_s[:, :]).then_inc(S["pbfT"], 1)
                        em("tensor", f)
                        C["pbfT"] += 1
                        tv = C["pbfT"]
                        def f2(scalar, dstT=dstT, e=e, tt=tt, tv=tv, wscan=wscan):
                            scalar.wait_ge(S["pbfT"], tv)
                            scalar.wait_ge(S["scan"], wscan)
                            scalar.copy(out=dstT[:, e * 256 + tt * 128:e * 256 + (tt + 1) * 128],
                                        in_=pBF[0:64, :]).then_inc(S["pbfe"], 1)
                        em("scalar", f2)
                        C["pbfe"] += 1
            for e in range(4):
                for tt in range(2):
                    ev = C["pbfe"]
                    wab = C["abm"]
                    def f(tensor, e=e, tt=tt, amnow=amnow, ev=ev):
                        tensor.wait_ge(S["am"], amnow)
                        tensor.wait_ge(S["pbfe"], ev)
                        tensor.transpose(pBF[0:1, :],
                                         abf[:, tt * 4 + e:tt * 4 + e + 1],
                                         ident_s[:, :]).then_inc(S["pbfT"], 1)
                    em("tensor", f)
                    C["pbfT"] += 1
                    tv = C["pbfT"]
                    def f2(scalar, e=e, tt=tt, tv=tv, wab=wab):
                        scalar.wait_ge(S["pbfT"], tv)
                        scalar.wait_ge(S["abm"], wab)
                        scalar.copy(out=arow[0:1, e * 256 + tt * 128:e * 256 + (tt + 1) * 128],
                                    in_=pBF[0:1, :]).then_inc(S["pbfe"], 1)
                    em("scalar", f2)
                    C["pbfe"] += 1
            acenow = C["pbfe"]
            for e in range(4):
                wabe = C["abe"]
                wwoe = C["woe"]
                def f(tensor, e=e, acenow=acenow, wabe=wabe, wwoe=wwoe):
                    tensor.wait_ge(S["pbfe"], acenow)
                    tensor.wait_ge(S["abe"], wabe)
                    tensor.wait_ge(S["woe"], wwoe)
                    tensor.matmul(pA[0:64, 1024 + e * 256:1024 + (e + 1) * 256],
                                  onesr_s[0:1, 0:64],
                                  arow[0:1, e * 256:(e + 1) * 256],
                                  start=True, stop=True).then_inc(S["abm"], 1)
                em("tensor", f)
                C["abm"] += 1
                tv = C["abm"]
                wscan = C["scan"]
                def f2(vector, e=e, tv=tv, wscan=wscan):
                    vector.wait_ge(S["abm"], tv)
                    vector.wait_ge(S["scan"], wscan)
                    vector.tensor_copy(ab_s[:, e * 256:(e + 1) * 256],
                                       pA[0:64, 1024 + e * 256:1024 + (e + 1) * 256])
                    vector.drain().then_inc(S["abe"], 1)
                em("vector", f2)
                C["abe"] += 1

            pbfe_now, abenow = C["pbfe"], C["abe"]
            if kind == "prev":
                colK = lambda e: 32 + e
                colV = lambda e: 36 + e
            else:
                colK = lambda e, j=j: j * 4 + e
                colV = lambda e, j=j: 16 + j * 4 + e
            fknow = C["fk"]
            def f_scan(vector, colK=colK, colV=colV, pbfe_now=pbfe_now, abenow=abenow,
                       fknow=fknow):
                vector.wait_ge(S["pbfe"], pbfe_now)
                vector.wait_ge(S["abe"], abenow)
                vector.wait_ge(S["fk"], fknow)
                vector.drain()
                for e in range(4):
                    vector.tensor_tensor_scan(
                        out=sk_s[:, e * 256:(e + 1) * 256],
                        data0=ab_s[0:64, e * 256:(e + 1) * 256],
                        data1=gkT[:, e * 256:(e + 1) * 256],
                        initial=ini_s[:, colK(e):colK(e) + 1],
                        op0=ALU.mult, op1=ALU.add)
                    vector.tensor_tensor_scan(
                        out=sv_s[:, e * 256:(e + 1) * 256],
                        data0=ab_s[0:64, e * 256:(e + 1) * 256],
                        data1=gvT[:, e * 256:(e + 1) * 256],
                        initial=ini_s[:, colV(e):colV(e) + 1],
                        op0=ALU.mult, op1=ALU.add)
                vector.drain().then_inc(S["scan"], 1)
            em("vector", f_scan)
            C["scan"] += 1
            scannow = C["scan"]

            if kind == "prev":
                def f_blend(vector, scannow=scannow):
                    vector.wait_ge(S["scan"], scannow)
                    vector.drain()
                    for e in range(4):
                        vector.tensor_scalar(out=w31k[:, e * 31:(e + 1) * 31],
                                             in0=sk_s[:, e * 256 + 225:e * 256 + 256],
                                             scalar1=mk_s[:, 0:1], scalar2=None, op0=ALU.mult)
                        vector.tensor_scalar(out=w31v[:, e * 31:(e + 1) * 31],
                                             in0=sv_s[:, e * 256 + 225:e * 256 + 256],
                                             scalar1=mk_s[:, 0:1], scalar2=None, op0=ALU.mult)
                    vector.drain()
                    for e in range(4):
                        vector.tensor_tensor(out=w31k[:, e * 31:(e + 1) * 31],
                                             in0=w31k[:, e * 31:(e + 1) * 31],
                                             in1=ini_s[:, 40 + e * 31:40 + (e + 1) * 31], op=ALU.add)
                        vector.tensor_tensor(out=w31v[:, e * 31:(e + 1) * 31],
                                             in0=w31v[:, e * 31:(e + 1) * 31],
                                             in1=ini_s[:, 164 + e * 31:164 + (e + 1) * 31], op=ALU.add)
                    vector.drain().then_inc(S["fk"], 1)
                em("vector", f_blend)
                C["fk"] += 1
                continue

            wm1, wpa2 = C["m1"], C["pa"]
            pbfe2 = C["pbfe"]
            def f_asm(vector, scannow=scannow, wm1=wm1, wpa2=wpa2, pbfe2=pbfe2):
                vector.wait_ge(S["scan"], scannow)
                vector.wait_ge(S["m1"], wm1)
                vector.wait_ge(S["pa"], wpa2)
                vector.wait_ge(S["pbfe"], pbfe2)
                vector.drain()
                for e in range(4):
                    vector.tensor_copy(fkA[0:64, e * 287:e * 287 + 31], w31k[:, e * 31:(e + 1) * 31])
                    vector.tensor_copy(fkA[0:64, e * 287 + 31:(e + 1) * 287], sk_s[:, e * 256:(e + 1) * 256])
                    vector.tensor_copy(svb[:, e * 287:e * 287 + 31], w31v[:, e * 31:(e + 1) * 31])
                    vector.tensor_copy(svb[:, e * 287 + 31:(e + 1) * 287], sv_s[:, e * 256:(e + 1) * 256])
                vector.drain()
                for e in range(4):
                    vector.tensor_copy(w31k[:, e * 31:(e + 1) * 31], sk_s[:, e * 256 + 225:e * 256 + 256])
                    vector.tensor_copy(w31v[:, e * 31:(e + 1) * 31], sv_s[:, e * 256 + 225:e * 256 + 256])
                vector.drain().then_inc(S["fk"], 1)
            em("vector", f_asm)
            C["fk"] += 1
            fknow2 = C["fk"]

            for e in range(4):
                for st in range(3):
                    c0, sl = ST3[st]
                    ev = C["pbfe"]
                    wm2 = C["m2"]
                    def f(tensor, e=e, c0=c0, sl=sl, fknow2=fknow2, ev=ev):
                        tensor.wait_ge(S["fk"], fknow2)
                        tensor.wait_ge(S["pbfe"], ev)
                        tensor.transpose(pBF[0:sl, 0:64],
                                         svb[:, e * 287 + c0:e * 287 + c0 + sl],
                                         ident_s[0:64, 0:64]).then_inc(S["pbfT"], 1)
                    em("tensor", f)
                    C["pbfT"] += 1
                    tv = C["pbfT"]
                    def f2(scalar, e=e, st=st, sl=sl, tv=tv, wm2=wm2):
                        scalar.wait_ge(S["pbfT"], tv)
                        scalar.wait_ge(S["m2"], wm2)
                        dst = [fv1, fv2, fv3][st]
                        scalar.copy(out=dst[0:sl, e * 65:e * 65 + 64],
                                    in_=pBF[0:sl, 0:64]).then_inc(S["pbfe"], 1)
                    em("scalar", f2)
                    C["pbfe"] += 1

            wkv3, wga3, wwoe3, wpa3 = C["kv"], C["ga"], C["woe"], C["pa"]
            def f_q(tensor, xbase=xbase, wkv3=wkv3, wga3=wga3, wwoe3=wwoe3, wpa3=wpa3):
                tensor.wait_ge(S["kv"], wkv3)
                tensor.wait_ge(S["ga"], wga3)
                tensor.wait_ge(S["woe"], wwoe3)
                tensor.wait_ge(S["pa"], wpa3)
                for m in range(4):
                    for k in range(8):
                        mm = tensor.matmul(
                            pA[:, m * 256:(m + 1) * 256],
                            wq_s[:, k * 512 + m * 128:k * 512 + (m + 1) * 128],
                            xmT[:, k * T + xbase:k * T + xbase + 256],
                            start=(k == 0), stop=(k == 7))
                mm.then_inc(S["qm"], 1)
            em("tensor", f_q)
            C["qm"] += 1
            qmnow = C["qm"]
            wpa4 = C["pa"]
            def f_qe(vector, qmnow=qmnow, wpa4=wpa4):
                vector.wait_ge(S["qm"], qmnow)
                vector.wait_ge(S["pa"], wpa4)
                vector.drain()
                for m in range(4):
                    vector.tensor_scalar(out=q_s[:, (2 * m) * 256:(2 * m + 1) * 256],
                                         in0=pA[0:64, m * 256:(m + 1) * 256], scalar1=0.125,
                                         scalar2=None, op0=ALU.mult)
                    vector.tensor_scalar(out=q_s[:, (2 * m + 1) * 256:(2 * m + 2) * 256],
                                         in0=pA[64:128, m * 256:(m + 1) * 256], scalar1=0.125,
                                         scalar2=None, op0=ALU.mult)
                vector.drain().then_inc(S["qe"], 1)
            em("vector", f_qe)
            C["qe"] += 1
            qenow = C["qe"]

            it0 = 16 * (b * 4 + j)
            for qb in range(2):
                for h in range(8):
                    for ceng, cfn in carry:
                        em(ceng, cfn)
                    carry = []
                    it = it0 + qb * 8 + h
                    pp = it % 2
                    w_mx = mx_hist.get(it - 2, 0)
                    w_woe = C["woe"]
                    def f_pa(tensor, qb=qb, h=h, pp=pp, qenow=qenow,
                             fknow2=fknow2, w_mx=w_mx, w_woe=w_woe):
                        tensor.wait_ge(S["qe"], qenow)
                        tensor.wait_ge(S["fk"], fknow2)
                        tensor.wait_ge(S["mx"], w_mx)
                        for e in range(4):
                            tensor.matmul(pA[:, pp * 1024 + e * 256:pp * 1024 + e * 256 + 159],
                                          q_s[:, h * 256 + qb * 128:h * 256 + (qb + 1) * 128],
                                          fkA[0:64, e * 287 + qb * 128:e * 287 + qb * 128 + 159],
                                          start=True, stop=True).then_inc(S["pa"], 1)
                    em("tensor", f_pa)
                    C["pa"] += 4
                    panow = C["pa"]
                    w_ngT = ngT_hist.get(it - 2, 0)
                    def f_mx(vector, pp=pp, panow=panow, w_ngT=w_ngT):
                        vector.wait_ge(S["pa"], panow)
                        vector.wait_ge(S["ngT"], w_ngT)
                        vector.drain()
                        vector.reduce_max(mcol[:, 0:1],
                                          pA[:, pp * 1024:pp * 1024 + 927], axis=AX.X)
                        vector.drain()
                        vector.tensor_scalar(out=nmb[:, pp:pp + 1], in0=mcol[:, 0:1], scalar1=-1.0,
                                             scalar2=None, op0=ALU.mult)
                        vector.drain().then_inc(S["mx"], 1)
                    em("vector", f_mx)
                    C["mx"] += 1
                    mx_hist[it] = C["mx"]
                    mxnow = C["mx"]
                    w_qa2 = C["qa"]
                    def f_ngT(tensor, pp=pp, mxnow=mxnow, w_qa2=w_qa2):
                        tensor.wait_ge(S["mx"], mxnow)
                        tensor.wait_ge(S["qa"], w_qa2)
                        tensor.transpose(pB[0:1, 1408:1536], nmb[:, pp:pp + 1],
                                         identf[:, :]).then_inc(S["ngT"], 1)
                    em("tensor", f_ngT)
                    C["ngT"] += 1
                    ngT_hist[it] = C["ngT"]
                    tv = C["ngT"]
                    w_m1q = m1_hist.get(it - 2, 0)
                    def f_ngE(scalar, qb=qb, h=h, pp=pp, tv=tv, w_m1q=w_m1q):
                        scalar.wait_ge(S["ngT"], tv)
                        scalar.wait_ge(S["m1"], w_m1q)
                        scalar.copy(out=qaug[64:65, pp * 128:(pp + 1) * 128],
                                    in_=pB[0:1, 1408:1536])
                        scalar.copy(out=qaug[0:64, pp * 128:(pp + 1) * 128],
                                    in_=q_s[:, h * 256 + qb * 128:h * 256 + (qb + 1) * 128]
                                    ).then_inc(S["qa"], 1)
                    em("scalar", f_ngE)
                    C["qa"] += 1
                    qanow = C["qa"]
                    expdnow = C["expd"]
                    w_woe2 = C["woe"]
                    def f_m1(tensor, qb=qb, pp=pp, qanow=qanow, expdnow=expdnow, w_woe2=w_woe2):
                        tensor.wait_ge(S["qa"], qanow)
                        tensor.wait_ge(S["expd"], expdnow)
                        tensor.wait_ge(S["woe"], w_woe2)
                        for e in range(4):
                            for st2 in range(2):
                                sl = 128 if st2 == 0 else 31
                                r = e * 2 + st2
                                c0 = qb * 128 + st2 * 128
                                tensor.matmul(pB[0:sl, r * 128:r * 128 + 128],
                                              fkA[0:65, e * 287 + c0:e * 287 + c0 + sl],
                                              qaug[0:65, pp * 128:(pp + 1) * 128],
                                              start=True, stop=True).then_inc(S["m1"], 1)
                    em("tensor", f_m1)
                    C["m1"] += 8
                    m1_hist[it] = C["m1"]
                    m1now = C["m1"]
                    w_m2 = m2_hist.get(it - 2, 0)
                    def f_exp(scalar, pp=pp, m1now=m1now, w_m2=w_m2):
                        scalar.wait_ge(S["m1"], m1now)
                        scalar.wait_ge(S["m2"], w_m2)
                        scalar.activation(out=expT[:, pp * 1024:(pp + 1) * 1024],
                                          in_=pB[:, 0:1024], func=AF.Exp,
                                          bias=0.0, scale=1.0).then_inc(S["expd"], 1)
                    em("scalar", f_exp)
                    C["expd"] += 1
                    expdnow2 = C["expd"]
                    def f_msk(vector, pp=pp, expdnow2=expdnow2):
                        vector.wait_ge(S["expd"], expdnow2)
                        vector.drain()
                        vector.tensor_tensor(out=expT[:, pp * 1024:(pp + 1) * 1024],
                                             in0=expT[:, pp * 1024:(pp + 1) * 1024],
                                             in1=band_s[:, :], op=ALU.mult)
                        vector.drain().then_inc(S["msk"], 1)
                    em("vector", f_msk)
                    C["msk"] += 1
                    msknow = C["msk"]
                    w_ye = C["ye"]
                    def f_m2(tensor, qb=qb, pp=pp, msknow=msknow, w_ye=w_ye):
                        tensor.wait_ge(S["msk"], msknow)
                        tensor.wait_ge(S["ye"], w_ye)
                        first = True
                        for e in range(4):
                            for st2 in range(2):
                                sl = 128 if st2 == 0 else 31
                                r = e * 2 + st2
                                fvx = [fv1, fv2][st2] if qb == 0 else [fv2, fv3][st2]
                                tensor.matmul(pB[:, 1280:1345],
                                              expT[0:sl, pp * 1024 + r * 128:pp * 1024 + r * 128 + 128],
                                              fvx[0:sl, e * 65:(e + 1) * 65],
                                              start=first, stop=(r == 7)).then_inc(S["m2"], 1)
                                first = False
                    carry.append(("tensor", f_m2))
                    C["m2"] += 8
                    m2_hist[it] = C["m2"]
                    m2now = C["m2"]
                    w_yT = yT_hist.get(it - 2, 0)
                    def f_ye(vector, pp=pp, m2now=m2now, w_yT=w_yT):
                        vector.wait_ge(S["m2"], m2now)
                        vector.wait_ge(S["pbfT"], w_yT)
                        vector.drain()
                        vector.reciprocal(rz1[:, 0:1], pB[:, 1344:1345])
                        vector.drain()
                        vector.tensor_scalar(out=y_s[:, pp * 64:(pp + 1) * 64], in0=pB[:, 1280:1344],
                                             scalar1=rz1[:, 0:1], scalar2=None, op0=ALU.mult)
                        vector.drain().then_inc(S["ye"], 1)
                    em("vector", f_ye)
                    C["ye"] += 1
                    yenow = C["ye"]
                    ev2 = C["pbfe"]
                    def f_yT(tensor, pp=pp, yenow=yenow, ev2=ev2):
                        tensor.wait_ge(S["ye"], yenow)
                        tensor.wait_ge(S["pbfe"], ev2)
                        tensor.transpose(pBF[0:64, :], y_s[:, pp * 64:(pp + 1) * 64],
                                         ident_s[:, :]).then_inc(S["pbfT"], 1)
                    carry.append(("tensor", f_yT))
                    C["pbfT"] += 1
                    yT_hist[it] = C["pbfT"]
                    tv2 = C["pbfT"]
                    wwo = C["wo"]
                    def f_yTe(scalar, qb=qb, h=h, tv2=tv2, wwo=wwo):
                        scalar.wait_ge(S["pbfT"], tv2)
                        scalar.wait_ge(S["wo"], wwo)
                        kt = h // 2
                        rbase = (h % 2) * 64
                        scalar.copy(out=yT[rbase:rbase + 64,
                                           (qb % 2) * 512 + kt * 128:(qb % 2) * 512 + (kt + 1) * 128],
                                    in_=pBF[0:64, :]).then_inc(S["pbfe"], 1)
                    carry.append(("scalar", f_yTe))
                    C["pbfe"] += 1

                for ceng, cfn in carry:
                    em(ceng, cfn)
                carry = []
                yTe_all = C["pbfe"]
                woev = C["woe"]
                w_expd3 = C["expd"]
                def f_wo(tensor, qb=qb, yTe_all=yTe_all, woev=woev, w_expd3=w_expd3):
                    tensor.wait_ge(S["pbfe"], yTe_all)
                    tensor.wait_ge(S["woe"], woev)
                    tensor.wait_ge(S["expd"], w_expd3)
                    for nh in range(2):
                        for kt in range(4):
                            mm = tensor.matmul(
                                pB[:, nh * 512:(nh + 1) * 512],
                                yT[:, (qb % 2) * 512 + kt * 128:(qb % 2) * 512 + (kt + 1) * 128],
                                wo_s[:, kt * D + nh * 512:kt * D + nh * 512 + 512],
                                start=(kt == 0), stop=(kt == 3))
                    mm.then_inc(S["wo"], 1)
                em("tensor", f_wo)
                C["wo"] += 1
                wonow = C["wo"]
                tile_idx = b * 8 + j * 2 + qb
                stsem = "st0" if tile_idx % 2 == 0 else "st1"
                stv = C[stsem]
                def f_woe(vector, qb=qb, wonow=wonow, stsem=stsem, stv=stv):
                    vector.wait_ge(S["wo"], wonow)
                    vector.wait_ge(S[stsem], stv)
                    vector.drain()
                    vector.tensor_copy(ao_s[:, (qb % 2) * D:(qb % 2 + 1) * D], pB[:, 0:1024])
                    vector.drain().then_inc(S["woe"], 1)
                em("vector", f_woe)
                C["woe"] += 1
                woen = C["woe"]
                def f_st(sync, tile_idx=tile_idx, qb=qb, woen=woen, stsem=stsem):
                    sync.wait_ge(S["woe"], woen)
                    sync.dma_start(out=attn_r[:, tile_idx, :],
                                   in_=ao_s[:, (qb % 2) * D:(qb % 2 + 1) * D]).then_inc(S[stsem], 16)
                em("sync", f_st)
                C[stsem] += 16

        st0f, st1f = C["st0"], C["st1"]
        def f_fin(sync):
            sync.wait_ge(S["st0"], st0f)
            sync.wait_ge(S["st1"], st1f)
        em("sync", f_fin)

        @block.sync
        def _(sync):
            for f in ops["sync"]:
                f(sync)

        @block.gpsimd
        def _(gpsimd):
            for f in ops["gpsimd"]:
                f(gpsimd)

        @block.vector
        def _(vector):
            for f in ops["vector"]:
                f(vector)

        @block.scalar
        def _(scalar):
            for f in ops["scalar"]:
                f(scalar)

        @block.tensor
        def _(tensor):
            for f in ops["tensor"]:
                f(tensor)

    return nc


def stage_B_weights(w_G, b_G, w_K, w_V, w_Q, w_O):
    wk = np.transpose(np.asarray(w_K), (2, 0, 1)).reshape(D, 512)
    wv = np.transpose(np.asarray(w_V), (2, 0, 1)).reshape(D, 512)
    wq = np.transpose(np.asarray(w_Q), (2, 0, 1)).reshape(D, 512)
    wg = np.transpose(np.asarray(w_G), (2, 0, 1)).reshape(D, 32)
    wkvg = np.ascontiguousarray(np.concatenate([wk, wv, wg], 1))
    bg = np.tile(np.asarray(b_G).reshape(32), (128, 1)).astype(np.float32)
    return wkvg, np.ascontiguousarray(wq), np.ascontiguousarray(np.asarray(w_O)), bg


def stage_B_inits(init_K, init_V, c):
    iK = np.asarray(init_K)
    iV = np.asarray(init_V)
    cols = np.zeros((64, 288), np.float32)
    for l in range(4):
        for e in range(4):
            cols[:, l * 4 + e] = iK[e, c * 4 + l]
            cols[:, 16 + l * 4 + e] = iV[e, c * 4 + l]
    if c > 0:
        for e in range(4):
            cols[:, 32 + e] = iK[e, c * 4 - 1]
            cols[:, 36 + e] = iV[e, c * 4 - 1]
    if c == 0:
        for e in range(4):
            for l in range(31):
                cols[:, 40 + e * 31 + l] = iK[e, l + 1]
                cols[:, 164 + e * 31 + l] = iV[e, l + 1]
    return cols


def stage_B_xprev(xn_full_bf, c):
    out = np.zeros((512, D), dtype=xn_full_bf.dtype)
    if c > 0:
        s0 = c * SLOC
        out[0:256] = xn_full_bf[0, s0 - 256:s0]
        out[256:512] = xn_full_bf[1, s0 - 256:s0]
    return out


# version 04
# speedup vs baseline: 1.1952x; 1.1952x over previous
from contextlib import ExitStack
import numpy as np
import ml_dtypes
import concourse.bass as bass
import concourse.mybir as mybir
from concourse.bass_utils import run_bass_kernel_spmd
from concourse.bass_core_rust import IndirectOffsetOnAxis

F32 = mybir.dt.float32
BF16 = mybir.dt.bfloat16
I32 = mybir.dt.int32
U32 = mybir.dt.uint32
AF = mybir.ActivationFunctionType
ALU = mybir.AluOpType
AX = mybir.AxisListType
BF = ml_dtypes.bfloat16

NCORES = 8
B = 2
S = 8192
D = 1024
SLOC = S // NCORES
T = B * SLOC
NT = T // 128
E = 8
CAP = 2560
LN_EPS = 1e-5
RMS_EPS = float(np.finfo(np.float32).eps)
BIG = 1.0e9

_NC_CACHE = {}



def _consts(nc):
    lexT = np.triu(np.ones((128, 128), np.float32), 1)
    c = {
        "lexT": nc.inline_tensor(lexT, name="c_lexT").ap(),
        "onescol": nc.inline_tensor(np.ones((128, 1), np.float32), name="c_onescol").ap(),
        "onesrow": nc.inline_tensor(np.ones((1, 128), np.float32), name="c_onesrow").ap(),
        "e8": nc.inline_tensor(np.tile(np.arange(8, dtype=np.float32), (128, 1)), name="c_e8").ap(),
        "identbf": nc.inline_tensor(np.eye(128, dtype=BF), name="c_identbf").ap(),
        "identf32": nc.inline_tensor(np.eye(128, dtype=np.float32), name="c_identf32").ap(),
        "bigtile": nc.inline_tensor(np.full((128, 128), BIG, np.float32), name="c_bigtile").ap(),
    }
    return c


class Counter:

    def __init__(self):
        self.v = 0

    def inc(self, n=1):
        self.v += n
        return self.v



def build_launch_A(topk=2, rms=True):
    nc = bass.Bass("TRN2", target_bir_lowering=False, debug=False)
    xs = nc.dram_tensor("xs", [T, D], F32, kind="ExternalInput").ap()
    rwTf = nc.dram_tensor("rwT", [D, E], F32, kind="ExternalInput").ap()
    agrow = nc.dram_tensor("agrow", [128, NT], F32, kind="ExternalInput").ap()

    xn_o = nc.dram_tensor("xn_o", [T, D], BF16, kind="ExternalOutput").ap()
    d2_o = nc.dram_tensor("d2_o", [2 * E * SLOC, 1], F32, kind="ExternalOutput").ap()
    cnt_o = nc.dram_tensor("cnt_o", [1, 16], F32, kind="ExternalOutput").ap()
    meta_o = nc.dram_tensor("meta_o", [T, 8], F32, kind="ExternalOutput").ap()

    C = _consts(nc)
    es = ExitStack()
    with es:
        sb = lambda name, shape, dt: es.enter_context(nc.sbuf_tensor(name, shape, dt))
        ps = lambda name, shape, dt: es.enter_context(nc.psum_tensor(name, shape, dt))
        x_all = sb("x_all", [128, NT * D], F32)
        xbf_all = sb("xbf_all", [128, NT * D], BF16)
        XT = sb("XT", [128, 8 * T], BF16)
        rwTs = sb("rwTs", [128, 8 * E], F32)
        rwTb = sb("rwTb", [128, 8 * E], BF16)
        agrow_s = sb("agrow_s", [128, NT], F32)
        ident_s = sb("ident_s", [128, 128], BF16)
        lexT_s = sb("lexT_s", [128, 128], F32)
        onescol_s = sb("onescol_s", [128, 1], F32)
        onesrow_s = sb("onesrow_s", [1, 128], F32)
        e8_s = sb("e8_s", [128, 8], F32)
        sum16 = sb("sum16", [128, NT], F32)
        sumsq16 = sb("sumsq16", [128, NT], F32)
        mean16 = sb("mean16", [128, NT], F32)
        var16 = sb("var16", [128, NT], F32)
        rstd16 = sb("rstd16", [128, NT], F32)
        scale16 = sb("scale16", [128, NT], F32)
        mus16 = sb("mus16", [128, NT], F32)
        tmpA = sb("tmpA", [128, NT], F32)
        tmpB = sb("tmpB", [128, NT], F32)
        sd16 = sb("sd16", [128, NT], F32)
        srms16 = sb("srms16", [128, NT], F32)
        sq_scr = sb("sq_scr", [128, D], F32)
        expsc = sb("expsc", [128, E], F32)
        zcol = sb("zcol", [128, NT], F32)
        xn_sb = sb("xn_sb", [128, NT * D], BF16)
        rws_s = sb("rws_s", [1, E], F32)
        rwsb_s = sb("rwsb_s", [128, E], F32)
        lga = sb("lga", [128, E], F32)
        lgb = sb("lgb", [128, E], F32)
        lg = sb("lg", [128, E], F32)
        m8 = sb("m8", [128, E], F32)
        i8 = sb("i8", [128, E], U32)
        e1_all = sb("e1_all", [128, NT], F32)
        e2_all = sb("e2_all", [128, NT], F32)
        w1_all = sb("w1_all", [128, NT], F32)
        w2_all = sb("w2_all", [128, NT], F32)
        d12 = sb("d12", [128, NT], F32)
        mask1_all = sb("mask1_all", [128, NT * E], F32)
        mask2_all = sb("mask2_all", [128, NT * E], F32)
        maskC = sb("maskC", [128, E], F32)
        cume_all = sb("cume_all", [128, NT * E], F32)
        totA = sb("totA", [1, E * 9], F32)
        totB = sb("totB", [1, E * 9], F32)
        totrowA = sb("totrowA", [1, E * 8], F32)
        totrowB = sb("totrowB", [1, E * 8], F32)
        tofc = sb("tofc", [1, E], F32)
        tob_s = sb("tob_s", [128, E], F32)
        cnt_s = sb("cnt_s", [1, 16], F32)
        pos1 = sb("pos1", [128, 1], F32)
        pos2 = sb("pos2", [128, 1], F32)
        cumfull = sb("cumfull", [128, E], F32)
        selt = sb("selt", [128, E], F32)
        idxf = sb("idxf", [128, 1], F32)
        idx1_i = sb("idx1_i", [128, NT], I32)
        idx2_i = sb("idx2_i", [128, NT], I32)
        meta_s = sb("meta_s", [128, NT * 8], F32)
        pXT0 = ps("pXT0", [128, 128], BF16)
        pXT1 = ps("pXT1", [128, 128], BF16)
        pLG = ps("pLG", [128, E], F32)
        pC = ps("pC", [128, E], F32)
        pT = ps("pT", [1, E], F32)
        pB2 = ps("pB2", [128, E], F32)
        pR = ps("pR", [1, E], F32)
        pB = ps("pB", [128, E], F32)

        s_ld = es.enter_context(nc.semaphore("s_ld"))
        s_gld = es.enter_context(nc.semaphore("s_gld"))
        s_gp = es.enter_context(nc.semaphore("s_gp"))
        s_stx = es.enter_context(nc.semaphore("s_stx"))
        s_st = es.enter_context(nc.semaphore("s_st"))
        s_pe = es.enter_context(nc.semaphore("s_pe"))
        s_pelg = es.enter_context(nc.semaphore("s_pelg"))
        s_pec = es.enter_context(nc.semaphore("s_pec"))
        s_peb = es.enter_context(nc.semaphore("s_peb"))
        s_perb = es.enter_context(nc.semaphore("s_perb"))
        s_aev = es.enter_context(nc.semaphore("s_aev"))
        s_asq = es.enter_context(nc.semaphore("s_asq"))
        s_asd = es.enter_context(nc.semaphore("s_asd"))
        s_asig = es.enter_context(nc.semaphore("s_asig"))
        s_aev2 = es.enter_context(nc.semaphore("s_aev2"))
        s_vsum = es.enter_context(nc.semaphore("s_vsum"))
        s_vvar = es.enter_context(nc.semaphore("s_vvar"))
        s_vr2 = es.enter_context(nc.semaphore("s_vr2"))
        s_vscale = es.enter_context(nc.semaphore("s_vscale"))
        s_vxn = es.enter_context(nc.semaphore("s_vxn"))
        s_vd = es.enter_context(nc.semaphore("s_vd"))
        s_vmask = es.enter_context(nc.semaphore("s_vmask"))
        s_vtot = es.enter_context(nc.semaphore("s_vtot"))
        s_vscan = es.enter_context(nc.semaphore("s_vscan"))
        s_vtof = es.enter_context(nc.semaphore("s_vtof"))
        s_vidx = es.enter_context(nc.semaphore("s_vidx"))
        s_vmeta = es.enter_context(nc.semaphore("s_vmeta"))
        block = es.enter_context(nc.Block())

        xs_r = xs.rearrange("(t p) d -> p t d", p=128)
        xn_r = xn_o.rearrange("(t p) d -> p t d", p=128)
        meta_r = meta_o.rearrange("(t p) m -> p t m", p=128)
        rwT_r = rwTf.rearrange("(k p) e -> p k e", p=128)
        totA_r = lambda ap: ap.rearrange("p (e j) -> p e j", j=9)

        @block.sync
        def _(sync):
            sync.dma_start(out=x_all[:, :].rearrange("p (t d) -> p t d", d=D), in_=xs_r).then_inc(s_ld, 16)
            sync.dma_start(out=rwTs[:, :].rearrange("p (k e) -> p k e", e=E), in_=rwT_r).then_inc(s_ld, 16)
            sync.dma_start(out=agrow_s[:, :], in_=agrow[:, :]).then_inc(s_ld, 16)
            sync.dma_start(out=ident_s[:, :], in_=C["identbf"][:, :]).then_inc(s_ld, 16)
            sync.dma_start(out=lexT_s[:, :], in_=C["lexT"][:, :]).then_inc(s_ld, 16)
            sync.dma_start(out=onescol_s[:, :], in_=C["onescol"][:, :]).then_inc(s_ld, 16)
            sync.dma_start(out=onesrow_s[:, :], in_=C["onesrow"][:, :]).then_inc(s_ld, 16)
            sync.dma_start(out=e8_s[:, :], in_=C["e8"][:, :]).then_inc(s_ld, 16)
            for i in range(NT):
                sync.wait_ge(s_vxn, i + 1)
                sync.dma_start(out=xn_r[:, i, :], in_=xn_sb[:, i * D:(i + 1) * D]).then_inc(s_stx, 16)
            sync.wait_ge(s_vscan, 1)
            sync.dma_start(out=cnt_o[:, :], in_=cnt_s[:, :]).then_inc(s_st, 16)
            for i in range(NT):
                sync.wait_ge(s_vmeta, i + 1)
                sync.dma_start(out=meta_r[:, i, :], in_=meta_s[:, i * 8:(i + 1) * 8]).then_inc(s_st, 16)
            sync.wait_ge(s_stx, 16 * NT)
            sync.wait_ge(s_gp, 16 + (32 if topk == 2 else 16) * 16)
            sync.wait_ge(s_st, 16 * (NT + 1))

        @block.gpsimd
        def _(gpsimd):
            gpsimd.dma_start(out=xbf_all[:, :].rearrange("p (t d) -> p t d", d=D), in_=xs_r).then_inc(s_gld, 16)
            gpsimd.dma_start(out=rwTb[:, :].rearrange("p (k e) -> p k e", e=E), in_=rwT_r).then_inc(s_gld, 16)
            gpsimd.dma_start(
                out=d2_o.rearrange("(a b) one -> a (b one)", a=128),
                in_=C["bigtile"][:, :]).then_inc(s_gp, 16)
            gpsimd.wait_ge(s_gp, 16)
            gpsimd.wait_ge(s_ld, 128)
            for i in range(NT):
                gpsimd.wait_ge(s_vidx, i + 1)
                gpsimd.indirect_dma_start(
                    out=d2_o[:, :], out_offset=IndirectOffsetOnAxis(ap=idx1_i[:, i:i + 1], axis=0),
                    in_=agrow_s[:, i:i + 1], in_offset=None,
                ).then_inc(s_gp, 16)
                if topk == 2:
                    gpsimd.indirect_dma_start(
                        out=d2_o[:, :], out_offset=IndirectOffsetOnAxis(ap=idx2_i[:, i:i + 1], axis=0),
                        in_=agrow_s[:, i:i + 1], in_offset=None,
                    ).then_inc(s_gp, 16)

        @block.scalar
        def _(scalar):
            scalar.wait_ge(s_ld, 128)
            for i in range(NT):
                if i >= 1:
                    scalar.wait_ge(s_asq, i)
                scalar.activation(out=sq_scr[:, :], in_=x_all[:, i * D:(i + 1) * D],
                                  func=AF.Square, bias=0.0, scale=1.0,
                                  accum_out=sumsq16[:, i:i + 1]).then_inc(s_asq, 1)
            scalar.wait_ge(s_vvar, 1)
            scalar.activation(out=sd16[:, :], in_=tmpA[:, :], func=AF.Sqrt,
                              bias=0.0, scale=1.0).then_inc(s_asd, 1)
            if rms:
                scalar.wait_ge(s_vr2, 1)
                scalar.activation(out=srms16[:, :], in_=tmpB[:, :], func=AF.Sqrt,
                                  bias=0.0, scale=1.0).then_inc(s_asd, 1)
            for j in range(NT * 8):
                scalar.wait_ge(s_pe, j + 1)
                i, k = divmod(j, 8)
                scalar.copy(out=XT[:, k * T + i * 128:k * T + (i + 1) * 128],
                            in_=(pXT0 if j % 2 == 0 else pXT1)[:, :]).then_inc(s_aev, 1)
            for i in range(NT):
                scalar.wait_ge(s_vd, i + 1)
                if topk == 2:
                    scalar.activation(out=w1_all[:, i:i + 1], in_=d12[:, i:i + 1],
                                      func=AF.Sigmoid, bias=0.0, scale=1.0).then_inc(s_asig, 1)
                else:
                    scalar.activation(out=expsc[:, :], in_=lg[:, :], func=AF.Exp,
                                      bias=d12[:, i:i + 1], scale=1.0,
                                      accum_out=zcol[:, i:i + 1]).then_inc(s_asig, 1)
            for i in range(NT):
                scalar.wait_ge(s_peb, i + 1)
                scalar.copy(out=tob_s[:, :], in_=pB2[:, :]).then_inc(s_aev2, 1)

        @block.tensor
        def _(tensor):
            tensor.wait_ge(s_gld, 32)
            tensor.wait_ge(s_ld, 128)
            for j in range(NT * 8):
                i, k = divmod(j, 8)
                if j >= 2:
                    tensor.wait_ge(s_aev, j - 1)
                tensor.transpose((pXT0 if j % 2 == 0 else pXT1)[:, :],
                                 xbf_all[:, i * D + k * 128:i * D + (k + 1) * 128],
                                 ident_s[:, :]).then_inc(s_pe, 1)
            for k in range(8):
                mm = tensor.matmul(pR[:, :], onescol_s[:, :], rwTs[:, k * E:(k + 1) * E],
                                   start=(k == 0), stop=(k == 7))
            mm.then_inc(s_perb, 1)
            tensor.wait_ge(s_vscale, 2)
            tensor.matmul(pB[:, :], onesrow_s[:1, :], rws_s[:1, :], start=True, stop=True).then_inc(s_perb, 1)
            tensor.wait_ge(s_aev, NT * 8)
            for i in range(NT):
                if i >= 1:
                    tensor.wait_ge(s_vmask, i)
                for k in range(8):
                    mm = tensor.matmul(pLG[:, :], XT[:, k * T + i * 128:k * T + (i + 1) * 128],
                                       rwTb[:, k * E:(k + 1) * E],
                                       start=(k == 0), stop=(k == 7))
                mm.then_inc(s_pelg, 1)
                tensor.wait_ge(s_vmask, i + 1)
                if i >= 1:
                    tensor.wait_ge(s_vtot, i)
                tensor.matmul(pC[:, :], lexT_s[:, :], maskC[:, :], start=True, stop=True).then_inc(s_pec, 1)
                tensor.matmul(pT[:, :], onescol_s[:, :], maskC[:, :], start=True, stop=True).then_inc(s_pec, 1)
            tensor.wait_ge(s_vscan, 1)
            for i in range(NT):
                tensor.wait_ge(s_vtof, i + 1)
                if i >= 1:
                    tensor.wait_ge(s_aev2, i)
                tensor.matmul(pB2[:, :], onesrow_s[:1, :], tofc[:1, :], start=True, stop=True).then_inc(s_peb, 1)

        @block.vector
        def _(vector):
            vector.wait_ge(s_ld, 128)
            for i in range(NT):
                vector.reduce_sum(sum16[:, i:i + 1], x_all[:, i * D:(i + 1) * D], axis=AX.X)
            vector.nop().then_inc(s_vsum, 1)
            vector.wait_ge(s_asq, NT)
            vector.drain()
            vector.tensor_scalar(out=mean16[:, :], in0=sum16[:, :], scalar1=1.0 / D, scalar2=None, op0=ALU.mult)
            vector.tensor_scalar(out=tmpA[:, :], in0=sumsq16[:, :], scalar1=1.0 / D, scalar2=None, op0=ALU.mult)
            vector.tensor_tensor(out=tmpB[:, :], in0=mean16[:, :], in1=mean16[:, :], op=ALU.mult)
            vector.drain()
            vector.tensor_tensor(out=var16[:, :], in0=tmpA[:, :], in1=tmpB[:, :], op=ALU.subtract)
            vector.drain()
            vector.tensor_scalar(out=tmpA[:, :], in0=var16[:, :], scalar1=LN_EPS, scalar2=None, op0=ALU.add)
            vector.drain().then_inc(s_vvar, 1)
            if rms:
                vector.reciprocal(tmpB[:, :], tmpA[:, :])
                vector.drain()
                vector.tensor_tensor(out=tmpB[:, :], in0=var16[:, :], in1=tmpB[:, :], op=ALU.mult)
                vector.drain()
                vector.tensor_scalar(out=tmpB[:, :], in0=tmpB[:, :], scalar1=RMS_EPS, scalar2=None, op0=ALU.add)
                vector.drain().then_inc(s_vr2, 1)
            vector.wait_ge(s_asd, 1)
            vector.reciprocal(rstd16[:, :], sd16[:, :])
            vector.drain()
            if rms:
                vector.wait_ge(s_asd, 2)
                vector.reciprocal(scale16[:, :], srms16[:, :])
                vector.drain()
                vector.tensor_tensor(out=scale16[:, :], in0=scale16[:, :], in1=rstd16[:, :], op=ALU.mult)
            else:
                vector.tensor_copy(scale16[:, :], rstd16[:, :])
            vector.drain()
            vector.tensor_tensor(out=mus16[:, :], in0=mean16[:, :], in1=scale16[:, :],
                                 op=ALU.mult).then_inc(s_vscale, 1)
            vector.wait_ge(s_perb, 1)
            vector.tensor_copy(rws_s[:1, :], pR[:1, :])
            vector.drain().then_inc(s_vscale, 1)
            vector.wait_ge(s_perb, 2)
            vector.tensor_copy(rwsb_s[:, :], pB[:, :])
            for i in range(NT):
                vector.tensor_scalar(out=xn_sb[:, i * D:(i + 1) * D],
                                     in0=x_all[:, i * D:(i + 1) * D],
                                     scalar1=mean16[:, i:i + 1], scalar2=rstd16[:, i:i + 1],
                                     op0=ALU.subtract, op1=ALU.mult).then_inc(s_vxn, 1)
            for i in range(NT):
                vector.wait_ge(s_pelg, i + 1)
                if topk == 1 and i >= 1:
                    vector.wait_ge(s_asig, i)
                vector.tensor_scalar(out=lga[:, :], in0=pLG[:, :], scalar1=scale16[:, i:i + 1], scalar2=None, op0=ALU.mult)
                vector.tensor_scalar(out=lgb[:, :], in0=rwsb_s[:, :], scalar1=mus16[:, i:i + 1], scalar2=None, op0=ALU.mult)
                vector.drain()
                vector.tensor_tensor(out=lg[:, :], in0=lga[:, :], in1=lgb[:, :], op=ALU.subtract)
                vector.drain()
                vector.max(m8[:, :], lg[:, :])
                vector.drain()
                vector.max_index(i8[:, :], m8[:, :], lg[:, :])
                vector.drain()
                vector.tensor_copy(e1_all[:, i:i + 1], i8[:, 0:1])
                if topk == 2:
                    vector.tensor_copy(e2_all[:, i:i + 1], i8[:, 1:2])
                    vector.tensor_tensor(out=d12[:, i:i + 1], in0=m8[:, 0:1], in1=m8[:, 1:2], op=ALU.subtract)
                else:
                    vector.tensor_scalar(out=d12[:, i:i + 1], in0=m8[:, 0:1], scalar1=-1.0,
                                         scalar2=None, op0=ALU.mult)
                vector.drain().then_inc(s_vd, 1)
                vector.tensor_scalar(out=mask1_all[:, i * E:(i + 1) * E], in0=e8_s[:, :],
                                     scalar1=e1_all[:, i:i + 1], scalar2=None, op0=ALU.is_equal)
                if topk == 2:
                    vector.tensor_scalar(out=mask2_all[:, i * E:(i + 1) * E], in0=e8_s[:, :],
                                         scalar1=e2_all[:, i:i + 1], scalar2=None, op0=ALU.is_equal)
                    vector.drain()
                    vector.tensor_tensor(out=maskC[:, :], in0=mask1_all[:, i * E:(i + 1) * E],
                                         in1=mask2_all[:, i * E:(i + 1) * E], op=ALU.add)
                else:
                    vector.drain()
                    vector.tensor_copy(maskC[:, :], mask1_all[:, i * E:(i + 1) * E])
                vector.drain().then_inc(s_vmask, 1)
                vector.wait_ge(s_pec, 2 * (i + 1))
                vector.tensor_copy(cume_all[:, i * E:(i + 1) * E], pC[:, :])
                trow = (totrowA if i < 8 else totrowB)
                vector.tensor_copy(trow[0:1, (i % 8) * E:(i % 8 + 1) * E], pT[:1, :])
                vector.drain().then_inc(s_vtot, 1)
            vector.memzero(totA[:1, :])
            vector.memzero(totB[:1, :])
            for e in range(E):
                rA = totrowA[0:1, :].rearrange("p (j e2) -> p e2 j", e2=8)
                vector.tensor_tensor_scan(
                    out=totA[0:1, e * 9 + 1:e * 9 + 9], data0=onesrow_s[0:1, 0:8],
                    data1=rA[0:1, e:e + 1, :].rearrange("p a j -> p (a j)"),
                    initial=0.0, op0=ALU.mult, op1=ALU.add)
                rB = totrowB[0:1, :].rearrange("p (j e2) -> p e2 j", e2=8)
                vector.tensor_tensor_scan(
                    out=totB[0:1, e * 9 + 1:e * 9 + 9], data0=onesrow_s[0:1, 0:8],
                    data1=rB[0:1, e:e + 1, :].rearrange("p a j -> p (a j)"),
                    initial=0.0, op0=ALU.mult, op1=ALU.add)
            vector.drain()
            cAr = totA[0:1, :].rearrange("p (e2 j) -> p j e2", j=9)
            cBr = totB[0:1, :].rearrange("p (e2 j) -> p j e2", j=9)
            vector.tensor_copy(cnt_s[0:1, 0:8], cAr[0:1, 8:9, :].rearrange("p a e2 -> p (a e2)"))
            vector.tensor_copy(cnt_s[0:1, 8:16],
                               cBr[0:1, 8:9, :].rearrange("p a e2 -> p (a e2)")).then_inc(s_vscan, 1)
            for i in range(NT):
                seg = 0 if i < 8 else 1
                srcr = cAr if seg == 0 else cBr
                m = i % 8
                vector.tensor_copy(tofc[0:1, :],
                                   srcr[0:1, m:m + 1, :].rearrange("p a e2 -> p (a e2)")).then_inc(s_vtof, 1)
                vector.wait_ge(s_aev2, i + 1)
                vector.tensor_tensor(out=cumfull[:, :], in0=cume_all[:, i * E:(i + 1) * E],
                                     in1=tob_s[:, :], op=ALU.add)
                vector.drain()
                vector.tensor_tensor(out=selt[:, :], in0=cumfull[:, :],
                                     in1=mask1_all[:, i * E:(i + 1) * E], op=ALU.mult)
                vector.drain()
                vector.reduce_sum(pos1[:, :], selt[:, :], axis=AX.X)
                vector.drain()
                if topk == 2:
                    vector.tensor_tensor(out=selt[:, :], in0=cumfull[:, :],
                                         in1=mask2_all[:, i * E:(i + 1) * E], op=ALU.mult)
                    vector.drain()
                    vector.reduce_sum(pos2[:, :], selt[:, :], axis=AX.X)
                    vector.drain()
                vector.scalar_tensor_tensor(out=idxf[:, :], in0=e1_all[:, i:i + 1],
                                            scalar=float(SLOC), in1=pos1[:, :],
                                            op0=ALU.mult, op1=ALU.add)
                vector.drain()
                vector.tensor_scalar(out=idxf[:, :], in0=idxf[:, :], scalar1=float(seg * E * SLOC), scalar2=None, op0=ALU.add)
                vector.drain()
                vector.tensor_copy(idx1_i[:, i:i + 1], idxf[:, :])
                vector.drain()
                if topk == 2:
                    vector.scalar_tensor_tensor(out=idxf[:, :], in0=e2_all[:, i:i + 1],
                                                scalar=float(SLOC), in1=pos2[:, :],
                                                op0=ALU.mult, op1=ALU.add)
                    vector.drain()
                    vector.tensor_scalar(out=idxf[:, :], in0=idxf[:, :], scalar1=float(seg * E * SLOC), scalar2=None, op0=ALU.add)
                    vector.drain()
                    vector.tensor_copy(idx2_i[:, i:i + 1], idxf[:, :])
                vector.drain().then_inc(s_vidx, 1)
                vector.wait_ge(s_asig, i + 1)
                mb8 = i * 8
                vector.tensor_copy(meta_s[:, mb8 + 0:mb8 + 1], e1_all[:, i:i + 1])
                vector.tensor_copy(meta_s[:, mb8 + 1:mb8 + 2], pos1[:, :])
                if topk == 2:
                    vector.tensor_copy(meta_s[:, mb8 + 2:mb8 + 3], w1_all[:, i:i + 1])
                    vector.tensor_copy(meta_s[:, mb8 + 3:mb8 + 4], e2_all[:, i:i + 1])
                    vector.tensor_copy(meta_s[:, mb8 + 4:mb8 + 5], pos2[:, :])
                    vector.tensor_scalar(out=meta_s[:, mb8 + 5:mb8 + 6], in0=w1_all[:, i:i + 1],
                                         scalar1=-1.0, scalar2=1.0, op0=ALU.mult, op1=ALU.add)
                    vector.memzero(meta_s[:, mb8 + 6:mb8 + 8])
                else:
                    vector.reciprocal(meta_s[:, mb8 + 2:mb8 + 3], zcol[:, i:i + 1])
                    vector.memzero(meta_s[:, mb8 + 3:mb8 + 8])
                vector.drain().then_inc(s_vmeta, 1)

    return nc



def stage_x_shard(X, c):
    sl = X[:, c * SLOC:(c + 1) * SLOC, :]
    return np.ascontiguousarray(sl.reshape(T, D))


def stage_A_inmaps(X, router_w):
    rwT = np.ascontiguousarray(router_w.T)
    maps = []
    for c in range(NCORES):
        maps.append({
            "xs": stage_x_shard(X, c),
            "rwT": rwT,
            "agrow": np.ascontiguousarray(
                (c * T + np.arange(T, dtype=np.float32)).reshape(NT, 128).T),
        })
    return maps


_CACHE = {}


def _get_nc(name, builder, *args):
    key = (name,) + tuple(args)
    if key not in _CACHE:
        _CACHE[key] = builder(*args)
    return _CACHE[key]


def _run(nc, maps):
    return run_bass_kernel_spmd(nc, maps, core_ids=list(range(NCORES))).results


def _meta_to_combine(meta):
    return np.ascontiguousarray(meta.reshape(NT, 128, 8).transpose(1, 0, 2).reshape(128, NT * 8))


def kernel(**inputs):
    g = {k: np.asarray(v) for k, v in inputs.items()}
    X = np.ascontiguousarray(g["L0_attn_X"], dtype=np.float32)

    ncA = _get_nc("A2", build_launch_A, 2, True)
    resA = _run(ncA, stage_A_inmaps(X, g["L0_ffn_router_w"]))
    xn_sh = [np.asarray(resA[c]["xn_o"]) for c in range(NCORES)]
    xnall = np.ascontiguousarray(np.concatenate(xn_sh, 0))
    d2g = np.stack([np.asarray(resA[c]["d2_o"]).reshape(2, E, SLOC) for c in range(NCORES)])
    D2G = np.zeros((16, E, SLOC), np.float32)
    cntG = np.zeros((16, E), np.float32)
    for c in range(NCORES):
        cr = np.asarray(resA[c]["cnt_o"]).reshape(2, E)
        for b in range(2):
            D2G[b * 8 + c] = d2g[c, b]
            cntG[b * 8 + c] = cr[b]
    metas0 = [np.asarray(resA[c]["meta_o"]) for c in range(NCORES)]

    wkvg, wq, wo, bg = stage_B_weights(g["L0_attn_w_G"], g["L0_attn_b_G"],
                                       g["L0_attn_w_K"], g["L0_attn_w_V"],
                                       g["L0_attn_w_Q"], g["L0_attn_w_O"])
    mapsB = []
    for c in range(NCORES):
        xprev = np.zeros((512, D), dtype=xn_sh[0].dtype)
        if c > 0:
            xprev[0:256] = xn_sh[c - 1][768:1024]
            xprev[256:512] = xn_sh[c - 1][1792:2048]
        mapsB.append(dict(
            xnm_i=xn_sh[c], xnp_i=xprev, wkvg_i=wkvg, wq_i=wq, wo_i=wo, bg_i=bg,
            mk_i=np.full((64, 1), 0.0 if c == 0 else 1.0, np.float32),
            ini_i=stage_B_inits(g["L0_attn_init_K_rec"], g["L0_attn_init_V_rec"], c)))
    ncB = _get_nc("B", build_launch_B)
    resB = _run(ncB, mapsB)
    attn0_sh = [np.asarray(resB[c]["attn_o"]) for c in range(NCORES)]

    sg = np.ascontiguousarray(g["L0_ffn_sg"], np.float32)
    su = np.ascontiguousarray(g["L0_ffn_su"], np.float32)
    sd = np.ascontiguousarray(g["L0_ffn_sd"], np.float32)
    ncC = _get_nc("C", build_launch_C)
    resC = _run(ncC, [dict(xn_i=xn_sh[c], sg_i=sg, su_i=su, sd_i=sd) for c in range(NCORES)])
    shared_sh = [np.ascontiguousarray(np.asarray(resC[c]["sharedT_o"]).T) for c in range(NCORES)]

    ncD0 = _get_nc("D0", build_launch_D, "l0")
    mapsD = []
    for e in range(NCORES):
        mapsD.append(dict(
            xnall_i=xnall,
            d2my_i=np.ascontiguousarray(D2G[:, e, :].reshape(-1, 1)),
            cntmy_i=np.ascontiguousarray(cntG[:, e].reshape(1, 16)),
            wg_i=np.ascontiguousarray(g["L0_ffn_rg"][e], np.float32),
            wu_i=np.ascontiguousarray(g["L0_ffn_ru"][e], np.float32),
            wd_i=np.ascontiguousarray(g["L0_ffn_rd"][e], np.float32)))
    resD0 = _run(ncD0, mapsD)
    eout0 = np.ascontiguousarray(np.concatenate(
        [np.asarray(resD0[e]["eout_o"]) for e in range(NCORES)], 0))

    ncE0 = _get_nc("E0", build_combine, 0)
    mapsE = []
    for c in range(NCORES):
        maskA = (np.arange(16) < c).astype(np.float32).reshape(16, 1)
        maskB = (np.arange(16) < 8 + c).astype(np.float32).reshape(16, 1)
        mapsE.append(dict(
            xres_i=stage_x_shard(X, c), attn_i=attn0_sh[c], shared_i=shared_sh[c],
            eout_i=eout0, meta_i=_meta_to_combine(metas0[c]),
            cnt_i=np.ascontiguousarray(cntG), maskA_i=maskA, maskB_i=maskB))
    resE0 = _run(ncE0, mapsE)
    x1_sh = [np.asarray(resE0[c]["xo_o"]) for c in range(NCORES)]

    ncA1 = _get_nc("A1", build_launch_A, 1, False)
    mapsA1 = []
    rwT1 = np.ascontiguousarray(g["L1_ffn_wg"], np.float32)
    for c in range(NCORES):
        mapsA1.append({"xs": x1_sh[c], "rwT": rwT1,
                       "agrow": np.ascontiguousarray(
                           (c * T + np.arange(T, dtype=np.float32)).reshape(NT, 128).T)})
    resA1 = _run(ncA1, mapsA1)
    xn1_sh = [np.asarray(resA1[c]["xn_o"]) for c in range(NCORES)]
    xn1all = np.ascontiguousarray(np.concatenate(xn1_sh, 0))
    d2g1 = np.stack([np.asarray(resA1[c]["d2_o"]).reshape(2, E, SLOC) for c in range(NCORES)])
    D2G1 = np.zeros((16, E, SLOC), np.float32)
    cntG1 = np.zeros((16, E), np.float32)
    for c in range(NCORES):
        cr = np.asarray(resA1[c]["cnt_o"]).reshape(2, E)
        for b in range(2):
            D2G1[b * 8 + c] = d2g1[c, b]
            cntG1[b * 8 + c] = cr[b]
    metas1 = [np.asarray(resA1[c]["meta_o"]) for c in range(NCORES)]

    ncF = _get_nc("F", build_launch_F)
    mapsF = []
    fw = dict(
        w1T_i=np.ascontiguousarray(g["L1_attn_w1"].T, np.float32),
        b1bc_i=np.tile(np.asarray(g["L1_attn_b1"], np.float32), (128, 1)),
        wwT_i=np.ascontiguousarray(g["L1_attn_ww"].T, np.float32),
        bwbc_i=np.tile(np.asarray(g["L1_attn_bw"], np.float32), (128, 1)),
        w2T_i=np.ascontiguousarray(g["L1_attn_w2"].T, np.float32),
        b2bc_i=np.tile(np.asarray(g["L1_attn_b2"], np.float32), (128, 1)),
        gbc_i=np.tile(np.asarray(g["L1_attn_gamma"], np.float32), (128, 1)),
        bbc_i=np.tile(np.asarray(g["L1_attn_beta"], np.float32), (128, 1)))
    for c in range(NCORES):
        m = dict(fw)
        m["xn_i"] = xn1_sh[c]
        mapsF.append(m)
    resF = _run(ncF, mapsF)
    attn1_sh = [np.asarray(resF[c]["attn_o"]) for c in range(NCORES)]

    ncD1 = _get_nc("D1", build_launch_D, "l1")
    mapsG = []
    for e in range(NCORES):
        mapsG.append(dict(
            xnall_i=xn1all,
            d2my_i=np.ascontiguousarray(D2G1[:, e, :].reshape(-1, 1)),
            cntmy_i=np.ascontiguousarray(cntG1[:, e].reshape(1, 16)),
            wg_i=np.ascontiguousarray(g["L1_ffn_w1"][e], np.float32),
            wd_i=np.ascontiguousarray(g["L1_ffn_w2"][e], np.float32),
            b1_i=np.ascontiguousarray(np.asarray(g["L1_ffn_b1"][e], np.float32).reshape(32, 128).T),
            b2bc_i=np.tile(np.asarray(g["L1_ffn_b2"][e], np.float32), (128, 1))))
    resD1 = _run(ncD1, mapsG)
    eout1 = np.ascontiguousarray(np.concatenate(
        [np.asarray(resD1[e]["eout_o"]) for e in range(NCORES)], 0))

    ncE1 = _get_nc("E1", build_combine, 1)
    mapsE1 = []
    for c in range(NCORES):
        maskA = (np.arange(16) < c).astype(np.float32).reshape(16, 1)
        maskB = (np.arange(16) < 8 + c).astype(np.float32).reshape(16, 1)
        mapsE1.append(dict(
            xres_i=x1_sh[c], attn_i=attn1_sh[c],
            eout_i=eout1, meta_i=_meta_to_combine(metas1[c]),
            cnt_i=np.ascontiguousarray(cntG1), maskA_i=maskA, maskB_i=maskB))
    resE1 = _run(ncE1, mapsE1)

    out = np.zeros((B, S, D), np.float32)
    for c in range(NCORES):
        sh = np.asarray(resE1[c]["xo_o"]).reshape(2, SLOC, D)
        out[0, c * SLOC:(c + 1) * SLOC] = sh[0]
        out[1, c * SLOC:(c + 1) * SLOC] = sh[1]
    return out



NI = 8192
NMI = NI // 128


def build_launch_C():
    nc = bass.Bass("TRN2", target_bir_lowering=False, debug=False)
    xn_i = nc.dram_tensor("xn_i", [T, D], BF16, kind="ExternalInput").ap()
    sg_i = nc.dram_tensor("sg_i", [D, NI], F32, kind="ExternalInput").ap()
    su_i = nc.dram_tensor("su_i", [D, NI], F32, kind="ExternalInput").ap()
    sd_i = nc.dram_tensor("sd_i", [NI, D], F32, kind="ExternalInput").ap()
    sharedT_o = nc.dram_tensor("sharedT_o", [D, T], F32, kind="ExternalOutput").ap()
    gu_d = nc.dram_tensor("gu_d", [NMI, 128, T], BF16, kind="Internal")

    es = ExitStack()
    with es:
        sb = lambda name, shape, dt: es.enter_context(nc.sbuf_tensor(name, shape, dt))
        ps = lambda name, shape, dt: es.enter_context(nc.psum_tensor(name, shape, dt))
        xnT = sb("xnT", [128, 8 * T], BF16)
        sgb = sb("sgb", [128, 2 * 8 * 128], BF16)
        sub = sb("sub", [128, 2 * 8 * 128], BF16)
        gsil = sb("gsil", [128, 1024], F32)
        gu_sb = sb("gu_sb", [128, 2 * T], BF16)
        sdb = sb("sdb", [128, NMI * 256], BF16)
        gub = sb("gub", [128, 4 * T], BF16)
        outb = sb("outb", [128, 2 * T], F32)
        pGA = ps("pGA", [128, T], F32)
        pGB = ps("pGB", [128, T], F32)

        s_ld = es.enter_context(nc.semaphore("s_ld"))
        s_wgP = [es.enter_context(nc.semaphore(f"s_wg{p}")) for p in range(2)]
        s_mm1 = es.enter_context(nc.semaphore("s_mm1"))
        s_sil = es.enter_context(nc.semaphore("s_sil"))
        s_gu = es.enter_context(nc.semaphore("s_gu"))
        s_gstP = [es.enter_context(nc.semaphore(f"s_gst{p}")) for p in range(2)]
        s_sd = es.enter_context(nc.semaphore("s_sd"))
        s_gldP = [es.enter_context(nc.semaphore(f"s_gld{p}")) for p in range(2)]
        s_mm2 = es.enter_context(nc.semaphore("s_mm2"))
        s_oev = es.enter_context(nc.semaphore("s_oev"))
        s_ost = es.enter_context(nc.semaphore("s_ost"))
        block = es.enter_context(nc.Block())

        @block.sync
        def _(sync):
            for k in range(8):
                sync.dma_start(out=xnT[:, k * T:(k + 1) * T],
                               in_=xn_i[:, k * 128:(k + 1) * 128],
                               transpose=True).then_inc(s_ld, 16)
            for mi in range(NMI):
                sync.wait_ge(s_gu, 8 * mi + 8)
                sync.dma_start(out=gu_d.ap()[mi, :, :],
                               in_=gu_sb[:, (mi % 2) * T:(mi % 2 + 1) * T]).then_inc(s_gstP[mi % 2], 16)
            for mg in range(4):
                for k2 in range(NMI // 2):
                    j2 = mg * (NMI // 2) + k2
                    if j2 >= 2:
                        sync.wait_ge(s_mm2, 2 * (j2 - 1))
                    sync.wait_ge(s_gstP[0], 16 * (k2 + 1))
                    sync.wait_ge(s_gstP[1], 16 * (k2 + 1))
                    sync.dma_start(out=gub[:, (j2 % 2) * 2 * T:(j2 % 2 + 1) * 2 * T]
                                   .rearrange("p (a t) -> p a t", a=2),
                                   in_=gu_d.ap()[2 * k2:2 * k2 + 2, :, :]
                                   .rearrange("a p t -> p a t")).then_inc(s_gldP[j2 % 2], 16)
                for md in range(2):
                    sync.wait_ge(s_oev, mg * 2 + md + 1)
                    sync.dma_start(out=sharedT_o[(mg * 2 + md) * 128:(mg * 2 + md + 1) * 128, :],
                                   in_=outb[:, md * T:(md + 1) * T]).then_inc(s_ost, 16)
            sync.wait_ge(s_ost, 8 * 16)

        @block.gpsimd
        def _(gpsimd):
            for mi in range(NMI):
                if mi >= 2:
                    gpsimd.wait_ge(s_mm1, 4 * (mi - 1))
                gpsimd.dma_start(out=sgb[:, (mi % 2) * 1024:(mi % 2) * 1024 + 1024]
                                 .rearrange("p (k i) -> p k i", i=128),
                                 in_=sg_i[:, mi * 128:(mi + 1) * 128]
                                 .rearrange("(k p) i -> p k i", p=128)).then_inc(s_wgP[mi % 2], 16)
                gpsimd.dma_start(out=sub[:, (mi % 2) * 1024:(mi % 2) * 1024 + 1024]
                                 .rearrange("p (k i) -> p k i", i=128),
                                 in_=su_i[:, mi * 128:(mi + 1) * 128]
                                 .rearrange("(k p) i -> p k i", p=128)).then_inc(s_wgP[mi % 2], 16)
            for mg in range(4):
                if mg >= 1:
                    gpsimd.wait_ge(s_mm2, mg * NMI)
                gpsimd.dma_start(
                    out=sdb[:, :].rearrange("p (k c) -> p k c", c=256),
                    in_=sd_i[:, mg * 256:(mg + 1) * 256].rearrange("(k p) c -> p k c", p=128),
                ).then_inc(s_sd, 16)

        @block.vector
        def _(vector):
            for mi in range(NMI):
                if mi >= 2:
                    vector.wait_ge(s_gstP[mi % 2], 16 * ((mi - 2) // 2 + 1))
                for n in range(4):
                    cidx = mi * 4 + n
                    np_ = (cidx % 2) * 1024
                    vector.wait_ge(s_sil, cidx + 1)
                    vector.tensor_tensor(
                        out=gu_sb[:, (mi % 2) * T + n * 512:(mi % 2) * T + (n + 1) * 512],
                        in0=gsil[:, (cidx % 2) * 512:(cidx % 2) * 512 + 512],
                        in1=pGB[:, np_:np_ + 512], op=ALU.mult).then_inc(s_gu, 2)
                    vector.drain()
            for mg in range(4):
                if mg >= 1:
                    vector.wait_ge(s_ost, 32 * mg)
                vector.wait_ge(s_mm2, mg * NMI + NMI)
                for md in range(2):
                    vector.tensor_copy(outb[:, md * T:(md + 1) * T],
                                       (pGA if md == 0 else pGB)[:, :]).then_inc(s_oev, 1)
                vector.drain()

        @block.scalar
        def _(scalar):
            for mi in range(NMI):
                for n in range(4):
                    cidx = mi * 4 + n
                    np_ = (cidx % 2) * 1024
                    scalar.wait_ge(s_mm1, cidx + 1)
                    if cidx >= 2:
                        scalar.wait_ge(s_gu, 2 * (cidx - 1))
                    scalar.activation(out=gsil[:, (cidx % 2) * 512:(cidx % 2) * 512 + 512],
                                      in_=pGA[:, np_:np_ + 512],
                                      func=AF.Silu, bias=0.0, scale=1.0).then_inc(s_sil, 1)

        @block.tensor
        def _(tensor):
            tensor.wait_ge(s_ld, 128)
            for mi in range(NMI):
                tensor.wait_ge(s_wgP[mi % 2], 32 * (mi // 2 + 1))
                base = (mi % 2) * 1024
                for n in range(4):
                    cidx = mi * 4 + n
                    if cidx >= 2:
                        tensor.wait_ge(s_gu, 2 * (cidx - 1))
                    np_ = (cidx % 2) * 1024
                    for k in range(8):
                        tensor.matmul(pGA[:, np_:np_ + 512],
                                      sgb[:, base + k * 128:base + (k + 1) * 128],
                                      xnT[:, k * T + n * 512:k * T + (n + 1) * 512],
                                      start=(k == 0), stop=(k == 7))
                    for k in range(8):
                        mm = tensor.matmul(pGB[:, np_:np_ + 512],
                                           sub[:, base + k * 128:base + (k + 1) * 128],
                                           xnT[:, k * T + n * 512:k * T + (n + 1) * 512],
                                           start=(k == 0), stop=(k == 7))
                    mm.then_inc(s_mm1, 1)
            tensor.wait_ge(s_gu, 512)
            for mg in range(4):
                tensor.wait_ge(s_sd, 16 * (mg + 1))
                if mg >= 1:
                    tensor.wait_ge(s_oev, mg * 2)
                for k in range(NMI):
                    j2 = (mg * NMI + k) // 2
                    tensor.wait_ge(s_gldP[j2 % 2], 16 * (j2 // 2 + 1))
                    for md in range(2):
                        pdst = pGA if md == 0 else pGB
                        for n in range(4):
                            mm = tensor.matmul(
                                pdst[:, n * 512:(n + 1) * 512],
                                sdb[:, k * 256 + md * 128:k * 256 + (md + 1) * 128],
                                gub[:, (((mg * NMI + k) // 2) % 2) * 2 * T + (k % 2) * T + n * 512:
                                    (((mg * NMI + k) // 2) % 2) * 2 * T + (k % 2) * T + (n + 1) * 512],
                                start=(k == 0), stop=(k == NMI - 1))
                    mm.then_inc(s_mm2, 1)

    return nc



IEXP = 4096
NSLOT_T = CAP // 128
NMIE = IEXP // 128


def build_launch_D(mode):
    nc = bass.Bass("TRN2", target_bir_lowering=False, debug=False)
    xnall_i = nc.dram_tensor("xnall_i", [NCORES * T, D], BF16, kind="ExternalInput").ap()
    d2my_i = nc.dram_tensor("d2my_i", [16 * SLOC, 1], F32, kind="ExternalInput").ap()
    cntmy_i = nc.dram_tensor("cntmy_i", [1, 16], F32, kind="ExternalInput").ap()
    wg_i = nc.dram_tensor("wg_i", [D, IEXP], F32, kind="ExternalInput").ap()
    if mode == "l0":
        wu_i = nc.dram_tensor("wu_i", [D, IEXP], F32, kind="ExternalInput").ap()
    wd_i = nc.dram_tensor("wd_i", [IEXP, D], F32, kind="ExternalInput").ap()
    if mode == "l1":
        b1_i = nc.dram_tensor("b1_i", [128, 32], F32, kind="ExternalInput").ap()
        b2bc_i = nc.dram_tensor("b2bc_i", [128, D], F32, kind="ExternalInput").ap()
    eout_o = nc.dram_tensor("eout_o", [CAP, D], BF16, kind="ExternalOutput").ap()
    xg_d = nc.dram_tensor("xg_d", [CAP, D], BF16, kind="Internal")
    h_d = nc.dram_tensor("h_d", [NMIE, 128, CAP], BF16, kind="Internal")

    slotcol_np = np.arange(CAP, dtype=np.float32).reshape(NSLOT_T, 128).T.copy()
    slotcol_c = nc.inline_tensor(slotcol_np, name="c_slotcol").ap()
    onesrow_c = nc.inline_tensor(np.ones((1, 128), np.float32), name="c_onesrow2").ap()

    groups = [list(range(g, min(g + 3, NSLOT_T))) for g in range(0, NSLOT_T, 3)]
    n_chunks = CAP // 512

    es = ExitStack()
    with es:
        sb = lambda name, shape, dt: es.enter_context(nc.sbuf_tensor(name, shape, dt))
        ps = lambda name, shape, dt: es.enter_context(nc.psum_tensor(name, shape, dt))
        cnt_s = sb("cnt_s", [1, 16], F32)
        pref = sb("pref", [1, 17], F32)
        onesrow_s = sb("onesrow_s", [1, 128], F32)
        slotcol_s = sb("slotcol_s", [128, NSLOT_T], F32)
        prefb = sb("prefb", [128, 17], F32)
        maskk = sb("maskk", [128, 16], F32)
        kcnt = sb("kcnt", [128, 1], F32)
        prefsel = sb("prefsel", [128, 16], F32)
        psel = sb("psel", [128, 1], F32)
        jcol = sb("jcol", [128, 1], F32)
        srcf = sb("srcf", [128, 1], F32)
        srci = sb("srci", [128, NSLOT_T], I32)
        d2val = sb("d2val", [128, NSLOT_T], F32)
        rowi = sb("rowi", [128, NSLOT_T], I32)
        xg_all = sb("xg_all", [128, NSLOT_T * D], BF16)
        xgT = sb("xgT", [128, 8 * CAP], BF16)
        wgb = sb("wgb", [128, 2 * 8 * 128], BF16)
        wub = sb("wub", [128, 2 * 8 * 128], BF16) if mode == "l0" else None
        gsil = sb("gsil", [128, CAP], F32) if mode == "l0" else None
        h_sb = sb("h_sb", [128, 2 * CAP], BF16)
        wdb = sb("wdb", [128, NMIE * D], BF16)
        hkb = sb("hkb", [128, 4 * CAP], BF16)
        outb = sb("outb", [128, 3 * D], BF16)
        if mode == "l1":
            b1_s = sb("b1_s", [128, 32], F32)
            b2bc_s = sb("b2bc_s", [128, D], F32)
        pP = ps("pP", [128, 512], F32)
        pU = ps("pU", [128, 512], F32)
        pO = [ps(f"pO{m}", [128, D], F32) for m in range(3)]

        s_ld = es.enter_context(nc.semaphore("s_ld"))
        s_vp = es.enter_context(nc.semaphore("s_vp"))
        s_pb = es.enter_context(nc.semaphore("s_pb"))
        s_vr = es.enter_context(nc.semaphore("s_vr"))
        s_g1 = es.enter_context(nc.semaphore("s_g1"))
        s_g2 = es.enter_context(nc.semaphore("s_g2"))
        s_vm = es.enter_context(nc.semaphore("s_vm"))
        s_xst = es.enter_context(nc.semaphore("s_xst"))
        s_xT = es.enter_context(nc.semaphore("s_xT"))
        s_wgP = [es.enter_context(nc.semaphore(f"s_wg{p}")) for p in range(2)]
        s_wd = es.enter_context(nc.semaphore("s_wd"))
        s_mm1 = es.enter_context(nc.semaphore("s_mm1"))
        s_mmc = es.enter_context(nc.semaphore("s_mmc"))
        s_sil = es.enter_context(nc.semaphore("s_sil"))
        s_h = es.enter_context(nc.semaphore("s_h"))
        s_hstP = [es.enter_context(nc.semaphore(f"s_hst{p}")) for p in range(2)]
        s_hldP = [es.enter_context(nc.semaphore(f"s_hld{p}")) for p in range(2)]
        s_mm2 = es.enter_context(nc.semaphore("s_mm2"))
        s_oev = es.enter_context(nc.semaphore("s_oev"))
        s_ost = es.enter_context(nc.semaphore("s_ost"))
        block = es.enter_context(nc.Block())

        nld = 5 if mode == "l1" else 3
        ost_counts = []
        tot = 0
        for g in groups:
            tot += len(g)
            ost_counts.append(tot)

        @block.sync
        def _(sync):
            sync.dma_start(out=cnt_s[:, :], in_=cntmy_i[:, :]).then_inc(s_ld, 16)
            sync.dma_start(out=onesrow_s[:, :], in_=onesrow_c[:, :]).then_inc(s_ld, 16)
            sync.dma_start(out=slotcol_s[:, :], in_=slotcol_c[:, :]).then_inc(s_ld, 16)
            if mode == "l1":
                sync.dma_start(out=b1_s[:, :], in_=b1_i[:, :]).then_inc(s_ld, 16)
                sync.dma_start(out=b2bc_s[:, :], in_=b2bc_i[:, :]).then_inc(s_ld, 16)
            sync.wait_ge(s_g2, 16 * NSLOT_T)
            sync.dma_start(out=xg_d.ap().rearrange("(t p) d -> p t d", p=128),
                           in_=xg_all[:, :].rearrange("p (t d) -> p t d", d=D)).then_inc(s_xst, 16)
            sync.wait_ge(s_xst, 16)
            for k in range(8):
                sync.dma_start(out=xgT[:, k * CAP:(k + 1) * CAP],
                               in_=xg_d.ap()[:, k * 128:(k + 1) * 128],
                               transpose=True).then_inc(s_xT, 16)
            for mi in range(NMIE):
                sync.wait_ge(s_h, n_chunks * (mi + 1))
                sync.dma_start(out=h_d.ap()[mi, :, :],
                               in_=h_sb[:, (mi % 2) * CAP:(mi % 2 + 1) * CAP]).then_inc(s_hstP[mi % 2], 16)
            eout_r = eout_o.rearrange("(t p) d -> p t d", p=128)
            nst = 0
            for gi, g in enumerate(groups):
                for k2 in range(NMIE // 2):
                    j2 = gi * (NMIE // 2) + k2
                    if j2 >= 2:
                        sync.wait_ge(s_mm2, 2 * (j2 - 1))
                    sync.wait_ge(s_hstP[0], 16 * (k2 + 1))
                    sync.wait_ge(s_hstP[1], 16 * (k2 + 1))
                    sync.dma_start(out=hkb[:, (j2 % 2) * 2 * CAP:(j2 % 2 + 1) * 2 * CAP]
                                   .rearrange("p (a t) -> p a t", a=2),
                                   in_=h_d.ap()[2 * k2:2 * k2 + 2, :, :]
                                   .rearrange("a p t -> p a t")).then_inc(s_hldP[j2 % 2], 16)
                for mm_, m in enumerate(g):
                    sync.wait_ge(s_oev, ost_counts[gi] - len(g) + mm_ + 1)
                    nst += 1
                    sync.dma_start(out=eout_r[:, m, :],
                                   in_=outb[:, mm_ * D:(mm_ + 1) * D]).then_inc(s_ost, 16)
            sync.wait_ge(s_ost, 16 * NSLOT_T)

        @block.gpsimd
        def _(gpsimd):
            gpsimd.wait_ge(s_vm, 1)
            for it in range(NSLOT_T):
                gpsimd.wait_ge(s_vp, 10 + it + 1)
                gpsimd.indirect_dma_start(
                    out=d2val[:, it:it + 1], out_offset=None,
                    in_=d2my_i[:, :],
                    in_offset=IndirectOffsetOnAxis(ap=srci[:, it:it + 1], axis=0),
                    bounds_check=16 * SLOC - 1, oob_is_err=False,
                ).then_inc(s_g1, 16)
                gpsimd.wait_ge(s_vr, it + 1)
                gpsimd.indirect_dma_start(
                    out=xg_all[:, it * D:(it + 1) * D], out_offset=None,
                    in_=xnall_i[:, :],
                    in_offset=IndirectOffsetOnAxis(ap=rowi[:, it:it + 1], axis=0),
                    bounds_check=NCORES * T - 1, oob_is_err=False,
                ).then_inc(s_g2, 16)
            for mi in range(NMIE):
                if mi >= 2:
                    gpsimd.wait_ge(s_mmc, n_chunks * (mi - 1))
                gpsimd.dma_start(out=wgb[:, (mi % 2) * 1024:(mi % 2) * 1024 + 1024]
                                 .rearrange("p (k i) -> p k i", i=128),
                                 in_=wg_i[:, mi * 128:(mi + 1) * 128]
                                 .rearrange("(k p) i -> p k i", p=128)).then_inc(s_wgP[mi % 2], 16)
                if mode == "l0":
                    gpsimd.dma_start(out=wub[:, (mi % 2) * 1024:(mi % 2) * 1024 + 1024]
                                     .rearrange("p (k i) -> p k i", i=128),
                                     in_=wu_i[:, mi * 128:(mi + 1) * 128]
                                     .rearrange("(k p) i -> p k i", p=128)).then_inc(s_wgP[mi % 2], 16)
            gpsimd.dma_start(out=wdb[:, :].rearrange("p (k c) -> p k c", c=D),
                             in_=wd_i[:, :].rearrange("(k p) c -> p k c", p=128)).then_inc(s_wd, 16)

        @block.vector
        def _(vector):
            vector.wait_ge(s_ld, 16 * nld)
            vector.memset(xg_all[:, :], 0.0).then_inc(s_vm, 1)
            vector.memzero(pref[:1, :])
            vector.drain()
            vector.tensor_tensor_scan(out=pref[0:1, 1:17], data0=onesrow_s[0:1, 0:16],
                                      data1=cnt_s[0:1, :], initial=0.0,
                                      op0=ALU.mult, op1=ALU.add)
            vector.drain().then_inc(s_vp, 1)
            vector.wait_ge(s_pb, 1)
            vector.tensor_copy(prefb[:, :], pP[:, 0:17])
            vector.drain().then_inc(s_pb, 1)
            for it in range(NSLOT_T):
                vector.tensor_scalar(out=maskk[:, :], in0=prefb[:, 0:16],
                                     scalar1=slotcol_s[:, it:it + 1], scalar2=None, op0=ALU.is_le)
                vector.drain()
                vector.reduce_sum(kcnt[:, :], maskk[:, :], axis=AX.X)
                vector.tensor_tensor(out=prefsel[:, :], in0=prefb[:, 0:16], in1=maskk[:, :], op=ALU.mult)
                vector.drain()
                vector.reduce_max(psel[:, :], prefsel[:, :], axis=AX.X)
                vector.drain()
                vector.tensor_scalar(out=jcol[:, :], in0=psel[:, :], scalar1=-1.0,
                                     scalar2=slotcol_s[:, it:it + 1], op0=ALU.mult, op1=ALU.add)
                vector.tensor_scalar(out=srcf[:, :], in0=kcnt[:, :], scalar1=float(SLOC),
                                     scalar2=-float(SLOC), op0=ALU.mult, op1=ALU.add)
                vector.drain()
                vector.tensor_tensor(out=srcf[:, :], in0=srcf[:, :], in1=jcol[:, :], op=ALU.add)
                vector.drain()
                vector.tensor_copy(srci[:, it:it + 1], srcf[:, :]).then_inc(s_vp, 10 if it == 0 else 1)
                vector.wait_ge(s_g1, 16 * (it + 1))
                vector.tensor_copy(rowi[:, it:it + 1], d2val[:, it:it + 1]).then_inc(s_vr, 1)
            if mode == "l0":
                for mi in range(NMIE):
                    if mi >= 2:
                        vector.wait_ge(s_hstP[mi % 2], 16 * ((mi - 2) // 2 + 1))
                    for n in range(n_chunks):
                        vector.wait_ge(s_sil, n_chunks * mi + n + 1)
                        vector.tensor_tensor(
                            out=h_sb[:, (mi % 2) * CAP + n * 512:(mi % 2) * CAP + (n + 1) * 512],
                            in0=gsil[:, n * 512:(n + 1) * 512],
                            in1=pU[:, :], op=ALU.mult).then_inc(s_h, 1)
            for gi, g in enumerate(groups):
                if gi >= 1:
                    vector.wait_ge(s_ost, 16 * ost_counts[gi - 1])
                vector.wait_ge(s_mm2, (gi + 1) * NMIE)
                for mm_, m in enumerate(g):
                    if mode == "l0":
                        vector.tensor_copy(outb[:, mm_ * D:(mm_ + 1) * D], pO[mm_][:, :]).then_inc(s_oev, 1)
                    else:
                        vector.tensor_tensor(out=outb[:, mm_ * D:(mm_ + 1) * D],
                                             in0=pO[mm_][:, :], in1=b2bc_s[:, :],
                                             op=ALU.add).then_inc(s_oev, 1)

        @block.scalar
        def _(scalar):
            if mode == "l0":
                for mi in range(NMIE):
                    for n in range(n_chunks):
                        scalar.wait_ge(s_mmc, n_chunks * mi + n + 1)
                        if mi >= 1:
                            scalar.wait_ge(s_h, n_chunks * (mi - 1) + n + 1)
                        scalar.activation(out=gsil[:, n * 512:(n + 1) * 512], in_=pP[:, :],
                                          func=AF.Silu, bias=0.0, scale=1.0).then_inc(s_sil, 1)
            else:
                scalar.wait_ge(s_ld, 16 * nld)
                for mi in range(NMIE):
                    if mi >= 2:
                        scalar.wait_ge(s_hstP[mi % 2], 16 * ((mi - 2) // 2 + 1))
                    for n in range(n_chunks):
                        scalar.wait_ge(s_mmc, n_chunks * mi + n + 1)
                        scalar.activation(
                            out=h_sb[:, (mi % 2) * CAP + n * 512:(mi % 2) * CAP + (n + 1) * 512],
                            in_=pP[:, :], func=AF.Relu,
                            bias=b1_s[:, mi:mi + 1], scale=1.0).then_inc(s_h, 1)

        @block.tensor
        def _(tensor):
            tensor.wait_ge(s_vp, 1)
            tensor.wait_ge(s_ld, 16 * nld)
            tensor.matmul(pP[:, 0:17], onesrow_s[:1, :], pref[:1, :],
                          start=True, stop=True).then_inc(s_pb, 1)
            tensor.wait_ge(s_pb, 2)
            tensor.wait_ge(s_xT, 128)
            for mi in range(NMIE):
                nw = 32 if mode == "l0" else 16
                tensor.wait_ge(s_wgP[mi % 2], nw * (mi // 2 + 1))
                for n in range(n_chunks):
                    cidx = n_chunks * mi + n
                    if cidx >= 1:
                        tensor.wait_ge(s_h, cidx)
                    base = (mi % 2) * 1024
                    for k in range(8):
                        mmg = tensor.matmul(pP[:, :],
                                            wgb[:, base + k * 128:base + (k + 1) * 128],
                                            xgT[:, k * CAP + n * 512:k * CAP + (n + 1) * 512],
                                            start=(k == 0), stop=(k == 7))
                    if mode == "l0":
                        for k in range(8):
                            mmg = tensor.matmul(pU[:, :],
                                                wub[:, base + k * 128:base + (k + 1) * 128],
                                                xgT[:, k * CAP + n * 512:k * CAP + (n + 1) * 512],
                                                start=(k == 0), stop=(k == 7))
                    mmg.then_inc(s_mmc, 1)
            tensor.wait_ge(s_wd, 16)
            for gi, g in enumerate(groups):
                if gi >= 1:
                    tensor.wait_ge(s_oev, ost_counts[gi - 1])
                for k in range(NMIE):
                    j2 = (gi * NMIE + k) // 2
                    hb = (j2 % 2) * 2 * CAP + (k % 2) * CAP
                    if k % 2 == 0:
                        tensor.wait_ge(s_hldP[j2 % 2], 16 * (j2 // 2 + 1))
                    for mm_, m in enumerate(g):
                        for nn in range(2):
                            mm2 = tensor.matmul(
                                pO[mm_][:, nn * 512:(nn + 1) * 512],
                                hkb[:, hb + m * 128:hb + (m + 1) * 128],
                                wdb[:, k * D + nn * 512:k * D + (nn + 1) * 512],
                                start=(k == 0), stop=(k == NMIE - 1))
                    mm2.then_inc(s_mm2, 1)

    return nc



def build_launch_F():
    nc = bass.Bass("TRN2", target_bir_lowering=False, debug=False)
    xn_i = nc.dram_tensor("xn_i", [T, D], BF16, kind="ExternalInput").ap()
    w1T_i = nc.dram_tensor("w1T_i", [D, D], F32, kind="ExternalInput").ap()
    b1bc_i = nc.dram_tensor("b1bc_i", [128, D], F32, kind="ExternalInput").ap()
    wwT_i = nc.dram_tensor("wwT_i", [D, 496], F32, kind="ExternalInput").ap()
    bwbc_i = nc.dram_tensor("bwbc_i", [128, 496], F32, kind="ExternalInput").ap()
    w2T_i = nc.dram_tensor("w2T_i", [D, D], F32, kind="ExternalInput").ap()
    b2bc_i = nc.dram_tensor("b2bc_i", [128, D], F32, kind="ExternalInput").ap()
    gbc_i = nc.dram_tensor("gbc_i", [128, D], F32, kind="ExternalInput").ap()
    bbc_i = nc.dram_tensor("bbc_i", [128, D], F32, kind="ExternalInput").ap()
    attn_o = nc.dram_tensor("attn_o", [T, D], F32, kind="ExternalOutput").ap()
    h_dr = nc.dram_tensor("h_dr", [T, D], BF16, kind="Internal")
    cv_dr = nc.dram_tensor("cv_dr", [T, D], BF16, kind="Internal")

    es = ExitStack()
    with es:
        sb = lambda name, shape, dt: es.enter_context(nc.sbuf_tensor(name, shape, dt))
        ps = lambda name, shape, dt: es.enter_context(nc.psum_tensor(name, shape, dt))
        xT = sb("xT", [128, 8 * T], BF16)
        h_all = sb("h_all", [128, NT * D], BF16)
        w1b = sb("w1b", [128, 8 * D], BF16)
        wwb = sb("wwb", [128, 8 * 496], BF16)
        w2b = sb("w2b", [128, 8 * D], BF16)
        b1bc = sb("b1bc", [128, D], F32)
        bwbc = sb("bwbc", [128, 496], F32)
        b2bc = sb("b2bc", [128, D], F32)
        gbc = sb("gbc", [128, D], F32)
        bbc = sb("bbc", [128, D], F32)
        wl_all = sb("wl_all", [128, NT * 496], F32)
        mx = sb("mx", [128, NT * 16], F32)
        zz = sb("zz", [128, NT * 16], F32)
        rz = sb("rz", [128, NT * 16], F32)
        w29a = sb("w29a", [128, NT * 16], F32)
        w30a = sb("w30a", [128, NT * 16], F32)
        cv2 = sb("cv2", [128, 2 * D], BF16)
        tmp64 = sb("tmp64", [128, D], F32)
        xn2 = sb("xn2", [128, 2 * D], BF16)
        z2 = sb("z2", [128, 2 * D], F32)
        o2 = sb("o2", [128, 2 * D], F32)
        colA = sb("colA", [128, 8], F32)
        pM = [ps(f"pM{i}", [128, D], F32) for i in range(2)]

        s_ld = es.enter_context(nc.semaphore("s_ld"))
        s_xT = es.enter_context(nc.semaphore("s_xT"))
        s_wld = es.enter_context(nc.semaphore("s_wld"))
        s_mmh = es.enter_context(nc.semaphore("s_mmh"))
        s_hev = es.enter_context(nc.semaphore("s_hev"))
        s_hst = es.enter_context(nc.semaphore("s_hst"))
        s_hT = es.enter_context(nc.semaphore("s_hT"))
        s_mmw = es.enter_context(nc.semaphore("s_mmw"))
        s_wev = es.enter_context(nc.semaphore("s_wev"))
        s_sm = es.enter_context(nc.semaphore("s_sm"))
        s_exp = es.enter_context(nc.semaphore("s_exp"))
        s_cv = es.enter_context(nc.semaphore("s_cv"))
        s_cst = [es.enter_context(nc.semaphore(f"s_cst{p}")) for p in range(2)]
        s_cT = es.enter_context(nc.semaphore("s_cT"))
        s_mmz = es.enter_context(nc.semaphore("s_mmz"))
        s_xn2 = [es.enter_context(nc.semaphore(f"s_xn2{p}")) for p in range(2)]
        s_zev = es.enter_context(nc.semaphore("s_zev"))
        s_sq = es.enter_context(nc.semaphore("s_sq"))
        s_oev = es.enter_context(nc.semaphore("s_oev"))
        s_ostP = [es.enter_context(nc.semaphore(f"s_ost{p}")) for p in range(2)]
        block = es.enter_context(nc.Block())

        xn_r = xn_i.rearrange("(t p) d -> p t d", p=128)
        attn_r = attn_o.rearrange("(t p) d -> p t d", p=128)
        h_r = h_dr.ap().rearrange("(t p) d -> p t d", p=128)
        cv_r = cv_dr.ap().rearrange("(t p) d -> p t d", p=128)

        @block.sync
        def _(sync):
            for k in range(8):
                sync.dma_start(out=xT[:, k * T:(k + 1) * T],
                               in_=xn_i[:, k * 128:(k + 1) * 128], transpose=True).then_inc(s_xT, 16)
            for dst, srcin in [(b1bc, b1bc_i), (bwbc, bwbc_i), (b2bc, b2bc_i),
                               (gbc, gbc_i), (bbc, bbc_i)]:
                sync.dma_start(out=dst[:, :], in_=srcin[:, :]).then_inc(s_ld, 16)
            sync.wait_ge(s_hev, NT)
            sync.dma_start(out=h_r, in_=h_all[:, :].rearrange("p (t d) -> p t d", d=D)).then_inc(s_hst, 16)
            sync.wait_ge(s_hst, 16)
            sync.wait_ge(s_mmh, NT)
            for k in range(8):
                sync.dma_start(out=xT[:, k * T:(k + 1) * T],
                               in_=h_dr.ap()[:, k * 128:(k + 1) * 128], transpose=True).then_inc(s_hT, 16)
            for i in range(NT):
                sync.wait_ge(s_cv, i + 1)
                sync.dma_start(out=cv_r[:, i, :],
                               in_=cv2[:, (i % 2) * D:(i % 2 + 1) * D]).then_inc(s_cst[i % 2], 16)
            for p in range(2):
                sync.wait_ge(s_cst[p], 16 * 8)
            sync.wait_ge(s_mmw, NT)
            for k in range(8):
                sync.dma_start(out=xT[:, k * T:(k + 1) * T],
                               in_=cv_dr.ap()[:, k * 128:(k + 1) * 128], transpose=True).then_inc(s_cT, 16)
            for i in range(NT):
                if i >= 2:
                    sync.wait_ge(s_zev, i - 1)
                sync.dma_start(out=xn2[:, (i % 2) * D:(i % 2 + 1) * D],
                               in_=xn_r[:, i, :]).then_inc(s_xn2[i % 2], 16)
                sync.wait_ge(s_oev, i + 1)
                sync.dma_start(out=attn_r[:, i, :],
                               in_=o2[:, (i % 2) * D:(i % 2 + 1) * D]).then_inc(s_ostP[i % 2], 16)
            for p in range(2):
                sync.wait_ge(s_ostP[p], 16 * 8)

        @block.gpsimd
        def _(gpsimd):
            gpsimd.dma_start(out=w1b[:, :].rearrange("p (k c) -> p k c", c=D),
                             in_=w1T_i.rearrange("(k p) c -> p k c", p=128)).then_inc(s_wld, 16)
            gpsimd.dma_start(out=wwb[:, :].rearrange("p (k c) -> p k c", c=496),
                             in_=wwT_i.rearrange("(k p) c -> p k c", p=128)).then_inc(s_wld, 16)
            gpsimd.dma_start(out=w2b[:, :].rearrange("p (k c) -> p k c", c=D),
                             in_=w2T_i.rearrange("(k p) c -> p k c", p=128)).then_inc(s_wld, 16)

        @block.tensor
        def _(tensor):
            tensor.wait_ge(s_xT, 128)
            tensor.wait_ge(s_wld, 48)
            for i in range(NT):
                if i >= 2:
                    tensor.wait_ge(s_hev, i - 1)
                for n in range(2):
                    for k in range(8):
                        mm = tensor.matmul(pM[i % 2][:, n * 512:(n + 1) * 512],
                                           xT[:, k * T + i * 128:k * T + (i + 1) * 128],
                                           w1b[:, k * D + n * 512:k * D + (n + 1) * 512],
                                           start=(k == 0), stop=(k == 7))
                mm.then_inc(s_mmh, 1)
            tensor.wait_ge(s_hT, 128)
            tensor.wait_ge(s_hev, NT)
            for i in range(NT):
                if i >= 2:
                    tensor.wait_ge(s_wev, i - 1)
                for k in range(8):
                    mm = tensor.matmul(pM[i % 2][:, 0:496],
                                       xT[:, k * T + i * 128:k * T + (i + 1) * 128],
                                       wwb[:, k * 496:(k + 1) * 496],
                                       start=(k == 0), stop=(k == 7))
                mm.then_inc(s_mmw, 1)
            tensor.wait_ge(s_cT, 128)
            tensor.wait_ge(s_wev, NT)
            for i in range(NT):
                if i >= 2:
                    tensor.wait_ge(s_zev, i - 1)
                for n in range(2):
                    for k in range(8):
                        mm = tensor.matmul(pM[i % 2][:, n * 512:(n + 1) * 512],
                                           xT[:, k * T + i * 128:k * T + (i + 1) * 128],
                                           w2b[:, k * D + n * 512:k * D + (n + 1) * 512],
                                           start=(k == 0), stop=(k == 7))
                mm.then_inc(s_mmz, 1)

        @block.vector
        def _(vector):
            vector.wait_ge(s_ld, 80)
            for i in range(NT):
                vector.wait_ge(s_mmh, i + 1)
                vector.tensor_tensor(out=h_all[:, i * D:(i + 1) * D], in0=pM[i % 2][:, :],
                                     in1=b1bc[:, :], op=ALU.add).then_inc(s_hev, 1)
            for i in range(NT):
                vector.wait_ge(s_mmw, i + 1)
                vector.tensor_tensor(out=wl_all[:, i * 496:(i + 1) * 496],
                                     in0=pM[i % 2][:, 0:496], in1=bwbc[:, :],
                                     op=ALU.add).then_inc(s_wev, 1)
            wl4 = wl_all[:, :].rearrange("p (g kk) -> p g kk", kk=31)
            vector.reduce_max(mx[:, :], wl4, axis=AX.X)
            vector.drain()
            for gidx in range(NT * 16):
                vector.tensor_scalar(out=wl_all[:, gidx * 31:(gidx + 1) * 31],
                                     in0=wl_all[:, gidx * 31:(gidx + 1) * 31],
                                     scalar1=mx[:, gidx:gidx + 1], scalar2=None, op0=ALU.subtract)
            vector.drain().then_inc(s_sm, 1)
            vector.wait_ge(s_exp, 1)
            vector.reduce_sum(zz[:, :], wl4, axis=AX.X)
            vector.drain()
            vector.reciprocal(rz[:, :], zz[:, :])
            vector.drain()
            wlk = wl_all[:, :].rearrange("p (g kk) -> p kk g", kk=31)
            vector.tensor_copy(w29a[:, :], wlk[:, 29:30, :].rearrange("p a g -> p (a g)"))
            vector.tensor_copy(w30a[:, :], wlk[:, 30:31, :].rearrange("p a g -> p (a g)"))
            vector.drain()
            vector.tensor_tensor(out=w29a[:, :], in0=w29a[:, :], in1=rz[:, :], op=ALU.mult)
            vector.tensor_tensor(out=w30a[:, :], in0=w30a[:, :], in1=rz[:, :], op=ALU.mult)
            vector.drain()
            for i in range(NT):
                if i >= 2:
                    vector.wait_ge(s_cst[i % 2], 16 * (i // 2))
                if i < 8:
                    for hh in range(16):
                        vector.tensor_scalar(
                            out=cv2[:, (i % 2) * D + hh * 64:(i % 2) * D + (hh + 1) * 64],
                            in0=h_all[:, i * D + hh * 64:i * D + (hh + 1) * 64],
                            scalar1=w30a[:, i * 16 + hh:i * 16 + hh + 1], scalar2=None,
                            op0=ALU.mult)
                else:
                    for hh in range(16):
                        vector.tensor_scalar(
                            out=tmp64[:, hh * 64:(hh + 1) * 64],
                            in0=h_all[:, (i - 8) * D + hh * 64:(i - 8) * D + (hh + 1) * 64],
                            scalar1=w29a[:, i * 16 + hh:i * 16 + hh + 1], scalar2=None,
                            op0=ALU.mult)
                    for hh in range(16):
                        vector.scalar_tensor_tensor(
                            out=cv2[:, (i % 2) * D + hh * 64:(i % 2) * D + (hh + 1) * 64],
                            in0=h_all[:, i * D + hh * 64:i * D + (hh + 1) * 64],
                            scalar=w30a[:, i * 16 + hh:i * 16 + hh + 1],
                            in1=tmp64[:, hh * 64:(hh + 1) * 64],
                            op0=ALU.mult, op1=ALU.add)
                vector.drain().then_inc(s_cv, 1)
            for i in range(NT):
                vector.wait_ge(s_mmz, i + 1)
                vector.wait_ge(s_xn2[i % 2], 16 * (i // 2 + 1))
                if i >= 2:
                    vector.wait_ge(s_ostP[i % 2], 16 * (i // 2))
                bb = (i % 2) * D
                vector.tensor_tensor(out=z2[:, bb:bb + D], in0=pM[i % 2][:, :],
                                     in1=b2bc[:, :], op=ALU.add).then_inc(s_zev, 1)
                vector.drain()
                vector.tensor_tensor(out=z2[:, bb:bb + D], in0=z2[:, bb:bb + D],
                                     in1=xn2[:, bb:bb + D], op=ALU.add)
                vector.drain()
                vector.reduce_sum(colA[:, 0:1], z2[:, bb:bb + D], axis=AX.X)
                vector.drain().then_inc(s_sq, 1)
                vector.wait_ge(s_sq, 4 * i + 2)
                vector.tensor_scalar(out=colA[:, 1:2], in0=colA[:, 0:1], scalar1=1.0 / D,
                                     scalar2=None, op0=ALU.mult)
                vector.tensor_scalar(out=colA[:, 2:3], in0=colA[:, 4:5], scalar1=1.0 / D,
                                     scalar2=None, op0=ALU.mult)
                vector.tensor_tensor(out=colA[:, 3:4], in0=colA[:, 1:2], in1=colA[:, 1:2], op=ALU.mult)
                vector.drain()
                vector.tensor_tensor(out=colA[:, 3:4], in0=colA[:, 2:3], in1=colA[:, 3:4], op=ALU.subtract)
                vector.drain()
                vector.tensor_scalar(out=colA[:, 3:4], in0=colA[:, 3:4], scalar1=LN_EPS,
                                     scalar2=None, op0=ALU.add).then_inc(s_sq, 1)
                vector.wait_ge(s_sq, 4 * i + 4)
                vector.reciprocal(colA[:, 6:7], colA[:, 5:6])
                vector.drain()
                vector.tensor_scalar(out=o2[:, bb:bb + D], in0=z2[:, bb:bb + D],
                                     scalar1=colA[:, 1:2], scalar2=colA[:, 6:7],
                                     op0=ALU.subtract, op1=ALU.mult)
                vector.drain()
                vector.tensor_tensor(out=o2[:, bb:bb + D], in0=o2[:, bb:bb + D],
                                     in1=gbc[:, :], op=ALU.mult)
                vector.drain()
                vector.tensor_tensor(out=o2[:, bb:bb + D], in0=o2[:, bb:bb + D],
                                     in1=bbc[:, :], op=ALU.add).then_inc(s_oev, 1)

        @block.scalar
        def _(scalar):
            scalar.wait_ge(s_sm, 1)
            scalar.activation(out=wl_all[:, :], in_=wl_all[:, :], func=AF.Exp,
                              bias=0.0, scale=1.0).then_inc(s_exp, 1)
            for i in range(NT):
                scalar.wait_ge(s_sq, 4 * i + 1)
                bb = (i % 2) * D
                scalar.activation(out=tmp64[:, :], in_=z2[:, bb:bb + D], func=AF.Square,
                                  bias=0.0, scale=1.0, accum_out=colA[:, 4:5]).then_inc(s_sq, 1)
                scalar.wait_ge(s_sq, 4 * i + 3)
                scalar.activation(out=colA[:, 5:6], in_=colA[:, 3:4], func=AF.Sqrt,
                                  bias=0.0, scale=1.0).then_inc(s_sq, 1)

    return nc



def build_combine(layer):
    nch = 2 if layer == 0 else 1
    nc = bass.Bass("TRN2", target_bir_lowering=False, debug=False)
    xres_i = nc.dram_tensor("xres_i", [T, D], F32, kind="ExternalInput").ap()
    attn_i = nc.dram_tensor("attn_i", [T, D], F32, kind="ExternalInput").ap()
    if layer == 0:
        shared_i = nc.dram_tensor("shared_i", [T, D], F32, kind="ExternalInput").ap()
    eout_i = nc.dram_tensor("eout_i", [E * CAP, D], BF16, kind="ExternalInput").ap()
    meta_i = nc.dram_tensor("meta_i", [128, NT * 8], F32, kind="ExternalInput").ap()
    cnt_i = nc.dram_tensor("cnt_i", [16, E], F32, kind="ExternalInput").ap()
    maskA_i = nc.dram_tensor("maskA_i", [16, 1], F32, kind="ExternalInput").ap()
    maskB_i = nc.dram_tensor("maskB_i", [16, 1], F32, kind="ExternalInput").ap()
    xo_o = nc.dram_tensor("xo_o", [T, D], F32, kind="ExternalOutput").ap()
    e8_c = nc.inline_tensor(np.tile(np.arange(8, dtype=np.float32), (128, 1)), name="c_e8c").ap()
    onesrow_c = nc.inline_tensor(np.ones((1, 128), np.float32), name="c_onesrow3").ap()

    es = ExitStack()
    with es:
        sb = lambda name, shape, dt: es.enter_context(nc.sbuf_tensor(name, shape, dt))
        ps = lambda name, shape, dt: es.enter_context(nc.psum_tensor(name, shape, dt))
        acc = sb("acc", [128, NT * D], F32)
        str2 = sb("str2", [128, 2 * D], F32)
        str3 = sb("str3", [128, 2 * D], F32)
        meta_s = sb("meta_s", [128, NT * 8], F32)
        cnt_s = sb("cnt_s", [16, E], F32)
        mA_s = sb("mA_s", [16, 1], F32)
        mB_s = sb("mB_s", [16, 1], F32)
        e8_s = sb("e8_s", [128, 8], F32)
        onesrow_s = sb("onesrow_s", [1, 128], F32)
        segb = sb("segb", [1, 16], F32)
        segbb = sb("segbb", [128, 16], F32)
        maskc = sb("maskc", [128, 8], F32)
        basec = sb("basec", [128, 1], F32)
        posg = sb("posg", [128, 1], F32)
        kept = sb("kept", [128, 1], F32)
        wfin = sb("wfin", [128, 1], F32)
        slotf = sb("slotf", [128, 1], F32)
        sloti = sb("sloti", [128, 2], I32)
        gbuf = sb("gbuf", [128, 2 * D], BF16)
        pS = ps("pS", [1, 16], F32)
        pB = ps("pB", [128, 16], F32)

        s_ld = es.enter_context(nc.semaphore("s_ld"))
        s_acc = es.enter_context(nc.semaphore("s_acc"))
        s_st2 = [es.enter_context(nc.semaphore(f"s_st2{p}")) for p in range(2)]
        s_st3 = [es.enter_context(nc.semaphore(f"s_st3{p}")) for p in range(2)]
        s_pe = es.enter_context(nc.semaphore("s_pe"))
        s_vseg = es.enter_context(nc.semaphore("s_vseg"))
        s_vidx = es.enter_context(nc.semaphore("s_vidx"))
        s_g = es.enter_context(nc.semaphore("s_g"))
        s_gc = es.enter_context(nc.semaphore("s_gc"))
        s_ost = [es.enter_context(nc.semaphore(f"s_ost{p}")) for p in range(2)]
        block = es.enter_context(nc.Block())

        xres_r = xres_i.rearrange("(t p) d -> p t d", p=128)
        attn_r = attn_i.rearrange("(t p) d -> p t d", p=128)
        xo_r = xo_o.rearrange("(t p) d -> p t d", p=128)

        @block.sync
        def _(sync):
            sync.dma_start(out=acc[:, :].rearrange("p (t d) -> p t d", d=D), in_=xres_r).then_inc(s_ld, 16)
            sync.dma_start(out=meta_s[:, :], in_=meta_i[:, :]).then_inc(s_ld, 16)
            sync.dma_start(out=cnt_s[:, :], in_=cnt_i[:, :]).then_inc(s_ld, 16)
            sync.dma_start(out=mA_s[:, :], in_=maskA_i[:, :]).then_inc(s_ld, 16)
            sync.dma_start(out=mB_s[:, :], in_=maskB_i[:, :]).then_inc(s_ld, 16)
            sync.dma_start(out=e8_s[:, :], in_=e8_c[:, :]).then_inc(s_ld, 16)
            sync.dma_start(out=onesrow_s[:, :], in_=onesrow_c[:, :]).then_inc(s_ld, 16)
            for i in range(NT):
                if i >= 2:
                    sync.wait_ge(s_acc, i - 1)
                sync.dma_start(out=str2[:, (i % 2) * D:(i % 2 + 1) * D],
                               in_=attn_r[:, i, :]).then_inc(s_st2[i % 2], 16)
                if layer == 0:
                    sync.dma_start(out=str3[:, (i % 2) * D:(i % 2 + 1) * D],
                                   in_=shared_i.rearrange("(t p) d -> p t d", p=128)[:, i, :]
                                   ).then_inc(s_st3[i % 2], 16)
            for i in range(NT):
                sync.wait_ge(s_gc, nch * (i + 1))
                sync.dma_start(out=xo_r[:, i, :],
                               in_=acc[:, i * D:(i + 1) * D]).then_inc(s_ost[i % 2], 16)
            for p in range(2):
                sync.wait_ge(s_ost[p], 16 * 8)

        @block.tensor
        def _(tensor):
            tensor.wait_ge(s_ld, 112)
            tensor.matmul(pS[:1, 0:8], mA_s[:, :], cnt_s[:, :], start=True, stop=True).then_inc(s_pe, 1)
            tensor.matmul(pS[:1, 8:16], mB_s[:, :], cnt_s[:, :], start=True, stop=True).then_inc(s_pe, 1)
            tensor.wait_ge(s_vseg, 1)
            tensor.matmul(pB[:, :], onesrow_s[:1, :], segb[:1, :], start=True, stop=True).then_inc(s_pe, 1)

        @block.gpsimd
        def _(gpsimd):
            for g in range(NT * nch):
                gpsimd.wait_ge(s_vidx, g + 1)
                if g >= 2:
                    gpsimd.wait_ge(s_gc, g - 1)
                gpsimd.indirect_dma_start(
                    out=gbuf[:, (g % 2) * D:(g % 2 + 1) * D], out_offset=None,
                    in_=eout_i[:, :],
                    in_offset=IndirectOffsetOnAxis(ap=sloti[:, g % 2:g % 2 + 1], axis=0),
                    bounds_check=E * CAP - 1, oob_is_err=False,
                ).then_inc(s_g, 16)

        @block.vector
        def _(vector):
            vector.wait_ge(s_ld, 112)
            vector.wait_ge(s_pe, 2)
            vector.tensor_copy(segb[:1, :], pS[:1, :])
            vector.drain().then_inc(s_vseg, 1)
            vector.wait_ge(s_pe, 3)
            vector.tensor_copy(segbb[:, :], pB[:, :])
            vector.drain()
            for i in range(NT):
                vector.wait_ge(s_st2[i % 2], 16 * (i // 2 + 1))
                vector.tensor_tensor(out=acc[:, i * D:(i + 1) * D], in0=acc[:, i * D:(i + 1) * D],
                                     in1=str2[:, (i % 2) * D:(i % 2 + 1) * D], op=ALU.add)
                vector.drain()
                if layer == 0:
                    vector.wait_ge(s_st3[i % 2], 16 * (i // 2 + 1))
                    vector.tensor_tensor(out=acc[:, i * D:(i + 1) * D], in0=acc[:, i * D:(i + 1) * D],
                                         in1=str3[:, (i % 2) * D:(i % 2 + 1) * D], op=ALU.add)
                vector.nop().then_inc(s_acc, 1)
                seg = 0 if i < 8 else 1
                for ch in range(nch):
                    g = i * nch + ch
                    mb8 = i * 8 + (0 if ch == 0 else 3)
                    ecol = meta_s[:, mb8:mb8 + 1]
                    pcol = meta_s[:, mb8 + 1:mb8 + 2]
                    wcol = meta_s[:, mb8 + 2:mb8 + 3]
                    vector.tensor_scalar(out=maskc[:, :], in0=e8_s[:, :], scalar1=ecol,
                                         scalar2=None, op0=ALU.is_equal)
                    vector.drain()
                    vector.tensor_tensor(out=maskc[:, :], in0=maskc[:, :],
                                         in1=segbb[:, seg * 8:(seg + 1) * 8], op=ALU.mult)
                    vector.drain()
                    vector.reduce_sum(basec[:, :], maskc[:, :], axis=AX.X)
                    vector.drain()
                    vector.tensor_tensor(out=posg[:, :], in0=pcol, in1=basec[:, :], op=ALU.add)
                    vector.drain()
                    vector.tensor_scalar(out=kept[:, :], in0=posg[:, :], scalar1=float(CAP),
                                         scalar2=None, op0=ALU.is_lt)
                    vector.drain()
                    vector.tensor_tensor(out=wfin[:, :], in0=wcol, in1=kept[:, :], op=ALU.mult)
                    vector.scalar_tensor_tensor(out=slotf[:, :], in0=ecol, scalar=float(CAP),
                                                in1=posg[:, :], op0=ALU.mult, op1=ALU.add)
                    vector.tensor_scalar(out=kept[:, :], in0=kept[:, :], scalar1=-BIG,
                                         scalar2=BIG, op0=ALU.mult, op1=ALU.add)
                    vector.drain()
                    vector.tensor_tensor(out=slotf[:, :], in0=slotf[:, :], in1=kept[:, :], op=ALU.add)
                    vector.drain()
                    vector.tensor_copy(sloti[:, g % 2:g % 2 + 1],
                                       slotf[:, :]).then_inc(s_vidx, 1)
                    vector.wait_ge(s_g, 16 * (g + 1))
                    vector.drain()
                    vector.scalar_tensor_tensor(out=acc[:, i * D:(i + 1) * D],
                                                in0=gbuf[:, (g % 2) * D:(g % 2 + 1) * D],
                                                scalar=wfin[:, :],
                                                in1=acc[:, i * D:(i + 1) * D],
                                                op0=ALU.mult, op1=ALU.add).then_inc(s_gc, 1)

    return nc



_B_SKIP_WIN = False
_SIM_SAFE = False


def _dr(eng):
    if _SIM_SAFE:
        eng.drain()


def build_launch_B():
    nc = bass.Bass("TRN2", target_bir_lowering=False, debug=False)
    xnm_i = nc.dram_tensor("xnm_i", [T, D], BF16, kind="ExternalInput").ap()
    xnp_i = nc.dram_tensor("xnp_i", [512, D], BF16, kind="ExternalInput").ap()
    wkvg_i = nc.dram_tensor("wkvg_i", [D, 1056], F32, kind="ExternalInput").ap()
    wq_i = nc.dram_tensor("wq_i", [D, 512], F32, kind="ExternalInput").ap()
    wo_i = nc.dram_tensor("wo_i", [512, D], F32, kind="ExternalInput").ap()
    bg_i = nc.dram_tensor("bg_i", [128, 32], F32, kind="ExternalInput").ap()
    mk_i = nc.dram_tensor("mk_i", [64, 1], F32, kind="ExternalInput").ap()
    ini_i = nc.dram_tensor("ini_i", [64, 288], F32, kind="ExternalInput").ap()
    attn_o = nc.dram_tensor("attn_o", [T, D], F32, kind="ExternalOutput").ap()
    identc = nc.inline_tensor(np.eye(128, dtype=BF), name="c_idB").ap()
    identfc = nc.inline_tensor(np.eye(128, dtype=np.float32), name="c_idBf").ap()
    onesrc = nc.inline_tensor(np.ones((1, 128), BF), name="c_onesB").ap()
    bandm_np = np.zeros((128, 1024), np.float32)
    for e_ in range(4):
        for st2_ in range(2):
            r_ = e_ * 2 + st2_
            for srow_ in range(128 if st2_ == 0 else 31):
                sglob_ = st2_ * 128 + srow_
                for p_ in range(128):
                    if p_ <= sglob_ <= p_ + 31:
                        bandm_np[srow_, r_ * 128 + p_] = 1.0
    bandc = nc.inline_tensor(bandm_np.astype(BF), name="c_bandB").ap()

    es = ExitStack()
    with es:
        sb = lambda name, shape, dt: es.enter_context(nc.sbuf_tensor(name, shape, dt))
        ps = lambda name, shape, dt: es.enter_context(nc.psum_tensor(name, shape, dt))
        xmT = sb("xmT", [128, 8 * T], BF16)
        xpT = sb("xpT", [128, 8 * 512], BF16)
        wkvg_s = sb("wkvg_s", [128, 8 * 1056], BF16)
        wq_s = sb("wq_s", [128, 8 * 512], BF16)
        wo_s = sb("wo_s", [128, 4 * D], BF16)
        bg_s = sb("bg_s", [128, 32], F32)
        mk_s = sb("mk_s", [64, 1], F32)
        ini_s = sb("ini_s", [64, 288], F32)
        ident_s = sb("ident_s", [128, 128], BF16)
        onesr_s = sb("onesr_s", [1, 128], BF16)
        kv_f = sb("kv_f", [128, 2 * 1024], F32)
        gt_f = sb("gt_f", [128, 2 * 32], F32)
        g_s = sb("g_s", [128, 2 * 32], F32)
        a_s = sb("a_s", [128, 2 * 4], F32)
        abf = sb("abf", [128, 2 * 4], BF16)
        gk_f = sb("gk_f", [128, 2 * 256], F32)
        gv_f = sb("gv_f", [128, 2 * 256], F32)
        gkbf = sb("gkbf", [128, 512], BF16)
        gvbf = sb("gvbf", [128, 512], BF16)
        gkT = sb("gkT", [64, 1024], BF16)
        gvT = sb("gvT", [64, 1024], BF16)
        arow = sb("arow", [1, 1024], BF16)
        ab_s = sb("ab_s", [64, 1024], F32)
        sk_s = sb("sk_s", [64, 1024], F32)
        sv_s = sb("sv_s", [64, 1024], F32)
        w31k = sb("w31k", [64, 124], F32)
        w31v = sb("w31v", [64, 124], F32)
        fkA = sb("fkA", [65, 4 * 287], BF16)
        svb = sb("svb", [64, 4 * 287], BF16)
        fv1 = sb("fv1", [128, 4 * 65], BF16)
        fv2 = sb("fv2", [128, 4 * 65], BF16)
        fv3 = sb("fv3", [32, 4 * 65], BF16)
        q_s = sb("q_s", [64, 8 * 256], BF16)
        qaug = sb("qaug", [65, 2 * 128], BF16)
        mx4 = sb("mx4", [128, 4], F32)
        mcol = sb("mcol", [128, 1], F32)
        nmb = sb("nmb", [128, 2], F32)
        expT = sb("expT", [128, 2048], BF16)
        identf = sb("identf", [128, 128], F32)
        band_s = sb("band_s", [128, 1024], BF16)
        rz1 = sb("rz1", [128, 1], F32)
        y_s = sb("y_s", [128, 2 * 64], BF16)
        yT = sb("yT", [128, 2 * 512], BF16)
        ao_s = sb("ao_s", [128, 2 * D], F32)
        pA = ps("pA", [128, 2048], F32)
        pB = ps("pB", [128, 1536], F32)
        pBF = ps("pBF", [128, 128], BF16)

        semnames = ["ld", "gld", "proj", "kv", "ga", "gs", "am", "gk", "pbfT", "pbfe",
                    "abm", "abe", "scan", "fk",
                    "qm", "qe", "pa", "mx", "qa", "m1", "expd", "ngT",
                    "m2", "msk", "ye", "wo", "woe", "st0", "st1"]
        S = {n: es.enter_context(nc.semaphore("sB_" + n)) for n in semnames}
        block = es.enter_context(nc.Block())

        from collections import defaultdict
        C = defaultdict(int)
        ops = {k: [] for k in ["sync", "gpsimd", "vector", "scalar", "tensor"]}
        mx_hist, ngT_hist, m1_hist, yT_hist, m2_hist = {}, {}, {}, {}, {}
        carry = []

        def em(eng, fn):
            ops[eng].append(fn)

        def f_loads(sync):
            for k in range(8):
                sync.dma_start(out=xmT[:, k * T:(k + 1) * T],
                               in_=xnm_i[:, k * 128:(k + 1) * 128], transpose=True).then_inc(S["ld"], 16)
            for k in range(8):
                sync.dma_start(out=xpT[:, k * 512:(k + 1) * 512],
                               in_=xnp_i[:, k * 128:(k + 1) * 128], transpose=True).then_inc(S["ld"], 16)
            sync.dma_start(out=bg_s[:, :], in_=bg_i[:, :]).then_inc(S["ld"], 16)
            sync.dma_start(out=mk_s[:, :], in_=mk_i[:, :]).then_inc(S["ld"], 16)
            sync.dma_start(out=ini_s[:, :], in_=ini_i[:, :]).then_inc(S["ld"], 16)
            sync.dma_start(out=ident_s[:, :], in_=identc[:, :]).then_inc(S["ld"], 16)
            sync.dma_start(out=identf[:, :], in_=identfc[:, :]).then_inc(S["ld"], 16)
            sync.dma_start(out=onesr_s[:, :], in_=onesrc[:, :]).then_inc(S["ld"], 16)
            sync.dma_start(out=band_s[:, :], in_=bandc[:, :]).then_inc(S["ld"], 16)
        em("sync", f_loads)
        C["ld"] = 23 * 16

        def f_gld(gpsimd):
            gpsimd.dma_start(out=wkvg_s[:, :].rearrange("p (k c) -> p k c", c=1056),
                             in_=wkvg_i.rearrange("(k p) c -> p k c", p=128)).then_inc(S["gld"], 16)
            gpsimd.dma_start(out=wq_s[:, :].rearrange("p (k c) -> p k c", c=512),
                             in_=wq_i.rearrange("(k p) c -> p k c", p=128)).then_inc(S["gld"], 16)
            gpsimd.dma_start(out=wo_s[:, :].rearrange("p (k c) -> p k c", c=D),
                             in_=wo_i.rearrange("(k p) c -> p k c", p=128)).then_inc(S["gld"], 16)
        em("gpsimd", f_gld)
        C["gld"] = 48
        LD, GLD = C["ld"], C["gld"]

        def f_init(vector):
            vector.wait_ge(S["ld"], LD)
            vector.memset(pB[:, :], 0.0)
            vector.memset(pA[:, :], 0.0)
            vector.memset(fkA[64:65, :], 1.0)
            for fvx in (fv1, fv2, fv3):
                for e in range(4):
                    vector.memset(fvx[:, e * 65 + 64:e * 65 + 65], 1.0)
            vector.drain().then_inc(S["fk"], 1)
        em("vector", f_init)
        C["fk"] += 1

        chunks = []
        for b in range(2):
            chunks.append((b, "prev", -1))
            for j in range(4):
                chunks.append((b, "main", j))

        attn_r = attn_o.rearrange("(t p) d -> p t d", p=128)
        ST3 = [(0, 128), (128, 128), (256, 31)]

        for (b, kind, j) in chunks:
            if kind == "prev":
                xsrc, xbase, xstride = xpT, b * 256, 512
            else:
                xsrc, xbase, xstride = xmT, b * 1024 + j * 256, T

            for tt in range(2):
                w = dict(kv=C["kv"], ga=C["ga"], qe=C["qe"], woe=C["woe"], pa=C["pa"])
                def f(tensor, tt=tt, xsrc=xsrc, xbase=xbase, xstride=xstride, w=w):
                    tensor.wait_ge(S["ld"], LD)
                    tensor.wait_ge(S["gld"], GLD)
                    tensor.wait_ge(S["fk"], 1)
                    tensor.wait_ge(S["kv"], w["kv"])
                    tensor.wait_ge(S["ga"], w["ga"])
                    tensor.wait_ge(S["qe"], w["qe"])
                    tensor.wait_ge(S["woe"], w["woe"])
                    tensor.wait_ge(S["pa"], w["pa"])
                    for (c0, nn) in [(0, 512), (512, 512), (1024, 32)]:
                        for k in range(8):
                            mm = tensor.matmul(
                                pA[:, c0:c0 + nn],
                                xsrc[:, k * xstride + xbase + tt * 128:
                                     k * xstride + xbase + tt * 128 + 128],
                                wkvg_s[:, k * 1056 + c0:k * 1056 + c0 + nn],
                                start=(k == 0), stop=(k == 7))
                    mm.then_inc(S["proj"], 1)
                em("tensor", f)
                C["proj"] += 1
                pv = C["proj"]
                wgk = C["gk"]
                def f2(scalar, tt=tt, pv=pv, wgk=wgk):
                    scalar.wait_ge(S["proj"], pv)
                    scalar.wait_ge(S["gk"], wgk)
                    scalar.copy(out=kv_f[:, tt * 1024:(tt + 1) * 1024],
                                in_=pA[:, 0:1024]).then_inc(S["kv"], 1)
                em("scalar", f2)
                C["kv"] += 1
                wgs = C["gs"]
                def f3(vector, tt=tt, pv=pv, wgs=wgs):
                    vector.wait_ge(S["proj"], pv)
                    vector.wait_ge(S["gs"], wgs)
                    vector.tensor_tensor(out=gt_f[:, tt * 32:(tt + 1) * 32],
                                         in0=pA[:, 1024:1056], in1=bg_s[:, :],
                                         op=ALU.add).then_inc(S["ga"], 1)
                em("vector", f3)
                C["ga"] += 1
                gav2 = C["ga"]
                wgk2 = C["gk"]
                def f4(scalar, tt=tt, gav2=gav2, wgk2=wgk2):
                    scalar.wait_ge(S["ga"], gav2)
                    scalar.wait_ge(S["gk"], wgk2)
                    scalar.activation(out=g_s[:, tt * 32:(tt + 1) * 32],
                                      in_=gt_f[:, tt * 32:(tt + 1) * 32],
                                      func=AF.Sigmoid, bias=0.0, scale=1.0).then_inc(S["gs"], 1)
                em("scalar", f4)
                C["gs"] += 1
                gsv2, kvv2 = C["gs"], C["kv"]
                wT = C["pbfT"]
                def f5(vector, tt=tt, gsv2=gsv2, kvv2=kvv2, wT=wT):
                    vector.wait_ge(S["gs"], gsv2)
                    vector.wait_ge(S["kv"], kvv2)
                    vector.wait_ge(S["pbfT"], wT)
                    g3 = g_s[:, tt * 32:(tt + 1) * 32].rearrange("p (h e) -> p e h", e=4)
                    vector.reduce_sum(a_s[:, tt * 4:(tt + 1) * 4], g3, axis=AX.X)
                    vector.drain()
                    vector.tensor_scalar(out=a_s[:, tt * 4:(tt + 1) * 4],
                                         in0=a_s[:, tt * 4:(tt + 1) * 4],
                                         scalar1=1.0, scalar2=None, op0=ALU.min)
                    vector.drain()
                    vector.tensor_scalar(out=abf[:, tt * 4:(tt + 1) * 4],
                                         in0=a_s[:, tt * 4:(tt + 1) * 4],
                                         scalar1=-1.0, scalar2=1.0, op0=ALU.mult,
                                         op1=ALU.add).then_inc(S["am"], 1)
                    for h in range(8):
                        for (srcoff, accf, accb) in [(0, gk_f, gkbf), (512, gv_f, gvbf)]:
                            for e in range(4):
                                kslc = kv_f[:, tt * 1024 + srcoff + h * 64:
                                            tt * 1024 + srcoff + (h + 1) * 64]
                                gcol = g_s[:, tt * 32 + h * 4 + e:tt * 32 + h * 4 + e + 1]
                                dstf = accf[:, tt * 256 + e * 64:tt * 256 + (e + 1) * 64]
                                dstb = accb[:, tt * 256 + e * 64:tt * 256 + (e + 1) * 64]
                                if h == 0:
                                    vector.tensor_scalar(out=dstf, in0=kslc, scalar1=gcol,
                                                         scalar2=None, op0=ALU.mult)
                                elif h < 7:
                                    vector.scalar_tensor_tensor(out=dstf, in0=kslc, scalar=gcol,
                                                                in1=dstf, op0=ALU.mult, op1=ALU.add)
                                else:
                                    vector.scalar_tensor_tensor(out=dstb, in0=kslc, scalar=gcol,
                                                                in1=dstf, op0=ALU.mult, op1=ALU.add)
                                _dr(vector)
                    vector.drain().then_inc(S["gk"], 1)
                em("vector", f5)
                C["am"] += 1
                C["gk"] += 1

            gknow, amnow = C["gk"], C["am"]
            for kvsel in range(2):
                for e in range(4):
                    for tt in range(2):
                        src = gkbf if kvsel == 0 else gvbf
                        dstT = gkT if kvsel == 0 else gvT
                        ev = C["pbfe"]
                        wscan = C["scan"]
                        def f(tensor, src=src, e=e, tt=tt, gknow=gknow, ev=ev, wscan=wscan):
                            tensor.wait_ge(S["gk"], gknow)
                            tensor.wait_ge(S["pbfe"], ev)
                            tensor.transpose(pBF[0:64, :],
                                             src[:, tt * 256 + e * 64:tt * 256 + (e + 1) * 64],
                                             ident_s[:, :]).then_inc(S["pbfT"], 1)
                        em("tensor", f)
                        C["pbfT"] += 1
                        tv = C["pbfT"]
                        def f2(scalar, dstT=dstT, e=e, tt=tt, tv=tv, wscan=wscan):
                            scalar.wait_ge(S["pbfT"], tv)
                            scalar.wait_ge(S["scan"], wscan)
                            scalar.copy(out=dstT[:, e * 256 + tt * 128:e * 256 + (tt + 1) * 128],
                                        in_=pBF[0:64, :]).then_inc(S["pbfe"], 1)
                        em("scalar", f2)
                        C["pbfe"] += 1
            for e in range(4):
                for tt in range(2):
                    ev = C["pbfe"]
                    wab = C["abm"]
                    def f(tensor, e=e, tt=tt, amnow=amnow, ev=ev):
                        tensor.wait_ge(S["am"], amnow)
                        tensor.wait_ge(S["pbfe"], ev)
                        tensor.transpose(pBF[0:1, :],
                                         abf[:, tt * 4 + e:tt * 4 + e + 1],
                                         ident_s[:, :]).then_inc(S["pbfT"], 1)
                    em("tensor", f)
                    C["pbfT"] += 1
                    tv = C["pbfT"]
                    def f2(scalar, e=e, tt=tt, tv=tv, wab=wab):
                        scalar.wait_ge(S["pbfT"], tv)
                        scalar.wait_ge(S["abm"], wab)
                        scalar.copy(out=arow[0:1, e * 256 + tt * 128:e * 256 + (tt + 1) * 128],
                                    in_=pBF[0:1, :]).then_inc(S["pbfe"], 1)
                    em("scalar", f2)
                    C["pbfe"] += 1
            acenow = C["pbfe"]
            for e in range(4):
                wabe = C["abe"]
                wwoe = C["woe"]
                def f(tensor, e=e, acenow=acenow, wabe=wabe, wwoe=wwoe):
                    tensor.wait_ge(S["pbfe"], acenow)
                    tensor.wait_ge(S["abe"], wabe)
                    tensor.wait_ge(S["woe"], wwoe)
                    tensor.matmul(pA[0:64, 1024 + e * 256:1024 + (e + 1) * 256],
                                  onesr_s[0:1, 0:64],
                                  arow[0:1, e * 256:(e + 1) * 256],
                                  start=True, stop=True).then_inc(S["abm"], 1)
                em("tensor", f)
                C["abm"] += 1
                tv = C["abm"]
                wscan = C["scan"]
                def f2(vector, e=e, tv=tv, wscan=wscan):
                    vector.wait_ge(S["abm"], tv)
                    vector.wait_ge(S["scan"], wscan)
                    vector.tensor_copy(ab_s[:, e * 256:(e + 1) * 256],
                                       pA[0:64, 1024 + e * 256:1024 + (e + 1) * 256]
                                       ).then_inc(S["abe"], 1)
                em("vector", f2)
                C["abe"] += 1

            pbfe_now, abenow = C["pbfe"], C["abe"]
            if kind == "prev":
                colK = lambda e: 32 + e
                colV = lambda e: 36 + e
            else:
                colK = lambda e, j=j: j * 4 + e
                colV = lambda e, j=j: 16 + j * 4 + e
            fknow = C["fk"]
            def f_scan(vector, colK=colK, colV=colV, pbfe_now=pbfe_now, abenow=abenow,
                       fknow=fknow):
                vector.wait_ge(S["pbfe"], pbfe_now)
                vector.wait_ge(S["abe"], abenow)
                vector.wait_ge(S["fk"], fknow)
                for e in range(4):
                    vector.tensor_tensor_scan(
                        out=sk_s[:, e * 256:(e + 1) * 256],
                        data0=ab_s[0:64, e * 256:(e + 1) * 256],
                        data1=gkT[:, e * 256:(e + 1) * 256],
                        initial=ini_s[:, colK(e):colK(e) + 1],
                        op0=ALU.mult, op1=ALU.add)
                    vector.tensor_tensor_scan(
                        out=sv_s[:, e * 256:(e + 1) * 256],
                        data0=ab_s[0:64, e * 256:(e + 1) * 256],
                        data1=gvT[:, e * 256:(e + 1) * 256],
                        initial=ini_s[:, colV(e):colV(e) + 1],
                        op0=ALU.mult, op1=ALU.add)
                vector.drain().then_inc(S["scan"], 1)
            em("vector", f_scan)
            C["scan"] += 1
            scannow = C["scan"]

            if kind == "prev":
                def f_blend(vector, scannow=scannow):
                    vector.wait_ge(S["scan"], scannow)
                    vector.drain()
                    for e in range(4):
                        vector.tensor_scalar(out=w31k[:, e * 31:(e + 1) * 31],
                                             in0=sk_s[:, e * 256 + 225:e * 256 + 256],
                                             scalar1=mk_s[:, 0:1], scalar2=None, op0=ALU.mult)
                        vector.tensor_scalar(out=w31v[:, e * 31:(e + 1) * 31],
                                             in0=sv_s[:, e * 256 + 225:e * 256 + 256],
                                             scalar1=mk_s[:, 0:1], scalar2=None, op0=ALU.mult)
                    vector.drain()
                    for e in range(4):
                        vector.tensor_tensor(out=w31k[:, e * 31:(e + 1) * 31],
                                             in0=w31k[:, e * 31:(e + 1) * 31],
                                             in1=ini_s[:, 40 + e * 31:40 + (e + 1) * 31], op=ALU.add)
                        vector.tensor_tensor(out=w31v[:, e * 31:(e + 1) * 31],
                                             in0=w31v[:, e * 31:(e + 1) * 31],
                                             in1=ini_s[:, 164 + e * 31:164 + (e + 1) * 31], op=ALU.add)
                    vector.drain().then_inc(S["fk"], 1)
                em("vector", f_blend)
                C["fk"] += 1
                continue

            wm1, wpa2 = C["m1"], C["pa"]
            pbfe2 = C["pbfe"]
            def f_asm(vector, scannow=scannow, wm1=wm1, wpa2=wpa2, pbfe2=pbfe2):
                vector.wait_ge(S["scan"], scannow)
                vector.wait_ge(S["m1"], wm1)
                vector.wait_ge(S["pa"], wpa2)
                vector.wait_ge(S["pbfe"], pbfe2)
                for e in range(4):
                    vector.tensor_copy(fkA[0:64, e * 287:e * 287 + 31], w31k[:, e * 31:(e + 1) * 31])
                    vector.tensor_copy(fkA[0:64, e * 287 + 31:(e + 1) * 287], sk_s[:, e * 256:(e + 1) * 256])
                    vector.tensor_copy(svb[:, e * 287:e * 287 + 31], w31v[:, e * 31:(e + 1) * 31])
                    vector.tensor_copy(svb[:, e * 287 + 31:(e + 1) * 287], sv_s[:, e * 256:(e + 1) * 256])
                _dr(vector)
                for e in range(4):
                    vector.tensor_copy(w31k[:, e * 31:(e + 1) * 31], sk_s[:, e * 256 + 225:e * 256 + 256])
                    vector.tensor_copy(w31v[:, e * 31:(e + 1) * 31], sv_s[:, e * 256 + 225:e * 256 + 256])
                vector.drain().then_inc(S["fk"], 1)
            em("vector", f_asm)
            C["fk"] += 1
            fknow2 = C["fk"]

            for e in range(4):
                for st in range(3):
                    c0, sl = ST3[st]
                    ev = C["pbfe"]
                    wm2 = C["m2"]
                    def f(tensor, e=e, c0=c0, sl=sl, fknow2=fknow2, ev=ev):
                        tensor.wait_ge(S["fk"], fknow2)
                        tensor.wait_ge(S["pbfe"], ev)
                        tensor.transpose(pBF[0:sl, 0:64],
                                         svb[:, e * 287 + c0:e * 287 + c0 + sl],
                                         ident_s[0:64, 0:64]).then_inc(S["pbfT"], 1)
                    em("tensor", f)
                    C["pbfT"] += 1
                    tv = C["pbfT"]
                    def f2(scalar, e=e, st=st, sl=sl, tv=tv, wm2=wm2):
                        scalar.wait_ge(S["pbfT"], tv)
                        scalar.wait_ge(S["m2"], wm2)
                        dst = [fv1, fv2, fv3][st]
                        scalar.copy(out=dst[0:sl, e * 65:e * 65 + 64],
                                    in_=pBF[0:sl, 0:64]).then_inc(S["pbfe"], 1)
                    em("scalar", f2)
                    C["pbfe"] += 1

            wkv3, wga3, wwoe3, wpa3 = C["kv"], C["ga"], C["woe"], C["pa"]
            def f_q(tensor, xbase=xbase, wkv3=wkv3, wga3=wga3, wwoe3=wwoe3, wpa3=wpa3):
                tensor.wait_ge(S["kv"], wkv3)
                tensor.wait_ge(S["ga"], wga3)
                tensor.wait_ge(S["woe"], wwoe3)
                tensor.wait_ge(S["pa"], wpa3)
                for m in range(4):
                    for k in range(8):
                        mm = tensor.matmul(
                            pA[:, m * 256:(m + 1) * 256],
                            wq_s[:, k * 512 + m * 128:k * 512 + (m + 1) * 128],
                            xmT[:, k * T + xbase:k * T + xbase + 256],
                            start=(k == 0), stop=(k == 7))
                mm.then_inc(S["qm"], 1)
            em("tensor", f_q)
            C["qm"] += 1
            qmnow = C["qm"]
            wpa4 = C["pa"]
            def f_qe(vector, qmnow=qmnow, wpa4=wpa4):
                vector.wait_ge(S["qm"], qmnow)
                vector.wait_ge(S["pa"], wpa4)
                for m in range(4):
                    vector.tensor_scalar(out=q_s[:, (2 * m) * 256:(2 * m + 1) * 256],
                                         in0=pA[0:64, m * 256:(m + 1) * 256], scalar1=0.125,
                                         scalar2=None, op0=ALU.mult)
                    vector.tensor_scalar(out=q_s[:, (2 * m + 1) * 256:(2 * m + 2) * 256],
                                         in0=pA[64:128, m * 256:(m + 1) * 256], scalar1=0.125,
                                         scalar2=None, op0=ALU.mult)
                vector.drain().then_inc(S["qe"], 1)
            em("vector", f_qe)
            C["qe"] += 1
            qenow = C["qe"]

            it0 = 16 * (b * 4 + j)
            if _B_SKIP_WIN:
                continue
            for qb in range(2):
                for h in range(8):
                    for ceng, cfn in carry:
                        em(ceng, cfn)
                    carry = []
                    it = it0 + qb * 8 + h
                    pp = it % 2
                    w_mx = mx_hist.get(it - 2, 0)
                    w_woe = C["woe"]
                    def f_pa(tensor, qb=qb, h=h, pp=pp, qenow=qenow,
                             fknow2=fknow2, w_mx=w_mx, w_woe=w_woe):
                        tensor.wait_ge(S["qe"], qenow)
                        tensor.wait_ge(S["fk"], fknow2)
                        tensor.wait_ge(S["mx"], w_mx)
                        for e in range(4):
                            tensor.matmul(pA[:, pp * 1024 + e * 256:pp * 1024 + e * 256 + 159],
                                          q_s[:, h * 256 + qb * 128:h * 256 + (qb + 1) * 128],
                                          fkA[0:64, e * 287 + qb * 128:e * 287 + qb * 128 + 159],
                                          start=True, stop=True).then_inc(S["pa"], 1)
                    em("tensor", f_pa)
                    C["pa"] += 4
                    panow = C["pa"]
                    w_ngT = ngT_hist.get(it - 2, 0)
                    def f_mx(vector, pp=pp, panow=panow, w_ngT=w_ngT):
                        vector.wait_ge(S["pa"], panow)
                        vector.wait_ge(S["ngT"], w_ngT)
                        vector.reduce_max(mcol[:, 0:1],
                                          pA[:, pp * 1024:pp * 1024 + 927], axis=AX.X)
                        vector.drain()
                        vector.tensor_scalar(out=nmb[:, pp:pp + 1], in0=mcol[:, 0:1], scalar1=-1.0,
                                             scalar2=None, op0=ALU.mult).then_inc(S["mx"], 1)
                    em("vector", f_mx)
                    C["mx"] += 1
                    mx_hist[it] = C["mx"]
                    mxnow = C["mx"]
                    w_qa2 = C["qa"]
                    def f_ngT(tensor, pp=pp, mxnow=mxnow, w_qa2=w_qa2):
                        tensor.wait_ge(S["mx"], mxnow)
                        tensor.wait_ge(S["qa"], w_qa2)
                        tensor.transpose(pB[0:1, 1408:1536], nmb[:, pp:pp + 1],
                                         identf[:, :]).then_inc(S["ngT"], 1)
                    em("tensor", f_ngT)
                    C["ngT"] += 1
                    ngT_hist[it] = C["ngT"]
                    tv = C["ngT"]
                    w_m1q = m1_hist.get(it - 2, 0)
                    def f_ngE(scalar, qb=qb, h=h, pp=pp, tv=tv, w_m1q=w_m1q):
                        scalar.wait_ge(S["ngT"], tv)
                        scalar.wait_ge(S["m1"], w_m1q)
                        scalar.copy(out=qaug[64:65, pp * 128:(pp + 1) * 128],
                                    in_=pB[0:1, 1408:1536])
                        scalar.copy(out=qaug[0:64, pp * 128:(pp + 1) * 128],
                                    in_=q_s[:, h * 256 + qb * 128:h * 256 + (qb + 1) * 128]
                                    ).then_inc(S["qa"], 1)
                    em("scalar", f_ngE)
                    C["qa"] += 1
                    qanow = C["qa"]
                    expdnow = C["expd"]
                    w_woe2 = C["woe"]
                    def f_m1(tensor, qb=qb, pp=pp, qanow=qanow, expdnow=expdnow, w_woe2=w_woe2):
                        tensor.wait_ge(S["qa"], qanow)
                        tensor.wait_ge(S["expd"], expdnow)
                        tensor.wait_ge(S["woe"], w_woe2)
                        for e in range(4):
                            for st2 in range(2):
                                sl = 128 if st2 == 0 else 31
                                r = e * 2 + st2
                                c0 = qb * 128 + st2 * 128
                                tensor.matmul(pB[0:sl, r * 128:r * 128 + 128],
                                              fkA[0:65, e * 287 + c0:e * 287 + c0 + sl],
                                              qaug[0:65, pp * 128:(pp + 1) * 128],
                                              start=True, stop=True).then_inc(S["m1"], 1)
                    em("tensor", f_m1)
                    C["m1"] += 8
                    m1_hist[it] = C["m1"]
                    m1now = C["m1"]
                    w_m2 = m2_hist.get(it - 2, 0)
                    def f_exp(scalar, pp=pp, m1now=m1now, w_m2=w_m2):
                        scalar.wait_ge(S["m1"], m1now)
                        scalar.wait_ge(S["m2"], w_m2)
                        scalar.activation(out=expT[:, pp * 1024:(pp + 1) * 1024],
                                          in_=pB[:, 0:1024], func=AF.Exp,
                                          bias=0.0, scale=1.0).then_inc(S["expd"], 1)
                    em("scalar", f_exp)
                    C["expd"] += 1
                    expdnow2 = C["expd"]
                    def f_msk(vector, pp=pp, expdnow2=expdnow2):
                        vector.wait_ge(S["expd"], expdnow2)
                        vector.tensor_tensor(out=expT[:, pp * 1024:(pp + 1) * 1024],
                                             in0=expT[:, pp * 1024:(pp + 1) * 1024],
                                             in1=band_s[:, :],
                                             op=ALU.mult).then_inc(S["msk"], 1)
                    em("vector", f_msk)
                    C["msk"] += 1
                    msknow = C["msk"]
                    w_ye = C["ye"]
                    def f_m2(tensor, qb=qb, pp=pp, msknow=msknow, w_ye=w_ye):
                        tensor.wait_ge(S["msk"], msknow)
                        tensor.wait_ge(S["ye"], w_ye)
                        first = True
                        for e in range(4):
                            for st2 in range(2):
                                sl = 128 if st2 == 0 else 31
                                r = e * 2 + st2
                                fvx = [fv1, fv2][st2] if qb == 0 else [fv2, fv3][st2]
                                tensor.matmul(pB[:, 1280:1345],
                                              expT[0:sl, pp * 1024 + r * 128:pp * 1024 + r * 128 + 128],
                                              fvx[0:sl, e * 65:(e + 1) * 65],
                                              start=first, stop=(r == 7)).then_inc(S["m2"], 1)
                                first = False
                    carry.append(("tensor", f_m2))
                    C["m2"] += 8
                    m2_hist[it] = C["m2"]
                    m2now = C["m2"]
                    w_yT = yT_hist.get(it - 2, 0)
                    def f_ye(vector, pp=pp, m2now=m2now, w_yT=w_yT):
                        vector.wait_ge(S["m2"], m2now)
                        vector.wait_ge(S["pbfT"], w_yT)
                        vector.reciprocal(rz1[:, 0:1], pB[:, 1344:1345])
                        vector.drain()
                        vector.tensor_scalar(out=y_s[:, pp * 64:(pp + 1) * 64], in0=pB[:, 1280:1344],
                                             scalar1=rz1[:, 0:1], scalar2=None,
                                             op0=ALU.mult).then_inc(S["ye"], 1)
                    em("vector", f_ye)
                    C["ye"] += 1
                    yenow = C["ye"]
                    ev2 = C["pbfe"]
                    def f_yT(tensor, pp=pp, yenow=yenow, ev2=ev2):
                        tensor.wait_ge(S["ye"], yenow)
                        tensor.wait_ge(S["pbfe"], ev2)
                        tensor.transpose(pBF[0:64, :], y_s[:, pp * 64:(pp + 1) * 64],
                                         ident_s[:, :]).then_inc(S["pbfT"], 1)
                    carry.append(("tensor", f_yT))
                    C["pbfT"] += 1
                    yT_hist[it] = C["pbfT"]
                    tv2 = C["pbfT"]
                    wwo = C["wo"]
                    def f_yTe(scalar, qb=qb, h=h, tv2=tv2, wwo=wwo):
                        scalar.wait_ge(S["pbfT"], tv2)
                        scalar.wait_ge(S["wo"], wwo)
                        kt = h // 2
                        rbase = (h % 2) * 64
                        scalar.copy(out=yT[rbase:rbase + 64,
                                           (qb % 2) * 512 + kt * 128:(qb % 2) * 512 + (kt + 1) * 128],
                                    in_=pBF[0:64, :]).then_inc(S["pbfe"], 1)
                    carry.append(("scalar", f_yTe))
                    C["pbfe"] += 1

                for ceng, cfn in carry:
                    em(ceng, cfn)
                carry = []
                yTe_all = C["pbfe"]
                woev = C["woe"]
                w_expd3 = C["expd"]
                def f_wo(tensor, qb=qb, yTe_all=yTe_all, woev=woev, w_expd3=w_expd3):
                    tensor.wait_ge(S["pbfe"], yTe_all)
                    tensor.wait_ge(S["woe"], woev)
                    tensor.wait_ge(S["expd"], w_expd3)
                    for nh in range(2):
                        for kt in range(4):
                            mm = tensor.matmul(
                                pB[:, nh * 512:(nh + 1) * 512],
                                yT[:, (qb % 2) * 512 + kt * 128:(qb % 2) * 512 + (kt + 1) * 128],
                                wo_s[:, kt * D + nh * 512:kt * D + nh * 512 + 512],
                                start=(kt == 0), stop=(kt == 3))
                    mm.then_inc(S["wo"], 1)
                em("tensor", f_wo)
                C["wo"] += 1
                wonow = C["wo"]
                tile_idx = b * 8 + j * 2 + qb
                stsem = "st0" if tile_idx % 2 == 0 else "st1"
                stv = C[stsem]
                def f_woe(vector, qb=qb, wonow=wonow, stsem=stsem, stv=stv):
                    vector.wait_ge(S["wo"], wonow)
                    vector.wait_ge(S[stsem], stv)
                    vector.tensor_copy(ao_s[:, (qb % 2) * D:(qb % 2 + 1) * D],
                                       pB[:, 0:1024]).then_inc(S["woe"], 1)
                em("vector", f_woe)
                C["woe"] += 1
                woen = C["woe"]
                def f_st(sync, tile_idx=tile_idx, qb=qb, woen=woen, stsem=stsem):
                    sync.wait_ge(S["woe"], woen)
                    sync.dma_start(out=attn_r[:, tile_idx, :],
                                   in_=ao_s[:, (qb % 2) * D:(qb % 2 + 1) * D]).then_inc(S[stsem], 16)
                em("sync", f_st)
                C[stsem] += 16

        st0f, st1f = C["st0"], C["st1"]
        def f_fin(sync):
            sync.wait_ge(S["st0"], st0f)
            sync.wait_ge(S["st1"], st1f)
        em("sync", f_fin)

        @block.sync
        def _(sync):
            for f in ops["sync"]:
                f(sync)

        @block.gpsimd
        def _(gpsimd):
            for f in ops["gpsimd"]:
                f(gpsimd)

        @block.vector
        def _(vector):
            for f in ops["vector"]:
                f(vector)

        @block.scalar
        def _(scalar):
            for f in ops["scalar"]:
                f(scalar)

        @block.tensor
        def _(tensor):
            for f in ops["tensor"]:
                f(tensor)

    return nc


def stage_B_weights(w_G, b_G, w_K, w_V, w_Q, w_O):
    wk = np.transpose(np.asarray(w_K), (2, 0, 1)).reshape(D, 512)
    wv = np.transpose(np.asarray(w_V), (2, 0, 1)).reshape(D, 512)
    wq = np.transpose(np.asarray(w_Q), (2, 0, 1)).reshape(D, 512)
    wg = np.transpose(np.asarray(w_G), (2, 0, 1)).reshape(D, 32)
    wkvg = np.ascontiguousarray(np.concatenate([wk, wv, wg], 1))
    bg = np.tile(np.asarray(b_G).reshape(32), (128, 1)).astype(np.float32)
    return wkvg, np.ascontiguousarray(wq), np.ascontiguousarray(np.asarray(w_O)), bg


def stage_B_inits(init_K, init_V, c):
    iK = np.asarray(init_K)
    iV = np.asarray(init_V)
    cols = np.zeros((64, 288), np.float32)
    for l in range(4):
        for e in range(4):
            cols[:, l * 4 + e] = iK[e, c * 4 + l]
            cols[:, 16 + l * 4 + e] = iV[e, c * 4 + l]
    if c > 0:
        for e in range(4):
            cols[:, 32 + e] = iK[e, c * 4 - 1]
            cols[:, 36 + e] = iV[e, c * 4 - 1]
    if c == 0:
        for e in range(4):
            for l in range(31):
                cols[:, 40 + e * 31 + l] = iK[e, l + 1]
                cols[:, 164 + e * 31 + l] = iV[e, l + 1]
    return cols


def stage_B_xprev(xn_full_bf, c):
    out = np.zeros((512, D), dtype=xn_full_bf.dtype)
    if c > 0:
        s0 = c * SLOC
        out[0:256] = xn_full_bf[0, s0 - 256:s0]
        out[256:512] = xn_full_bf[1, s0 - 256:s0]
    return out
